# version 01
# baseline (speedup 1.0000x reference)
import numpy as np
import ml_dtypes
from contextlib import ExitStack
import concourse.bass as bass
import concourse.mybir as mybir
from concourse.bass_utils import run_bass_kernel_spmd

F32 = mybir.dt.float32
BF16 = mybir.dt.bfloat16
AF = mybir.ActivationFunctionType
ALU = mybir.AluOpType
AX = mybir.AxisListType
BF16NP = ml_dtypes.bfloat16

N_CORES = 8
B, SEQ, DM = 4, 4096, 1024
T_ALL = B * SEQ
T_CORE = T_ALL // N_CORES


class Plan:
    COMPUTE = ("tensor", "vector", "scalar", "gpsimd", "sync")

    def __init__(self, nc):
        self.nc = nc
        self.steps = []

    @staticmethod
    def _h(x):
        t = getattr(x, "tensor", None)
        return id(t) if t is not None else id(x)

    def step(self, eng, emit, reads=(), writes=(), dma=False, tag=""):
        self.steps.append(dict(
            eng=eng, emit=emit,
            reads=[self._h(r) for r in reads],
            writes=[self._h(w) for w in writes],
            dma=dma, tag=tag))

    def dma(self, q, out, in_, tag=""):
        def emit(e, out=out, in_=in_):
            return e.dma_start(out=out, in_=in_)
        self.step(q, emit, reads=[in_], writes=[out], dma=True, tag=tag)

    def barrier_outputs(self, bufs):
        def emit(e):
            return e.nop()
        self.step("sync", emit, reads=list(bufs), writes=(), tag="final_barrier")

    DMA_POOL = 6

    def build(self, es):
        nc = self.nc
        vengs = set()
        qcount = {}
        for st in self.steps:
            if st["dma"]:
                q = "q_" + st["eng"]
                i = qcount.get(q, 0)
                qcount[q] = i + 1
                st["veng"] = f"{q}.{i % self.DMA_POOL}"
                st["slot_round"] = i // self.DMA_POOL
            else:
                st["veng"] = st["eng"]
            vengs.add(st["veng"])
            vengs.add(st["eng"])
        sems = {v: es.enter_context(nc.semaphore("sem_" + v.replace(".", "_")))
                for v in sorted(vengs)}
        seq = {v: 0 for v in vengs}
        last_w = {}
        readers = {}
        for st in self.steps:
            v = st["veng"]
            inc = 16 if st["dma"] else 1
            s = seq[v] + inc
            deps = {}

            def add_dep(dv, ds):
                deps[dv] = max(deps.get(dv, 0), ds)

            for b in st["reads"]:
                lw = last_w.get(b)
                if lw:
                    add_dep(*lw)
            for b in st["writes"]:
                lw = last_w.get(b)
                if lw:
                    add_dep(*lw)
                for dv, ds in readers.get(b, {}).items():
                    add_dep(dv, ds)
            if st["dma"] and st["slot_round"] > 0:
                add_dep(v, 16 * st["slot_round"])
            deps.pop(None, None)
            deps = {dv: ds for dv, ds in deps.items()
                    if not (dv == v and not st["dma"] and ds >= s)}
            st["deps"] = deps
            st["seqno"] = s
            seq[v] = s
            for b in st["reads"]:
                readers.setdefault(b, {})[v] = s
            for b in st["writes"]:
                last_w[b] = (v, s)
                readers[b] = {}
        for v_ in sorted(vengs):
            nc.gpsimd.sem_clear(sems[v_])
        nc.all_engine_barrier()
        block = es.enter_context(nc.Block())
        for eng_name in self.COMPUTE:
            mysteps = [st for st in self.steps if st["eng"] == eng_name]
            if not mysteps:
                continue

            def body(e, mysteps=mysteps, eng_name=eng_name):
                wm = {}
                for st in mysteps:
                    for dv, val in sorted(st["deps"].items()):
                        if wm.get(dv, 0) < val:
                            e.wait_ge(sems[dv], val)
                            wm[dv] = val
                    inst = st["emit"](e)
                    if inst is not None:
                        inst.then_inc(sems[st["veng"]], 16 if st["dma"] else 1)
                    else:
                        raise RuntimeError(f"step emit returned None: {st['tag']}")
            getattr(block, eng_name)(body)


def ev(plan, eng, fn, reads=(), writes=(), tag=""):
    plan.step(eng, fn, reads=reads, writes=writes, tag=tag)


def mm(plan, out, pairs, start=True, stop=True, tag=""):
    rd = []
    for l, r in pairs:
        rd += [l, r]

    def emit(e, out=out, pairs=tuple(pairs), start=start, stop=stop):
        n = len(pairs)
        inst = None
        for i, (l, r) in enumerate(pairs):
            inst = e.matmul(out, l, r, start=(start and i == 0),
                            stop=(stop and i == n - 1))
        return inst
    plan.step("tensor", emit, reads=rd, writes=[out], tag=tag)


def transpose_step(plan, out_psum, in_sb, ident, tag=""):
    def emit(e, out=out_psum, in_=in_sb, ident=ident):
        return e.transpose(out, in_, ident)
    plan.step("tensor", emit, reads=[in_sb, ident], writes=[out_psum], tag=tag)


def ln_plain_tile(plan, P, out_bf, x_f32, stats, eps=1e-5, tag="ln"):
    v = "vector"
    s6, s2, rstd, b2 = stats["s6"], stats["s2"], stats["rstd"], stats["b2"]
    W = x_f32.shape[-1]
    nh = (W + 511) // 512
    for i in range(nh):
        ev(plan, v, lambda e, i=i: e.bn_stats(s6[:, 6 * i:6 * i + 6], x_f32[:, 512 * i:512 * (i + 1)]),
           reads=[x_f32], writes=[s6], tag=tag + f".bs{i}")
    ev(plan, v, lambda e: e.bn_aggr(s2[:], s6[:, 0:6 * nh]), reads=[s6], writes=[s2], tag=tag + ".ba")
    ev(plan, "scalar", lambda e: e.activation(rstd[:], s2[:, 1:2], AF.Sqrt, bias=stats["eps"], scale=1.0),
       reads=[s2], writes=[rstd], tag=tag + ".sq")
    ev(plan, v, lambda e: e.reciprocal(rstd[:], rstd[:]), reads=[rstd], writes=[rstd], tag=tag + ".rc")
    ev(plan, v, lambda e: e.scalar_tensor_tensor(b2[:], s2[:, 0:1], -1.0, rstd[:], ALU.mult, ALU.mult),
       reads=[s2, rstd], writes=[b2], tag=tag + ".b2")
    ev(plan, v, lambda e: e.tensor_scalar(out_bf, x_f32, rstd[:], b2[:], ALU.mult, ALU.add),
       reads=[x_f32, rstd, b2], writes=[out_bf], tag=tag + ".ap")


def tr128(plan, psum_bf, src_bf, ident, dst_bf, tag="tr"):
    transpose_step(plan, psum_bf, src_bf, ident, tag=tag + ".t")
    ev(plan, "scalar", lambda e: e.copy(dst_bf, psum_bf), reads=[psum_bf], writes=[dst_bf],
       tag=tag + ".e")


def build_a0(dbg=False):
    nc = bass.Bass("TRN2", target_bir_lowering=False, debug=False)
    T = SEQ
    NB = T // 128
    NQ = T // 512
    SCALE = 1.0 / 8.0
    DL = [-384, -256, -128, 0, 128, 256]

    d_x = nc.dram_tensor("x", [T, DM], F32, kind="ExternalInput").ap()
    d_wq = nc.dram_tensor("wq", [DM, 512], F32, kind="ExternalInput").ap()
    d_wk = nc.dram_tensor("wk", [DM, 512], F32, kind="ExternalInput").ap()
    d_wv = nc.dram_tensor("wv", [DM, 512], F32, kind="ExternalInput").ap()
    d_wp = nc.dram_tensor("wp", [512, DM], F32, kind="ExternalInput").ap()
    d_cos = nc.dram_tensor("cos", [T, 64], F32, kind="ExternalInput").ap()
    d_sin = nc.dram_tensor("sin", [T, 64], F32, kind="ExternalInput").ap()
    d_qn = nc.dram_tensor("qn", [128, 2048], F32, kind="ExternalInput").ap()
    d_msk = nc.dram_tensor("msk", [128, 6 * 512], F32, kind="ExternalInput").ap()
    d_sink = nc.dram_tensor("sink", [1, 8], F32, kind="ExternalInput").ap()
    d_id = nc.dram_tensor("idm", [128, 128], F32, kind="ExternalInput").ap()
    d_oh = nc.dram_tensor("oh", [8, 8 * 128], F32, kind="ExternalInput").ap()
    d_o1 = nc.dram_tensor("o1", [1, 64], F32, kind="ExternalInput").ap()
    d_cst = nc.dram_tensor("cst", [128, 8], F32, kind="ExternalInput").ap()
    d_out = nc.dram_tensor("out", [DM, T], F32, kind="ExternalOutput").ap()
    if dbg:
        d_dq = nc.dram_tensor("dq", [128, 4 * T], BF16, kind="ExternalOutput").ap()
        d_dk = nc.dram_tensor("dk", [128, 4 * T], BF16, kind="ExternalOutput").ap()
        d_dv = nc.dram_tensor("dv", [128, 32 * 8 * 66], BF16, kind="ExternalOutput").ap()
        d_dm = nc.dram_tensor("dmx", [8, 8], F32, kind="ExternalOutput").ap()
        d_db = nc.dram_tensor("dbc", [128, 8], F32, kind="ExternalOutput").ap()
        d_dao = nc.dram_tensor("dao", [128, 4 * 512], BF16, kind="ExternalOutput").ap()

    es = ExitStack()
    sb = lambda n, s, dt=F32: es.enter_context(nc.sbuf_tensor(n, s, dt))
    ps = lambda n, s, dt=F32: es.enter_context(nc.psum_tensor(n, s, dt))
    with es:
        xt = sb("xt", [128, 4 * 1024])
        xn_tm = sb("xn_tm", [128, 4 * 1024], BF16)
        xn_fm = sb("xn_fm", [128, 8 * 512], BF16)
        q_fm = sb("q_fm", [128, 4 * T], BF16)
        k_fm = sb("k_fm", [128, 4 * T], BF16)
        v_tm = sb("v_tm", [128, NB * 8 * 66], BF16)
        ao = sb("ao", [128, 4 * 512], BF16)
        wq = sb("s_wq", [128, 8 * 512], BF16)
        wk = sb("s_wk", [128, 8 * 512], BF16)
        wv = sb("s_wv", [128, 8 * 512], BF16)
        wp = sb("s_wp", [128, 4 * 1024], BF16)
        ctab = sb("s_ctab", [128, NB * 64], BF16)
        stab = sb("s_stab", [128, NB * 64], BF16)
        qn4 = sb("qn4", [128, 2048])
        msk = sb("s_msk", [128, 6 * 512], BF16)
        snk = sb("snk", [1, 8])
        idm = sb("s_idm", [128, 128], BF16)
        oh8 = sb("oh8", [8, 8 * 128], BF16)
        o1 = sb("s_o1", [1, 64], BF16)
        qraw = sb("qraw", [128, 512])
        sq = sb("sq", [128, 512])
        u1 = sb("u1", [128, 512])
        u2 = sb("u2", [128, 512])
        qs8 = sb("qs8", [128, 8]); qss = sb("qss", [128, 8])
        mu8 = sb("mu8", [128, 8]); var8 = sb("var8", [128, 8])
        rstd8 = sb("rstd8", [128, 8]); b28 = sb("b28", [128, 8])
        nq8 = sb("nq8", [128, 8])
        runq = sb("runq", [128, 8]); runk = sb("runk", [128, 8])
        runb = sb("runb", [128, 8], BF16)
        mx8 = sb("mx8", [8, 8])
        csb = sb("csb", [8, 1], BF16)
        bias_col = sb("bias_col", [128, 8])
        sink_e = sb("sink_e", [1, 8])
        ptA = sb("ptA", [128, 512], BF16)
        ptB = sb("ptB", [128, 512], BF16)
        den = sb("den", [1, 512])
        denb = sb("denb", [1, 512], BF16)
        rb_sb = None
        outsb1 = None
        outsb2 = None
        st6 = sb("st6", [128, 12]); st2 = sb("st2", [128, 2])
        cst = sb("s_cst", [128, 8])
        rstd1 = sb("rstd1", [128, 1]); b2c = sb("b2c", [128, 1])
        stats = dict(s6=st6, s2=st2, rstd=rstd1, b2=b2c, eps=cst[:, 0:1])

        PA = ps("PA", [128, 512]); PB = ps("PB", [128, 512])
        PC = ps("PC", [128, 512])
        PEb = ps("PEb", [128, 512])
        PF = ps("PF", [128, 512]); PH = ps("PH", [128, 512])
        PG1 = ps("PG1", [128, 1024], BF16)
        PG2 = ps("PG2", [128, 1024], BF16)

        plan = Plan(nc)
        g, s, v, a = "gpsimd", "sync", "vector", "scalar"

        plan.dma(g, wq[:].rearrange("p (c o) -> p c o", c=8), d_wq.rearrange("(c p) o -> p c o", p=128))
        plan.dma(g, wk[:].rearrange("p (c o) -> p c o", c=8), d_wk.rearrange("(c p) o -> p c o", p=128))
        plan.dma(g, wv[:].rearrange("p (c o) -> p c o", c=8), d_wv.rearrange("(c p) o -> p c o", p=128))
        plan.dma(g, wp[:].rearrange("p (c o) -> p c o", c=4), d_wp.rearrange("(c p) o -> p c o", p=128))
        plan.dma(g, ctab[:].rearrange("p (b d) -> p b d", b=NB), d_cos.rearrange("(b p) d -> p b d", p=128))
        plan.dma(g, stab[:].rearrange("p (b d) -> p b d", b=NB), d_sin.rearrange("(b p) d -> p b d", p=128))
        plan.dma(g, qn4[:], d_qn)
        plan.dma(g, msk[:], d_msk)
        plan.dma(s, snk[:], d_sink)
        plan.dma(g, idm[:], d_id)
        plan.dma(g, oh8[:], d_oh)
        plan.dma(g, o1[:], d_o1)
        plan.dma(s, cst[:], d_cst)
        ev(plan, v, lambda e: e.memset(
            v_tm[:].rearrange("p (b h f) -> p b h f", b=NB, h=8)[:, :, :, 64:66], 1.0),
            writes=[v_tm], tag="vones")
        ev(plan, v, lambda e: e.memset(runq[:], 0.0), writes=[runq])
        ev(plan, v, lambda e: e.memset(runk[:], 0.0), writes=[runk])

        q_fm3 = q_fm[:].rearrange("p (hp t) -> p hp t", hp=4)
        k_fm3 = k_fm[:].rearrange("p (hp t) -> p hp t", hp=4)
        v_tm4 = v_tm[:].rearrange("p (b h f) -> p b h f", b=NB, h=8)
        xt3 = xt[:].rearrange("p (b f) -> p b f", b=4)
        xn_tm3 = xn_tm[:].rearrange("p (b f) -> p b f", b=4)
        xn_fm3 = xn_fm[:].rearrange("p (c t) -> p c t", c=8)
        ct3 = ctab[:].rearrange("p (b d) -> p b d", b=NB)
        st3 = stab[:].rearrange("p (b d) -> p b d", b=NB)
        msk3 = msk[:].rearrange("p (m t) -> p m t", m=6)
        wq3 = wq[:].rearrange("p (c o) -> p c o", c=8)
        wk3 = wk[:].rearrange("p (c o) -> p c o", c=8)
        wv3 = wv[:].rearrange("p (c o) -> p c o", c=8)
        wp3 = wp[:].rearrange("p (c o) -> p c o", c=4)
        x_dram3 = d_x.rearrange("(b p) f -> p b f", p=128)
        out_dram3 = d_out.rearrange("(c p) t -> c p t", p=128)
        ao2 = ao[:].rearrange("p (c t) -> p c t", c=4)[:, 0:2, :]
        ao3 = ao[:].rearrange("p (c t) -> p c t", c=4)

        def bc64(ap3):
            return ap3.rearrange("p (h o) -> p h o", h=8).broadcast_to([128, 8, 64])

        for ch in range(NQ):
            plan.dma(s, xt3[:, :, :], x_dram3[:, 4 * ch:4 * ch + 4, :], tag=f"ldx{ch}")
            for j in range(4):
                blk = 4 * ch + j
                ln_plain_tile(plan, None, xn_tm3[:, j, :], xt3[:, j, :], stats, tag=f"ln{blk}")
                for cc in range(8):
                    PG, PGo = (PG1, 0) if cc % 2 == 0 else (PG2, 0)
                    transpose_step(plan, PG[:, 0:128], xn_tm3[:, j, cc * 128:(cc + 1) * 128], idm[:],
                                   tag=f"xtr{blk}.{cc}")
                    ev(plan, a, lambda e, cc=cc, j=j, PG=PG: e.copy(
                        xn_fm3[:, cc, j * 128:(j + 1) * 128], PG[:, 0:128]),
                       reads=[PG], writes=[xn_fm], tag=f"xtre{blk}.{cc}")
            for j in range(4):
                blk = 4 * ch + j
                cstep = ct3[:, blk, :].rearrange("p (x d) -> p x d", x=1).broadcast_to([128, 8, 64])
                sstep = st3[:, blk, :].rearrange("p (x d) -> p x d", x=1).broadcast_to([128, 8, 64])
                mm(plan, PA[:], [(xn_fm3[:, ic, j * 128:(j + 1) * 128], wv3[:, ic, :]) for ic in range(8)],
                   tag=f"v{blk}")
                ev(plan, v, lambda e, blk=blk: e.tensor_copy(
                    v_tm4[:, blk, :, 0:64], PA[:].rearrange("p (h d) -> p h d", h=8)),
                   reads=[PA], writes=[v_tm], tag=f"ve{blk}")
                for which, (wmat, dst3, qoff, run) in enumerate(
                        ((wq3, q_fm3, 0, runq), (wk3, k_fm3, 1024, runk))):
                    P = PB
                    mm(plan, P[:], [(xn_fm3[:, ic, j * 128:(j + 1) * 128], wmat[:, ic, :]) for ic in range(8)],
                       tag=f"qk{blk}.{which}")
                    ev(plan, a, lambda e, P=P: e.copy(qraw[:], P[:]), reads=[P], writes=[qraw])
                    ev(plan, a, lambda e, P=P: e.activation(sq[:], P[:], AF.Square), reads=[P], writes=[sq])
                    qraw3 = qraw[:].rearrange("p (h d) -> p h d", h=8)
                    sq3 = sq[:].rearrange("p (h d) -> p h d", h=8)
                    ev(plan, v, lambda e, qraw3=qraw3: e.tensor_reduce(
                        qs8[:].rearrange("p (h o) -> p h o", h=8), qraw3, AX.X, ALU.add),
                       reads=[qraw], writes=[qs8])
                    ev(plan, v, lambda e, sq3=sq3: e.tensor_reduce(
                        qss[:].rearrange("p (h o) -> p h o", h=8), sq3, AX.X, ALU.add),
                       reads=[sq], writes=[qss])
                    ev(plan, v, lambda e: e.tensor_scalar_mul(mu8[:], qs8[:], 1.0 / 64),
                       reads=[qs8], writes=[mu8])
                    ev(plan, v, lambda e: e.scalar_tensor_tensor(var8[:], mu8[:], -1.0, mu8[:], ALU.mult, ALU.mult),
                       reads=[mu8], writes=[var8])
                    ev(plan, v, lambda e: e.scalar_tensor_tensor(var8[:], qss[:], 1.0 / 64, var8[:], ALU.mult, ALU.add),
                       reads=[qss, var8], writes=[var8])
                    ev(plan, a, lambda e: e.activation(rstd8[:], var8[:], AF.Sqrt, bias=cst[:, 0:1], scale=1.0),
                       reads=[var8, cst], writes=[rstd8])
                    ev(plan, v, lambda e: e.reciprocal(rstd8[:], rstd8[:]), reads=[rstd8], writes=[rstd8])
                    ev(plan, v, lambda e: e.scalar_tensor_tensor(b28[:], mu8[:], -1.0, rstd8[:], ALU.mult, ALU.mult),
                       reads=[mu8, rstd8], writes=[b28])
                    u13 = u1[:].rearrange("p (h d) -> p h d", h=8)
                    ev(plan, v, lambda e, qraw3=qraw3, u13=u13: e.tensor_tensor(
                        u13, qraw3, bc64(rstd8[:]), ALU.mult), reads=[qraw, rstd8], writes=[u1])
                    ev(plan, v, lambda e, u13=u13: e.tensor_tensor(
                        u13, u13, bc64(b28[:]), ALU.add), reads=[u1, b28], writes=[u1])
                    ev(plan, v, lambda e, qoff=qoff: e.tensor_mul(u1[:], u1[:], qn4[:, qoff:qoff + 512]),
                       reads=[u1, qn4], writes=[u1])
                    ev(plan, v, lambda e, qoff=qoff: e.tensor_add(u1[:], u1[:], qn4[:, qoff + 512:qoff + 1024]),
                       reads=[u1, qn4], writes=[u1])
                    ev(plan, a, lambda e: e.activation(sq[:], u1[:], AF.Square), reads=[u1], writes=[sq])
                    ev(plan, v, lambda e, sq3=sq3: e.tensor_reduce(
                        nq8[:].rearrange("p (h o) -> p h o", h=8), sq3, AX.X, ALU.add),
                       reads=[sq], writes=[nq8])
                    ev(plan, v, lambda e, run=run: e.tensor_max(run[:], run[:], nq8[:]),
                       reads=[run, nq8], writes=[run])
                    u23 = u2[:].rearrange("p (h d) -> p h d", h=8)
                    u14 = u1[:].rearrange("p (h x d) -> p h x d", h=8, x=2)
                    sq4 = sq[:].rearrange("p (h x d) -> p h x d", h=8, x=2)
                    ct4 = ct3[:, blk, :].rearrange("p (x d) -> p x d", x=1).broadcast_to([128, 8, 64])
                    st4a = st3[:, blk, 0:32].rearrange("p (x d) -> p x d", x=1).broadcast_to([128, 8, 32])
                    st4b = st3[:, blk, 32:64].rearrange("p (x d) -> p x d", x=1).broadcast_to([128, 8, 32])
                    ev(plan, v, lambda e, u23=u23, u13=u13, ct4=ct4: e.tensor_tensor(
                        u23, u13, ct4, ALU.mult), reads=[u1, ctab], writes=[u2])
                    ev(plan, v, lambda e, sq4=sq4, u14=u14, st4a=st4a: e.tensor_tensor(
                        sq4[:, :, 0, :], u14[:, :, 1, :], st4a, ALU.mult), reads=[u1, stab], writes=[sq])
                    ev(plan, v, lambda e, sq4=sq4, u14=u14, st4b=st4b: e.tensor_tensor(
                        sq4[:, :, 1, :], u14[:, :, 0, :], st4b, ALU.mult), reads=[u1, stab], writes=[sq])
                    ev(plan, v, lambda e, u23=u23, sq3=sq3: e.tensor_tensor(
                        u23, u23, sq3, ALU.add), reads=[u2, sq], writes=[u2])
                    ev(plan, v, lambda e: e.tensor_copy(ptA[:], u2[:]), reads=[u2], writes=[ptA])
                    for hp in range(4):
                        PG = PG1 if hp % 2 == 0 else PG2
                        transpose_step(plan, PG[:, 128:256], ptA[:, hp * 128:(hp + 1) * 128], idm[:],
                                       tag=f"qtr{blk}.{which}.{hp}")
                        ev(plan, a, lambda e, dst3=dst3, hp=hp, blk=blk, PG=PG: e.copy(
                            dst3[:, hp, blk * 128:(blk + 1) * 128], PG[:, 128:256]),
                           reads=[PG], writes=[dst3], tag=f"qtre{blk}.{which}.{hp}")

        for run, col in ((runq, 0), (runk, 1)):
            ev(plan, v, lambda e, run=run: e.tensor_copy(runb[:], run[:]), reads=[run], writes=[runb])
            transpose_step(plan, PG1[0:8, 0:128], runb[:, 0:8], idm[:], tag=f"ntr{col}")
            ev(plan, a, lambda e: e.copy(ptB[0:8, 0:128], PG1[0:8, 0:128]), reads=[PG1], writes=[ptB])
            ev(plan, v, lambda e, col=col: e.tensor_reduce(
                mx8[:, col:col + 1].rearrange("p (a o) -> p a o", a=1),
                ptB[0:8, 0:128].rearrange("p (a t) -> p a t", a=1), AX.X, ALU.max),
               reads=[ptB], writes=[mx8])
        ev(plan, v, lambda e: e.tensor_mul(mx8[:, 2:3], mx8[:, 0:1], mx8[:, 1:2]),
           reads=[mx8], writes=[mx8])
        ev(plan, a, lambda e: e.activation(mx8[:, 3:4], mx8[:, 2:3], AF.Sqrt), reads=[mx8], writes=[mx8])
        ev(plan, v, lambda e: e.tensor_scalar(csb[:], mx8[0:8, 3:4], -SCALE, None, ALU.mult),
           reads=[mx8], writes=[csb])
        for h in range(8):
            mm(plan, PEb[:, 0:1], [(oh8[:, h * 128:(h + 1) * 128], csb[:, 0:1])], tag=f"bb{h}")
            ev(plan, a, lambda e, h=h: e.copy(bias_col[:, h:h + 1], PEb[:, 0:1]),
               reads=[PEb], writes=[bias_col])
            ev(plan, a, lambda e, h=h: e.activation(
                sink_e[:, h:h + 1], snk[:, h:h + 1], AF.Exp, bias=bias_col[0:1, h:h + 1], scale=1.0),
               reads=[snk, bias_col], writes=[sink_e])

        for qb in range(NQ):
            q0 = qb * 512
            for h in range(8):
                hp = h // 2
                p0 = (h % 2) * 64
                kbs = [(q0 - d) // 128 for d in DL if 0 <= (q0 - d) // 128 < NB and (q0 - d) % 128 == 0]
                kbs = sorted(set(kbs))
                first = True
                for i, kb in enumerate(kbs):
                    d = q0 - kb * 128
                    PS = PA if i % 2 == 0 else PB
                    pt_t = ptA if i % 2 == 0 else ptB
                    mm(plan, PS[:], [(k_fm3[p0:p0 + 64, hp, kb * 128:(kb + 1) * 128],
                                      q_fm3[p0:p0 + 64, hp, q0:q0 + 512])], tag=f"s{qb}.{h}.{kb}")
                    ev(plan, a, lambda e, PS=PS, pt_t=pt_t, h=h: e.activation(
                        pt_t[:], PS[:], AF.Exp, bias=bias_col[:, h:h + 1], scale=SCALE),
                       reads=[PS, bias_col], writes=[pt_t], tag=f"e{qb}.{h}.{kb}")
                    mi = DL.index(d)
                    ev(plan, v, lambda e, pt_t=pt_t, mi=mi: e.tensor_mul(pt_t[:], pt_t[:], msk3[:, mi, :]),
                       reads=[pt_t, msk], writes=[pt_t], tag=f"m{qb}.{h}.{kb}")
                    mm(plan, PC[0:65, :], [(v_tm4[:, kb, h, 0:65], pt_t[:])],
                       start=first, stop=(i == len(kbs) - 1), tag=f"pv{qb}.{h}.{kb}")
                    first = False
                ev(plan, v, lambda e, h=h: e.tensor_scalar(
                    den[:], PC[64:65, :], sink_e[0:1, h:h + 1], 1e-30, ALU.add, ALU.max),
                   reads=[PC, sink_e], writes=[den], tag=f"dn{qb}.{h}")
                ev(plan, v, lambda e: e.reciprocal(den[:], den[:]), reads=[den], writes=[den])
                ev(plan, v, lambda e: e.tensor_copy(denb[:], den[:]), reads=[den], writes=[denb])
                mm(plan, PEb[0:64, :], [(o1[:], denb[:])], tag=f"rb{qb}.{h}")
                ev(plan, a, lambda e: e.copy(u2[0:64, :], PEb[0:64, :]), reads=[PEb], writes=[u2])
                ev(plan, v, lambda e, h=h, hp=hp, p0=p0: e.tensor_tensor(
                    ao3[p0:p0 + 64, hp, :], PC[0:64, :], u2[0:64, :], ALU.mult),
                   reads=[PC, u2], writes=[ao], tag=f"ao{qb}.{h}")
            for oc in range(8):
                PP = PF if oc % 2 == 0 else PH
                osb = qraw if oc % 2 == 0 else u1
                mm(plan, PP[:], [(wp3[:, ic, oc * 128:(oc + 1) * 128], ao3[:, ic, :]) for ic in range(4)],
                   tag=f"pj{qb}.{oc}")
                ev(plan, a, lambda e, PP=PP, osb=osb: e.copy(osb[:], PP[:]), reads=[PP], writes=[osb])
                plan.dma(g, out_dram3[oc, :, q0:q0 + 512], osb[:], tag=f"st{qb}.{oc}")

        if dbg:
            def dump(dst, src, tagn):
                n = src.shape[-1] if len(src.shape) == 2 else None
                plan.dma(g, dst, src, tag=tagn)
            dump(d_dq, q_fm[:], "dq")
            dump(d_dk, k_fm[:], "dk")
            dump(d_dv, v_tm[:], "dv")
            plan.dma(g, d_dm, mx8[:], tag="dmx")
            plan.dma(g, d_db, bias_col[:], tag="dbc")
            dump(d_dao, ao[:], "dao")
            plan.barrier_outputs([d_out, d_dq, d_dk, d_dv, d_dm, d_db, d_dao])
        else:
            plan.barrier_outputs([d_out])
        plan.build(es)
    return nc


def prep_a0(inputs):
    qkv_w = np.ascontiguousarray(inputs["L0_attn_qkv_w"])
    proj_w = np.ascontiguousarray(inputs["L0_attn_proj_w"])
    x = np.ascontiguousarray(inputs["L0_attn_x"])
    qn_w = inputs["L0_attn_qn_w"]; qn_b = inputs["L0_attn_qn_b"]
    kn_w = inputs["L0_attn_kn_w"]; kn_b = inputs["L0_attn_kn_b"]
    sinks = inputs["L0_attn_sinks"]
    T = SEQ
    j = np.arange(32)
    f = 1.0 / (10000.0 ** (2 * j / 64.0))
    ang = np.arange(T)[:, None] * f[None, :]
    cos = np.concatenate([np.cos(ang), np.cos(ang)], 1).astype(np.float32)
    sin = np.concatenate([-np.sin(ang), np.sin(ang)], 1).astype(np.float32)
    DL = [-384, -256, -128, 0, 128, 256]
    km = np.arange(128)[:, None]
    qm = np.arange(512)[None, :]
    msk = np.zeros((128, 6 * 512), np.float32)
    for m, d in enumerate(DL):
        r = km - qm - d
        msk[:, m * 512:(m + 1) * 512] = ((r <= 0) & (r > -256)).astype(np.float32)
    qn4 = np.zeros((128, 2048), np.float32)
    qn4[:, 0:512] = np.tile(qn_w, 8)[None, :]
    qn4[:, 512:1024] = np.tile(qn_b, 8)[None, :]
    qn4[:, 1024:1536] = np.tile(kn_w, 8)[None, :]
    qn4[:, 1536:2048] = np.tile(kn_b, 8)[None, :]
    oh = np.zeros((8, 8 * 128), np.float32)
    for h in range(8):
        oh[h, h * 128:(h + 1) * 128] = 1.0
    o1 = np.ones((1, 64), np.float32)
    idm = np.eye(128, dtype=np.float32)
    in_maps = []
    for c in range(N_CORES):
        b = c // 2
        hg = c % 2
        sl = slice(hg * 512, (hg + 1) * 512)
        in_maps.append({
            "x": np.ascontiguousarray(x[b]),
            "wq": np.ascontiguousarray(qkv_w[0 * 1024:1 * 1024][sl].T),
            "wk": np.ascontiguousarray(qkv_w[1 * 1024:2 * 1024][sl].T),
            "wv": np.ascontiguousarray(qkv_w[2 * 1024:3 * 1024][sl].T),
            "wp": np.ascontiguousarray(proj_w[:, sl].T),
            "cos": cos, "sin": sin, "qn": qn4, "msk": msk,
            "sink": np.ascontiguousarray(sinks[hg * 8:(hg + 1) * 8][None, :]),
            "idm": idm, "oh": oh, "o1": o1,
            "cst": np.broadcast_to(np.array([1e-5, 0, 0, 0, 0, 0, 0, 0], np.float32), (128, 8)).copy(),
        })
    return in_maps
def build_attn(kind):
    nc = bass.Bass("TRN2", target_bir_lowering=False, debug=False)
    T = SEQ
    NB = T // 128
    NQ = T // 512
    SCALE = 1.0 / 8.0
    causal = kind in ("l1a", "l2")
    has_rope = kind in ("l1a", "l1b")
    has_bias = kind == "l1b"

    d_x = nc.dram_tensor("x", [T, DM], F32, kind="ExternalInput").ap()
    d_wq = nc.dram_tensor("wq", [DM, 512], F32, kind="ExternalInput").ap()
    d_wk = nc.dram_tensor("wk", [DM, 512], F32, kind="ExternalInput").ap()
    d_wv = nc.dram_tensor("wv", [DM, 512], F32, kind="ExternalInput").ap()
    d_wp = nc.dram_tensor("wp", [512, DM], F32, kind="ExternalInput").ap()
    d_id = nc.dram_tensor("idm", [128, 128], F32, kind="ExternalInput").ap()
    d_oh = nc.dram_tensor("oh", [8, 8 * 128], F32, kind="ExternalInput").ap()
    d_o1 = nc.dram_tensor("o1", [1, 64], F32, kind="ExternalInput").ap()
    d_cst = nc.dram_tensor("cst", [128, 8], F32, kind="ExternalInput").ap()
    d_o2c = nc.dram_tensor("o2c", [128, 2], F32, kind="ExternalInput").ap()
    if has_rope:
        d_wqs = nc.dram_tensor("wqs", [DM, 512], F32, kind="ExternalInput").ap()
        d_wks = nc.dram_tensor("wks", [DM, 512], F32, kind="ExternalInput").ap()
        d_cfm = nc.dram_tensor("cfm", [128, T], F32, kind="ExternalInput").ap()
        d_sfm = nc.dram_tensor("sfm", [128, T], F32, kind="ExternalInput").ap()
    if has_bias:
        d_bq = nc.dram_tensor("bq", [1, 512], F32, kind="ExternalInput").ap()
        d_bqs = nc.dram_tensor("bqs", [1, 512], F32, kind="ExternalInput").ap()
        d_bk = nc.dram_tensor("bk", [1, 512], F32, kind="ExternalInput").ap()
        d_bks = nc.dram_tensor("bks", [1, 512], F32, kind="ExternalInput").ap()
        d_bv = nc.dram_tensor("bv", [1, 512], F32, kind="ExternalInput").ap()
    if causal:
        d_msk = nc.dram_tensor("msk", [128, 4 * 512], F32, kind="ExternalInput").ap()
    if kind == "l2":
        d_wqp = nc.dram_tensor("wqp", [DM, 512], F32, kind="ExternalInput").ap()
        d_wkp = nc.dram_tensor("wkp", [DM, 512], F32, kind="ExternalInput").ap()
        d_lam = nc.dram_tensor("lam", [1, 8], F32, kind="ExternalInput").ap()
    if kind == "l3":
        d_w1 = nc.dram_tensor("w1", [DM, 2048], F32, kind="ExternalInput").ap()
        d_w2 = nc.dram_tensor("w2", [2048, 512], F32, kind="ExternalInput").ap()
        d_b1c = nc.dram_tensor("b1c", [128, 16], F32, kind="ExternalInput").ap()
        d_w1c0 = nc.dram_tensor("w1c0", [128, 16], F32, kind="ExternalInput").ap()
        d_b2c = nc.dram_tensor("b2c", [128, 4], F32, kind="ExternalInput").ap()
        d_etac = nc.dram_tensor("etac", [128, 4], F32, kind="ExternalInput").ap()
        d_sgnc = nc.dram_tensor("sgnc", [128, 4], F32, kind="ExternalInput").ap()
    d_out = nc.dram_tensor("out", [2 * DM if kind == "l2" else DM, T], F32,
                           kind="ExternalOutput").ap()

    es = ExitStack()
    sb = lambda n, s, dt=F32: es.enter_context(nc.sbuf_tensor(n, s, dt))
    ps = lambda n, s, dt=F32: es.enter_context(nc.psum_tensor(n, s, dt))
    with es:
        xt = sb("s_xt", [128, 2 * 1024])
        xn_tm = sb("s_xn_tm", [128, 4 * 1024], BF16)
        xn_fm = sb("s_xn_fm", [128, 8 * 512], BF16)
        q_fm = sb("s_q_fm", [128, 2 * 512] if kind == "l2" else [128, 4 * T], BF16)
        k_fm = sb("s_k_fm", [128, 2 * T] if kind == "l2" else [128, 4 * T], BF16)
        v_tm = sb("s_v_tm", [128, NB * 8 * 66], BF16)
        ao = sb("s_ao", [128, 4 * 512], BF16)
        wq = sb("s_wq", [128, 8 * 512], BF16)
        wk = sb("s_wk", [128, 8 * 512], BF16)
        wv = sb("s_wv", [128, 8 * 512], BF16)
        wp = sb("s_wp", [128, 4 * 1024], BF16)
        idm = sb("s_idm", [128, 128], BF16)
        oh8 = sb("s_oh8", [8, 8 * 128], BF16)
        o1 = sb("s_o1", [1, 64], BF16)
        cst = sb("s_cst", [128, 8])
        qraw = sb("s_qraw", [128, 512])
        sq = sb("s_sq", [128, 512], BF16)
        u1 = sb("s_u1", [128, 512])
        u2 = sb("s_u2", [128, 512])
        nq8 = sb("s_nq8", [128, 8])
        runq = sb("s_runq", [128, 8]); runk = sb("s_runk", [128, 8])
        runb = sb("s_runb", [128, 8], BF16)
        mx8 = sb("s_mx8", [8, 8])
        csb = sb("s_csb", [8, 2], BF16)
        bias_col = sb("s_bias_col", [128, 8])
        ptA = sb("s_ptA", [128, 512], BF16)
        ptB = sb("s_ptB", [128, 512], BF16)
        den = sb("s_den", [1, 512])
        denb = sb("s_denb", [1, 512], BF16)
        st6 = sb("s_st6", [128, 12]); st2 = sb("s_st2", [128, 2])
        rstd1 = sb("s_rstd1", [128, 1]); b2c_ = sb("s_b2c", [128, 1])
        stats = dict(s6=st6, s2=st2, rstd=rstd1, b2=b2c_, eps=cst[:, 0:1])
        onesrow = sb("s_onesrow", [1, 512], BF16)
        o2c = sb("s_o2c", [128, 2], BF16)
        if has_rope:
            wqs = sb("s_wqs", [128, 8 * 512], BF16)
            wks = sb("s_wks", [128, 8 * 512], BF16)
            cfm = sb("s_cfm", [128, 512], BF16)
            sfm = sb("s_sfm", [128, 512], BF16)
        if has_bias:
            bq = sb("s_bq", [1, 512], BF16); bqs = sb("s_bqs", [1, 512], BF16)
            bk = sb("s_bk", [1, 512], BF16); bks = sb("s_bks", [1, 512], BF16)
            bv = sb("s_bv", [1, 512], BF16)
        if causal:
            msk = sb("s_msk", [128, 4 * 512], BF16)
        if kind == "l2":
            wqp = sb("s_wqp", [128, 8 * 512], BF16)
            wkp = sb("s_wkp", [128, 8 * 512], BF16)
            lamr = sb("s_lamr", [1, 8])
            lam_t = sb("s_lam_t", [128, 512], BF16)
            kc_fm = sb("s_kc_fm", [128, 2 * T], BF16)
            ks_fm = sb("s_ks_fm", [128, 2 * T], BF16)
            qc_fm = sb("s_qc_fm", [128, 2 * 512], BF16)
            qs_fm = sb("s_qs_fm", [128, 2 * 512], BF16)
            u3 = sb("s_u3", [128, 512])
            u4 = sb("s_u4", [128, 512])
        if kind == "l3":
            w1 = sb("s_w1", [128, 2 * 8 * 128], BF16)
            w2 = sb("s_w2", [128, 2 * 512], BF16)
            b1c = sb("s_b1c", [128, 16]); w1c0 = sb("s_w1c0", [128, 16])
            b2cc = sb("s_b2cc", [128, 4]); etac = sb("s_etac", [128, 4])
            sgnc = sb("s_sgnc", [128, 4])
            h1_fm = sb("s_h1_fm", [128, 2 * 512], BF16)
            g_fm = sb("s_g_fm", [128, 4 * 512], BF16)
            vm_tm = sb("s_vm_tm", [128, 4 * 1024], BF16)
            vm_fm = sb("s_vm_fm", [128, 8 * 512], BF16)

        PA = ps("PA", [128, 512]); PB = ps("PB", [128, 512])
        PC = ps("PC", [128, 512])
        PEb = ps("PEb", [128, 512])
        PF = ps("PF", [128, 512]); PH = ps("PH", [128, 512])
        PG1 = ps("PG1", [128, 1024], BF16)
        PG2 = ps("PG2", [128, 1024], BF16)

        plan = Plan(nc)
        g, s, v, a = "gpsimd", "sync", "vector", "scalar"

        def ld(dst, src, q=g):
            plan.dma(q, dst, src)

        ld(wq[:].rearrange("p (c o) -> p c o", c=8), d_wq.rearrange("(c p) o -> p c o", p=128))
        ld(wk[:].rearrange("p (c o) -> p c o", c=8), d_wk.rearrange("(c p) o -> p c o", p=128))
        ld(wv[:].rearrange("p (c o) -> p c o", c=8), d_wv.rearrange("(c p) o -> p c o", p=128))
        ld(wp[:].rearrange("p (c o) -> p c o", c=4), d_wp.rearrange("(c p) o -> p c o", p=128))
        ld(idm[:], d_id)
        ld(oh8[:], d_oh)
        ld(o1[:], d_o1)
        ld(cst[:], d_cst, q=s)
        ld(o2c[:], d_o2c)
        if has_rope:
            ld(wqs[:].rearrange("p (c o) -> p c o", c=8), d_wqs.rearrange("(c p) o -> p c o", p=128))
            ld(wks[:].rearrange("p (c o) -> p c o", c=8), d_wks.rearrange("(c p) o -> p c o", p=128))
        if has_bias:
            ld(bq[:], d_bq); ld(bqs[:], d_bqs); ld(bk[:], d_bk); ld(bks[:], d_bks); ld(bv[:], d_bv)
        if causal:
            ld(msk[:], d_msk)
        if kind == "l2":
            ld(wqp[:].rearrange("p (c o) -> p c o", c=8), d_wqp.rearrange("(c p) o -> p c o", p=128))
            ld(wkp[:].rearrange("p (c o) -> p c o", c=8), d_wkp.rearrange("(c p) o -> p c o", p=128))
            ld(lamr[:], d_lam, q=s)
        if kind == "l3":
            ld(b1c[:], d_b1c, q=s); ld(w1c0[:], d_w1c0, q=s)
            ld(b2cc[:], d_b2c, q=s); ld(etac[:], d_etac, q=s); ld(sgnc[:], d_sgnc, q=s)

        ev(plan, v, lambda e: e.memset(
            v_tm[:].rearrange("p (b h f) -> p b h f", b=NB, h=8)[:, :, :, 64:66], 1.0),
            writes=[v_tm], tag="vones")
        ev(plan, v, lambda e: e.memset(runq[:], 0.0), writes=[runq])
        ev(plan, v, lambda e: e.memset(runk[:], 0.0), writes=[runk])
        ev(plan, v, lambda e: e.memset(onesrow[:], 1.0), writes=[onesrow])

        nhp = 2 if kind == "l2" else 4
        q_fm3 = q_fm[:].rearrange("p (hp t) -> p hp t", hp=nhp)
        k_fm3 = k_fm[:].rearrange("p (hp t) -> p hp t", hp=nhp)
        v_tm4 = v_tm[:].rearrange("p (b h f) -> p b h f", b=NB, h=8)
        xt3 = xt[:].rearrange("p (b f) -> p b f", b=2)
        xn_tm3 = xn_tm[:].rearrange("p (b f) -> p b f", b=4)
        xn_fm3 = xn_fm[:].rearrange("p (c t) -> p c t", c=8)
        wq3 = wq[:].rearrange("p (c o) -> p c o", c=8)
        wk3 = wk[:].rearrange("p (c o) -> p c o", c=8)
        wv3 = wv[:].rearrange("p (c o) -> p c o", c=8)
        wp3 = wp[:].rearrange("p (c o) -> p c o", c=4)
        x_dram3 = d_x.rearrange("(b p) f -> p b f", p=128)
        out_dram3 = d_out.rearrange("(c p) t -> c p t", p=128)
        ao2 = ao[:].rearrange("p (c t) -> p c t", c=4)[:, 0:2, :]
        ao3 = ao[:].rearrange("p (c t) -> p c t", c=4)
        if has_rope:
            wqs3 = wqs[:].rearrange("p (c o) -> p c o", c=8)
            wks3 = wks[:].rearrange("p (c o) -> p c o", c=8)
        if kind == "l2":
            wqp3 = wqp[:].rearrange("p (c o) -> p c o", c=8)
            wkp3 = wkp[:].rearrange("p (c o) -> p c o", c=8)
            kc_fm3 = kc_fm[:].rearrange("p (hp t) -> p hp t", hp=2)
            ks_fm3 = ks_fm[:].rearrange("p (hp t) -> p hp t", hp=2)
            qc_fm3 = qc_fm[:].rearrange("p (hp t) -> p hp t", hp=2)
            qs_fm3 = qs_fm[:].rearrange("p (hp t) -> p hp t", hp=2)
        if kind == "l3":
            w14 = w1[:].rearrange("p (b c o) -> p b c o", b=2, c=8)
            w23 = w2[:].rearrange("p (b o) -> p b o", b=2)
            d_w1_4 = d_w1.rearrange("(c p) o -> p c o", p=128)
            d_w2_3 = d_w2.rearrange("(c p) o -> p c o", p=128)
            h1_fm3 = h1_fm[:].rearrange("p (c o) -> p c o", c=2)
            g_fm3 = g_fm[:].rearrange("p (c o) -> p c o", c=4)
            vm_tm3 = vm_tm[:].rearrange("p (b f) -> p b f", b=4)
            vm_fm3 = vm_fm[:].rearrange("p (c t) -> p c t", c=8)

        def bc64(ap2):
            return ap2.rearrange("p (h o) -> p h o", h=8).broadcast_to([128, 8, 64])

        def ln_and_transpose(ch, dst_fm3, src_sel="xn"):
            for half in range(2):
                plan.dma(s, xt3[:, :, :], x_dram3[:, 4 * ch + 2 * half:4 * ch + 2 * half + 2, :],
                         tag=f"ldx{ch}.{half}")
                for jj in range(2):
                    j = 2 * half + jj
                    blk = 4 * ch + j
                    ln_plain_tile(plan, None, xn_tm3[:, j, :], xt3[:, jj, :], stats, tag=f"ln{blk}")
            for j in range(4):
                blk = 4 * ch + j
                src3 = xn_tm3 if src_sel == "xn" else vm_tm3
                for cc in range(8):
                    PG = PG1 if cc % 2 == 0 else PG2
                    transpose_step(plan, PG[:, 0:128], src3[:, j, cc * 128:(cc + 1) * 128], idm[:],
                                   tag=f"xtr{blk}.{cc}")
                    ev(plan, a, lambda e, cc=cc, j=j, PG=PG, dst_fm3=dst_fm3: e.copy(
                        dst_fm3[:, cc, j * 128:(j + 1) * 128], PG[:, 0:128]),
                       reads=[PG], writes=[dst_fm3], tag=f"xtre{blk}.{cc}")

        def logmap(ch):
            for j in range(4):
                blk = 4 * ch + j
                xnb = xn_tm3[:, j, :]
                ev(plan, v, lambda e, xnb=xnb: e.tensor_scalar(u1[:, 0:512], xnb[:, 0:512], -8.0, 8.0, ALU.max, ALU.min),
                   reads=[xn_tm], writes=[u1])
                ev(plan, v, lambda e, xnb=xnb: e.tensor_scalar(u2[:, 0:512], xnb[:, 512:1024], -8.0, 8.0, ALU.max, ALU.min),
                   reads=[xn_tm], writes=[u2])
                ev(plan, a, lambda e: e.activation(sq[:], u1[:], AF.Square, accum_out=nq8[:, 0:1]),
                   reads=[u1], writes=[sq, nq8])
                ev(plan, a, lambda e: e.activation(sq[:], u2[:], AF.Square, accum_out=nq8[:, 1:2]),
                   reads=[u2], writes=[sq, nq8])
                ev(plan, v, lambda e: e.tensor_mul(nq8[:, 3:4], u1[:, 0:1], u1[:, 0:1]),
                   reads=[u1], writes=[nq8])
                ev(plan, v, lambda e: e.tensor_add(nq8[:, 2:3], nq8[:, 0:1], nq8[:, 1:2]),
                   reads=[nq8], writes=[nq8])
                ev(plan, v, lambda e: e.tensor_sub(nq8[:, 2:3], nq8[:, 2:3], nq8[:, 3:4]),
                   reads=[nq8], writes=[nq8])
                ev(plan, a, lambda e: e.activation(nq8[:, 4:5], nq8[:, 2:3], AF.Sqrt, bias=cst[:, 1:2]),
                   reads=[nq8, cst], writes=[nq8])
                ev(plan, v, lambda e: e.tensor_scalar(nq8[:, 5:6], nq8[:, 4:5], float(1 + 1e-6), None, ALU.max),
                   reads=[nq8], writes=[nq8])
                ev(plan, v, lambda e: e.scalar_tensor_tensor(runb[:, 0:1], nq8[:, 5:6], 1.0, nq8[:, 5:6], ALU.mult, ALU.mult),
                   reads=[nq8], writes=[runb])
                ev(plan, a, lambda e: e.activation(qraw[:, 0:1], runb[:, 0:1], AF.Sqrt, bias=cst[:, 2:3]),
                   reads=[runb, cst], writes=[qraw])
                ev(plan, v, lambda e: e.tensor_add(qraw[:, 1:2], qraw[:, 0:1], nq8[:, 5:6]),
                   reads=[qraw, nq8], writes=[qraw])
                ev(plan, a, lambda e: e.activation(qraw[:, 2:3], qraw[:, 1:2], AF.Ln),
                   reads=[qraw], writes=[qraw])
                ev(plan, v, lambda e: e.tensor_scalar(qraw[:, 3:4], qraw[:, 2:3], 1e-3, None, ALU.max),
                   reads=[qraw], writes=[qraw])
                ev(plan, v, lambda e: e.tensor_sub(qraw[:, 4:5], nq8[:, 4:5], nq8[:, 5:6]),
                   reads=[nq8], writes=[qraw])
                ev(plan, v, lambda e: e.tensor_mul(qraw[:, 5:6], qraw[:, 4:5], qraw[:, 4:5]),
                   reads=[qraw], writes=[qraw])
                ev(plan, v, lambda e: e.tensor_sub(qraw[:, 6:7], nq8[:, 2:3], qraw[:, 5:6]),
                   reads=[nq8, qraw], writes=[qraw])
                ev(plan, v, lambda e: e.tensor_scalar(qraw[:, 6:7], qraw[:, 6:7], 1e-6, None, ALU.max),
                   reads=[qraw], writes=[qraw])
                ev(plan, a, lambda e: e.activation(qraw[:, 7:8], qraw[:, 6:7], AF.Sqrt),
                   reads=[qraw], writes=[qraw])
                ev(plan, v, lambda e: e.tensor_scalar_add(qraw[:, 7:8], qraw[:, 7:8], 1e-6),
                   reads=[qraw], writes=[qraw])
                ev(plan, v, lambda e: e.reciprocal(qraw[:, 8:9], qraw[:, 7:8]),
                   reads=[qraw], writes=[qraw])
                ev(plan, v, lambda e: e.tensor_mul(qraw[:, 9:10], qraw[:, 8:9], qraw[:, 3:4]),
                   reads=[qraw], writes=[qraw])
                ev(plan, v, lambda e, j=j: e.tensor_scalar_mul(vm_tm3[:, j, 0:512], u1[:, 0:512], qraw[:, 9:10]),
                   reads=[u1, qraw], writes=[vm_tm])
                ev(plan, v, lambda e, j=j: e.tensor_scalar_mul(vm_tm3[:, j, 512:1024], u2[:, 0:512], qraw[:, 9:10]),
                   reads=[u2, qraw], writes=[vm_tm])
                ev(plan, v, lambda e, j=j: e.tensor_mul(vm_tm3[:, j, 0:1], qraw[:, 4:5], qraw[:, 9:10]),
                   reads=[qraw], writes=[vm_tm])

        if kind in ("l1a", "l1b"):
            for ch in range(NQ):
                ln_and_transpose(ch, xn_fm3)
                tb = ch
                plan.dma(g, cfm[:], d_cfm[:, tb * 512:(tb + 1) * 512], tag=f"ldc{ch}")
                plan.dma(g, sfm[:], d_sfm[:, tb * 512:(tb + 1) * 512], tag=f"lds{ch}")
                for j in range(4):
                    blk = 4 * ch + j
                    pairs = [(xn_fm3[:, ic, j * 128:(j + 1) * 128], wv3[:, ic, :]) for ic in range(8)]
                    if has_bias:
                        pairs.append((onesrow[0:1, 0:128], bv[:]))
                    mm(plan, PA[:], pairs, tag=f"v{blk}")
                    ev(plan, v, lambda e, blk=blk: e.tensor_copy(
                        v_tm4[:, blk, :, 0:64], PA[:].rearrange("p (h d) -> p h d", h=8)),
                       reads=[PA], writes=[v_tm], tag=f"ve{blk}")
                for hp in range(4):
                    for (wmat, wsm, bias2, dst3, run) in (
                            (wq3, wqs3, ("bq", "bqs"), q_fm3, runq),
                            (wk3, wks3, ("bk", "bks"), k_fm3, runk)):
                        pairs = [(wmat[:, ic, hp * 128:(hp + 1) * 128], xn_fm3[:, ic, :]) for ic in range(8)]
                        pairs_s = [(wsm[:, ic, hp * 128:(hp + 1) * 128], xn_fm3[:, ic, :]) for ic in range(8)]
                        if has_bias:
                            bmap = dict(bq=bq, bqs=bqs, bk=bk, bks=bks)
                            brow = bmap[bias2[0]]; bsrow = bmap[bias2[1]]
                            pairs.append((brow[:, hp * 128:(hp + 1) * 128], onesrow[:]))
                            pairs_s.append((bsrow[:, hp * 128:(hp + 1) * 128], onesrow[:]))
                        mm(plan, PA[:], pairs, tag=f"qf{ch}.{hp}")
                        mm(plan, PB[:], pairs_s, tag=f"qfs{ch}.{hp}")
                        cslice = cfm[:]
                        sslice = sfm[:]
                        ev(plan, v, lambda e, cslice=cslice: e.tensor_tensor(u1[:], PA[:], cslice, ALU.mult),
                           reads=[PA, cfm], writes=[u1])
                        ev(plan, v, lambda e, sslice=sslice: e.tensor_tensor(u2[:], PB[:], sslice, ALU.mult),
                           reads=[PB, sfm], writes=[u2])
                        ev(plan, v, lambda e, dst3=dst3, hp=hp, tb=tb: e.tensor_tensor(
                            dst3[:, hp, tb * 512:(tb + 1) * 512], u1[:], u2[:], ALU.add),
                           reads=[u1, u2], writes=[dst3], tag=f"rp{ch}.{hp}")
                        ev(plan, a, lambda e, dst3=dst3, hp=hp, tb=tb: e.activation(
                            sq[:], dst3[:, hp, tb * 512:(tb + 1) * 512], AF.Square),
                           reads=[dst3], writes=[sq])
                        mm(plan, PEb[0:2, :], [(o2c[:], sq[:])], tag=f"nm{ch}.{hp}")
                        ev(plan, v, lambda e: e.reduce_max(nq8[0:2, 0:1], PEb[0:2, :], axis=AX.X),
                           reads=[PEb], writes=[nq8])
                        ev(plan, v, lambda e, run=run, hp=hp: e.tensor_max(
                            run[0:2, hp:hp + 1], run[0:2, hp:hp + 1], nq8[0:2, 0:1]),
                           reads=[run, nq8], writes=[run], tag=f"nmx{ch}.{hp}")
        elif kind == "l3":
            ev(plan, v, lambda e: e.tensor_sub(b1c[:], b1c[:], w1c0[:]), reads=[b1c, w1c0], writes=[b1c])
            for ch in range(NQ):
                ln_and_transpose(ch, xn_fm3)
                logmap(ch)
                for j in range(4):
                    blk = 4 * ch + j
                    for cc in range(8):
                        PG = PG1 if cc % 2 == 0 else PG2
                        transpose_step(plan, PG[:, 128:256], vm_tm3[:, j, cc * 128:(cc + 1) * 128], idm[:],
                                       tag=f"vtr{blk}.{cc}")
                        ev(plan, a, lambda e, cc=cc, j=j, PG=PG: e.copy(
                            vm_fm3[:, cc, j * 128:(j + 1) * 128], PG[:, 128:256]),
                           reads=[PG], writes=[vm_fm], tag=f"vtre{blk}.{cc}")
                tb = ch
                GP = [PC, PEb, PF, PH]
                for ic16 in range(16):
                    P = PA if ic16 % 2 == 0 else PB
                    hb = h1_fm3[:, ic16 % 2, :]
                    wb = ic16 % 2
                    plan.dma(g, w14[:, wb, :, :], d_w1_4[:, :, ic16 * 128:(ic16 + 1) * 128],
                             tag=f"lw1{ch}.{ic16}")
                    plan.dma(g, w23[:, wb, :], d_w2_3[:, ic16, :], tag=f"lw2{ch}.{ic16}")
                    mm(plan, P[:], [(w14[:, wb, ic, :], xn_fm3[:, ic, :]) for ic in range(8)],
                       tag=f"h1{ch}.{ic16}")
                    ev(plan, a, lambda e, ic16=ic16, P=P, hb=hb: e.activation(
                        hb, P[:], AF.Tanh, bias=b1c[:, ic16:ic16 + 1]),
                       reads=[P, b1c], writes=[h1_fm], tag=f"h1e{ch}.{ic16}")
                    for oc in range(4):
                        mm(plan, GP[oc][:], [(w23[:, wb, oc * 128:(oc + 1) * 128], hb)],
                           start=(ic16 == 0), stop=(ic16 == 15), tag=f"g{ch}.{ic16}.{oc}")
                for oc in range(4):
                    ev(plan, a, lambda e, oc=oc: e.activation(
                        u1[:], GP[oc][:], AF.Tanh, bias=b2cc[:, oc:oc + 1]),
                       reads=[GP[oc], b2cc], writes=[u1], tag=f"ge{ch}.{oc}")
                    ev(plan, v, lambda e, oc=oc: e.tensor_scalar(
                        u1[:], u1[:], 0.1, etac[:, oc:oc + 1], ALU.mult, ALU.add),
                       reads=[u1, etac], writes=[u1])
                    ev(plan, a, lambda e, oc=oc: e.activation(u1[:], u1[:], AF.Abs),
                       reads=[u1], writes=[u1])
                    ev(plan, v, lambda e, oc=oc: e.tensor_scalar_mul(
                        g_fm3[:, oc, :], u1[:], sgnc[:, oc:oc + 1]),
                       reads=[u1, sgnc], writes=[g_fm])
                for j in range(4):
                    blk = 4 * ch + j
                    mm(plan, PC[:], [(vm_fm3[:, ic, j * 128:(j + 1) * 128], wv3[:, ic, :]) for ic in range(8)],
                       tag=f"v{blk}")
                    ev(plan, v, lambda e, blk=blk: e.tensor_copy(
                        v_tm4[:, blk, :, 0:64], PC[:].rearrange("p (h d) -> p h d", h=8)),
                       reads=[PC], writes=[v_tm], tag=f"ve{blk}")
                for hp in range(4):
                    mm(plan, PA[:], [(wq3[:, ic, hp * 128:(hp + 1) * 128], vm_fm3[:, ic, :]) for ic in range(8)],
                       tag=f"q{ch}.{hp}")
                    ev(plan, v, lambda e, hp=hp, tb=tb: e.tensor_tensor(
                        q_fm3[:, hp, tb * 512:(tb + 1) * 512], PA[:], g_fm3[:, hp, :], ALU.mult),
                       reads=[PA, g_fm], writes=[q_fm], tag=f"qe{ch}.{hp}")
                    mm(plan, PB[:], [(wk3[:, ic, hp * 128:(hp + 1) * 128], vm_fm3[:, ic, :]) for ic in range(8)],
                       tag=f"k{ch}.{hp}")
                    ev(plan, a, lambda e, hp=hp, tb=tb: e.copy(
                        k_fm3[:, hp, tb * 512:(tb + 1) * 512], PB[:]),
                       reads=[PB], writes=[k_fm], tag=f"ke{ch}.{hp}")
                    for dst3, run in ((q_fm3, runq), (k_fm3, runk)):
                        ev(plan, a, lambda e, dst3=dst3, hp=hp, tb=tb: e.activation(
                            sq[:], dst3[:, hp, tb * 512:(tb + 1) * 512], AF.Square),
                           reads=[dst3], writes=[sq])
                        mm(plan, PEb[0:2, :], [(o2c[:], sq[:])], tag=f"nm{ch}.{hp}")
                        ev(plan, v, lambda e: e.reduce_max(nq8[0:2, 0:1], PEb[0:2, :], axis=AX.X),
                           reads=[PEb], writes=[nq8])
                        ev(plan, v, lambda e, run=run, hp=hp: e.tensor_max(
                            run[0:2, hp:hp + 1], run[0:2, hp:hp + 1], nq8[0:2, 0:1]),
                           reads=[run, nq8], writes=[run])
        elif kind == "l2":
            ev(plan, a, lambda e: e.activation(den[0:1, 0:8], lamr[:], AF.Tanh),
               reads=[lamr], writes=[den])
            ev(plan, v, lambda e: e.tensor_copy(
                denb[0:1, 0:512].rearrange("p (h d) -> p h d", h=8),
                den[0:1, 0:8].rearrange("p (h o) -> p h o", h=8).broadcast_to([1, 8, 64])),
               reads=[den], writes=[denb])
            mm(plan, PEb[:], [(onesrow[0:1, 0:128], denb[0:1, 0:512])], tag="lamb")
            ev(plan, v, lambda e: e.tensor_scalar(lam_t[:], PEb[:], 0.5, None, ALU.mult),
               reads=[PEb], writes=[lam_t])

            def phase_block(blk, Praw, Pph, hg2, with_lam):
                W_ = slice(0, 256)
                ev(plan, a, lambda e, P=Praw: e.copy(qraw[:, W_], P[:, W_]), reads=[Praw], writes=[qraw])
                ev(plan, a, lambda e: e.activation(u1[:, W_], qraw[:, W_], AF.Abs),
                   reads=[qraw], writes=[u1])
                ev(plan, a, lambda e: e.activation(u1[:, W_], u1[:, W_], AF.Exp, scale=-1.0),
                   reads=[u1], writes=[u1])
                ev(plan, a, lambda e: e.activation(u1[:, W_], u1[:, W_], AF.Ln, bias=cst[:, 5:6]),
                   reads=[u1, cst], writes=[u1])
                ev(plan, v, lambda e: e.tensor_scalar(u2[:, W_], qraw[:, W_], 0.0, None, ALU.max),
                   reads=[qraw], writes=[u2])
                ev(plan, v, lambda e: e.tensor_add(u1[:, W_], u1[:, W_], u2[:, W_]), reads=[u1, u2], writes=[u1])
                ev(plan, a, lambda e: e.activation(sq[:, W_], u1[:, W_], AF.Square), reads=[u1], writes=[sq])
                ev(plan, v, lambda e: e.tensor_reduce(
                    nq8[:, 0:4].rearrange("p (h o) -> p h o", h=4),
                    sq[:, W_].rearrange("p (h d) -> p h d", h=4), AX.X, ALU.add),
                   reads=[sq], writes=[nq8])
                ev(plan, a, lambda e: e.activation(nq8[:, 0:4], nq8[:, 0:4], AF.Sqrt, bias=cst[:, 6:7], scale=1.0 / 64),
                   reads=[nq8, cst], writes=[nq8])
                ev(plan, v, lambda e: e.reciprocal(nq8[:, 0:4], nq8[:, 0:4]), reads=[nq8], writes=[nq8])
                ev(plan, v, lambda e: e.tensor_tensor(
                    u1[:, W_].rearrange("p (h d) -> p h d", h=4),
                    u1[:, W_].rearrange("p (h d) -> p h d", h=4),
                    nq8[:, 0:4].rearrange("p (h o) -> p h o", h=4).broadcast_to([128, 4, 64]), ALU.mult),
                   reads=[u1, nq8], writes=[u1])
                ev(plan, a, lambda e, P=Pph: e.activation(u2[:, W_], P[:, W_], AF.Tanh), reads=[Pph], writes=[u2])
                ev(plan, a, lambda e: e.activation(u3[:, W_], u2[:, W_], AF.Sin, scale=float(np.pi)),
                   reads=[u2], writes=[u3])
                ev(plan, a, lambda e: e.activation(u4[:, W_], u2[:, W_], AF.Sin, scale=float(np.pi / 2)),
                   reads=[u2], writes=[u4])
                ev(plan, a, lambda e: e.activation(u4[:, W_], u4[:, W_], AF.Square), reads=[u4], writes=[u4])
                ev(plan, v, lambda e: e.tensor_scalar(u4[:, W_], u4[:, W_], -2.0, 1.0, ALU.mult, ALU.add),
                   reads=[u4], writes=[u4])
                ev(plan, v, lambda e: e.tensor_mul(u4[:, W_], u4[:, W_], u1[:, W_]), reads=[u4, u1], writes=[u4])
                ev(plan, v, lambda e: e.tensor_mul(u3[:, W_], u3[:, W_], u1[:, W_]), reads=[u3, u1], writes=[u3])
                if with_lam:
                    lsl = lam_t[:, hg2 * 256:(hg2 + 1) * 256]
                    ev(plan, v, lambda e, lsl=lsl: e.tensor_mul(u4[:, W_], u4[:, W_], lsl), reads=[u4, lam_t], writes=[u4])
                    ev(plan, v, lambda e, lsl=lsl: e.tensor_mul(u3[:, W_], u3[:, W_], lsl), reads=[u3, lam_t], writes=[u3])
                return

            qss = sb("s_qss", [128, 8])

            def l2_norm_acc(run, nh4):
                ev(plan, a, lambda e: e.activation(sq[:, 0:256], qraw[:, 0:256], AF.Square),
                   reads=[qraw], writes=[sq])
                ev(plan, v, lambda e: e.tensor_reduce(
                    qss[:, 0:4].rearrange("p (h o) -> p h o", h=4),
                    sq[:, 0:256].rearrange("p (h d) -> p h d", h=4), AX.X, ALU.add),
                   reads=[sq], writes=[qss])
                for t_ in (u4, u3):
                    ev(plan, a, lambda e, t_=t_: e.activation(sq[:, 0:256], t_[:, 0:256], AF.Square),
                       reads=[t_], writes=[sq])
                    ev(plan, v, lambda e: e.tensor_reduce(
                        nq8[:, 0:4].rearrange("p (h o) -> p h o", h=4),
                        sq[:, 0:256].rearrange("p (h d) -> p h d", h=4), AX.X, ALU.add),
                       reads=[sq], writes=[nq8])
                    ev(plan, v, lambda e: e.tensor_add(qss[:, 0:4], qss[:, 0:4], nq8[:, 0:4]),
                       reads=[qss, nq8], writes=[qss])
                ev(plan, v, lambda e, run=run: e.tensor_max(run[:, 0:4], run[:, 0:4], qss[:, 0:4]),
                   reads=[run, qss], writes=[run])

            def l2_kside(hg2):
                osl = slice(hg2 * 256, (hg2 + 1) * 256)
                for ch in range(NQ):
                    ln_and_transpose(ch, xn_fm3)
                    for j in range(4):
                        blk = 4 * ch + j
                        if hg2 == 0:
                            mm(plan, PC[:], [(xn_fm3[:, ic, j * 128:(j + 1) * 128], wv3[:, ic, :]) for ic in range(8)],
                               tag=f"v{blk}")
                            ev(plan, v, lambda e, blk=blk: e.tensor_copy(
                                v_tm4[:, blk, :, 0:64], PC[:].rearrange("p (h d) -> p h d", h=8)),
                               reads=[PC], writes=[v_tm], tag=f"ve{blk}")
                        mm(plan, PA[:, 0:256], [(xn_fm3[:, ic, j * 128:(j + 1) * 128], wk3[:, ic, osl]) for ic in range(8)],
                           tag=f"k{blk}")
                        mm(plan, PB[:, 0:256], [(xn_fm3[:, ic, j * 128:(j + 1) * 128], wkp3[:, ic, osl]) for ic in range(8)],
                           tag=f"kp{blk}")
                        phase_block(blk, PA, PB, hg2, True)
                        l2_norm_acc(runk, 4)
                        for src, dst3, tg in ((qraw, k_fm3, "tk"), (u4, kc_fm3, "tkc"), (u3, ks_fm3, "tks")):
                            ev(plan, v, lambda e, src=src: e.tensor_copy(ptA[:, 0:256], src[:, 0:256]),
                               reads=[src], writes=[ptA])
                            for hp in range(2):
                                PG = PG1 if hp % 2 == 0 else PG2
                                transpose_step(plan, PG[:, 128:256], ptA[:, hp * 128:(hp + 1) * 128], idm[:],
                                               tag=f"{tg}{blk}.{hp}")
                                ev(plan, a, lambda e, dst3=dst3, hp=hp, blk=blk, PG=PG: e.copy(
                                    dst3[:, hp, blk * 128:(blk + 1) * 128], PG[:, 128:256]),
                                   reads=[PG], writes=[dst3], tag=f"{tg}e{blk}.{hp}")

        if kind in ("l0",):
            pass
        if kind in ("l1a", "l1b", "l3"):
            ev(plan, v, lambda e: e.memset(runb[:], 0.0), writes=[runb])
            for run, col in ((runq, 0), (runk, 1)):
                ev(plan, v, lambda e, run=run: e.tensor_copy(runb[0:2, 0:4], run[0:2, 0:4]),
                   reads=[run], writes=[runb])
                transpose_step(plan, PG1[0:4, 0:128], runb[:, 0:4], idm[:], tag=f"ntr{col}")
                ev(plan, a, lambda e, col=col: e.copy(mx8[0:4, col * 2:col * 2 + 2], PG1[0:4, 0:2]),
                   reads=[PG1], writes=[mx8])
            ev(plan, v, lambda e: e.tensor_mul(mx8[0:4, 4:6], mx8[0:4, 0:2], mx8[0:4, 2:4]),
               reads=[mx8], writes=[mx8])
            ev(plan, a, lambda e: e.activation(mx8[0:4, 6:8], mx8[0:4, 4:6], AF.Sqrt),
               reads=[mx8], writes=[mx8])
            ev(plan, v, lambda e: e.tensor_scalar(csb[0:4, 0:2], mx8[0:4, 6:8], -SCALE, None, ALU.mult),
               reads=[mx8], writes=[csb])
        def emit_bias_cols(hg2=None):
            if hg2 is None:
                for h in range(8):
                    hp = h // 2; par = h % 2
                    mm(plan, PEb[:, 0:1], [(oh8[0:4, hp * 128:(hp + 1) * 128], csb[0:4, par:par + 1])],
                       tag=f"bb{h}")
                    ev(plan, a, lambda e, h=h: e.copy(bias_col[:, h:h + 1], PEb[:, 0:1]),
                       reads=[PEb], writes=[bias_col])
            else:
                for hl in range(4):
                    h = hg2 * 4 + hl
                    mm(plan, PEb[:, 0:1], [(oh8[0:4, hl * 128:(hl + 1) * 128], csb[0:4, 0:1])],
                       tag=f"bb{h}")
                    ev(plan, a, lambda e, h=h: e.copy(bias_col[:, h:h + 1], PEb[:, 0:1]),
                       reads=[PEb], writes=[bias_col])
        if kind != "l2":
            emit_bias_cols()

        msk3 = msk[:].rearrange("p (m t) -> p m t", m=4) if causal else None

        def epilogue_head(h, p0, hpl, ao_v):
            ev(plan, v, lambda e, h=h: e.tensor_scalar(
                den[:], PC[64:65, :], 1e-30, None, ALU.max),
               reads=[PC], writes=[den], tag=f"dn{h}")
            ev(plan, v, lambda e: e.reciprocal(den[:], den[:]), reads=[den], writes=[den])
            ev(plan, v, lambda e: e.tensor_copy(denb[:], den[:]), reads=[den], writes=[denb])
            mm(plan, PEb[0:64, :], [(o1[:], denb[:])], tag=f"rb{h}")
            ev(plan, a, lambda e: e.copy(u2[0:64, :], PEb[0:64, :]), reads=[PEb], writes=[u2])
            ev(plan, v, lambda e, hpl=hpl, p0=p0, ao_v=ao_v: e.tensor_tensor(
                ao_v[p0:p0 + 64, hpl, :], PC[0:64, :], u2[0:64, :], ALU.mult),
               reads=[PC, u2], writes=[ao], tag=f"ao{h}")

        if kind == "l2":
            for hg2 in range(2):
                osl = slice(hg2 * 256, (hg2 + 1) * 256)
                ev(plan, v, lambda e: e.memset(runk[:], 0.0), writes=[runk])
                l2_kside(hg2)
                ev(plan, v, lambda e: e.memset(runb[:], 0.0), writes=[runb])
                ev(plan, v, lambda e: e.tensor_copy(runb[:, 0:4], runk[:, 0:4]), reads=[runk], writes=[runb])
                transpose_step(plan, PG1[0:4, 0:128], runb[:, 0:4], idm[:], tag=f"ntrk{hg2}")
                ev(plan, a, lambda e: e.copy(ptB[0:4, 0:128], PG1[0:4, 0:128]), reads=[PG1], writes=[ptB])
                ev(plan, v, lambda e: e.tensor_reduce(
                    mx8[0:4, 1:2].rearrange("p (a o) -> p a o", a=1),
                    ptB[0:4, 0:128].rearrange("p (a t) -> p a t", a=1), AX.X, ALU.max),
                   reads=[ptB], writes=[mx8])
                for qb in range(NQ):
                    q0 = qb * 512
                    ev(plan, v, lambda e: e.memset(runq[:], 0.0), writes=[runq])
                    ln_and_transpose(qb, xn_fm3)
                    for j in range(4):
                        blk = 4 * qb + j
                        mm(plan, PA[:, 0:256], [(xn_fm3[:, ic, j * 128:(j + 1) * 128], wq3[:, ic, osl]) for ic in range(8)],
                           tag=f"q{blk}")
                        mm(plan, PB[:, 0:256], [(xn_fm3[:, ic, j * 128:(j + 1) * 128], wqp3[:, ic, osl]) for ic in range(8)],
                           tag=f"qp{blk}")
                        phase_block(blk, PA, PB, hg2, False)
                        l2_norm_acc(runq, 4)
                        for src, dst3, tg in ((qraw, q_fm3, "tq"), (u4, qc_fm3, "tqc"), (u3, qs_fm3, "tqs")):
                            ev(plan, v, lambda e, src=src: e.tensor_copy(ptA[:, 0:256], src[:, 0:256]),
                               reads=[src], writes=[ptA])
                            for hp in range(2):
                                PG = PG1 if hp % 2 == 0 else PG2
                                transpose_step(plan, PG[:, 128:256], ptA[:, hp * 128:(hp + 1) * 128], idm[:],
                                               tag=f"{tg}{blk}.{hp}")
                                ev(plan, a, lambda e, dst3=dst3, hp=hp, j=j, PG=PG: e.copy(
                                    dst3[:, hp, j * 128:(j + 1) * 128], PG[:, 128:256]),
                                   reads=[PG], writes=[dst3], tag=f"{tg}e{blk}.{hp}")
                    ev(plan, v, lambda e: e.tensor_copy(runb[:, 0:4], runq[:, 0:4]), reads=[runq], writes=[runb])
                    transpose_step(plan, PG1[0:4, 0:128], runb[:, 0:4], idm[:], tag=f"ntrq{hg2}.{qb}")
                    ev(plan, a, lambda e: e.copy(ptB[0:4, 0:128], PG1[0:4, 0:128]), reads=[PG1], writes=[ptB])
                    ev(plan, v, lambda e: e.tensor_reduce(
                        mx8[0:4, 0:1].rearrange("p (a o) -> p a o", a=1),
                        ptB[0:4, 0:128].rearrange("p (a t) -> p a t", a=1), AX.X, ALU.max),
                       reads=[ptB], writes=[mx8])
                    ev(plan, v, lambda e: e.tensor_mul(mx8[0:4, 2:3], mx8[0:4, 0:1], mx8[0:4, 1:2]),
                       reads=[mx8], writes=[mx8])
                    ev(plan, a, lambda e: e.activation(mx8[0:4, 3:4], mx8[0:4, 2:3], AF.Sqrt),
                       reads=[mx8], writes=[mx8])
                    ev(plan, v, lambda e: e.tensor_scalar(csb[0:4, 0:1], mx8[0:4, 3:4], -SCALE, None, ALU.mult),
                       reads=[mx8], writes=[csb])
                    emit_bias_cols(hg2)
                    for hl in range(4):
                        h = hg2 * 4 + hl
                        hpl = hl // 2
                        p0 = (hl % 2) * 64
                        kbs = list(range(0, 4 * qb + 4))
                        first = True
                        for i, kb in enumerate(kbs):
                            PS = PA if i % 2 == 0 else PB
                            pt_t = ptA if i % 2 == 0 else ptB
                            pairs = [(k_fm3[p0:p0 + 64, hpl, kb * 128:(kb + 1) * 128], q_fm3[p0:p0 + 64, hpl, :]),
                                     (kc_fm3[p0:p0 + 64, hpl, kb * 128:(kb + 1) * 128], qc_fm3[p0:p0 + 64, hpl, :]),
                                     (ks_fm3[p0:p0 + 64, hpl, kb * 128:(kb + 1) * 128], qs_fm3[p0:p0 + 64, hpl, :])]
                            mm(plan, PS[:], pairs, tag=f"s{qb}.{h}.{kb}")
                            ev(plan, a, lambda e, PS=PS, pt_t=pt_t, h=h: e.activation(
                                pt_t[:], PS[:], AF.Exp, bias=bias_col[:, h:h + 1], scale=SCALE),
                               reads=[PS, bias_col], writes=[pt_t], tag=f"e{qb}.{h}.{kb}")
                            if kb >= 4 * qb:
                                mi = kb - 4 * qb
                                ev(plan, v, lambda e, pt_t=pt_t, mi=mi: e.tensor_mul(pt_t[:], pt_t[:], msk3[:, mi, :]),
                                   reads=[pt_t, msk], writes=[pt_t], tag=f"m{qb}.{h}.{kb}")
                            mm(plan, PC[0:65, :], [(v_tm4[:, kb, h, 0:65], pt_t[:])],
                               start=first, stop=(i == len(kbs) - 1), tag=f"pv{qb}.{h}.{kb}")
                            first = False
                        epilogue_head(h, p0, hpl, ao2)
                    for oc in range(8):
                        PP = PF if oc % 2 == 0 else PH
                        osb = qraw if oc % 2 == 0 else u1
                        mm(plan, PP[:], [(wp3[:, 2 * hg2 + icl, oc * 128:(oc + 1) * 128], ao2[:, icl, :])
                                         for icl in range(2)],
                           tag=f"pj{hg2}.{qb}.{oc}")
                        ev(plan, a, lambda e, PP=PP, osb=osb: e.copy(osb[:], PP[:]), reads=[PP], writes=[osb])
                        plan.dma(g, out_dram3[hg2 * 8 + oc, :, q0:q0 + 512], osb[:], tag=f"st{hg2}.{qb}.{oc}")
        else:
            for qb in range(NQ):
                q0 = qb * 512
                for h in range(8):
                    hp = h // 2
                    p0 = (h % 2) * 64
                    if kind == "l0":
                        kbs = []
                    elif causal:
                        kbs = list(range(0, 4 * qb + 4))
                    else:
                        kbs = list(range(NB))
                    first = True
                    for i, kb in enumerate(kbs):
                        PS = PA if i % 2 == 0 else PB
                        pt_t = ptA if i % 2 == 0 else ptB
                        pairs = [(k_fm3[p0:p0 + 64, hp, kb * 128:(kb + 1) * 128],
                                  q_fm3[p0:p0 + 64, hp, q0:q0 + 512])]
                        mm(plan, PS[:], pairs, tag=f"s{qb}.{h}.{kb}")
                        ev(plan, a, lambda e, PS=PS, pt_t=pt_t, h=h: e.activation(
                            pt_t[:], PS[:], AF.Exp, bias=bias_col[:, h:h + 1], scale=SCALE),
                           reads=[PS, bias_col], writes=[pt_t], tag=f"e{qb}.{h}.{kb}")
                        if causal and kb >= 4 * qb:
                            mi = kb - 4 * qb
                            ev(plan, v, lambda e, pt_t=pt_t, mi=mi: e.tensor_mul(pt_t[:], pt_t[:], msk3[:, mi, :]),
                               reads=[pt_t, msk], writes=[pt_t], tag=f"m{qb}.{h}.{kb}")
                        mm(plan, PC[0:65, :], [(v_tm4[:, kb, h, 0:65], pt_t[:])],
                           start=first, stop=(i == len(kbs) - 1), tag=f"pv{qb}.{h}.{kb}")
                        first = False
                    epilogue_head(h, p0, hp, ao3)
                for oc in range(8):
                    PP = PF if oc % 2 == 0 else PH
                    osb = qraw if oc % 2 == 0 else u1
                    mm(plan, PP[:], [(wp3[:, ic, oc * 128:(oc + 1) * 128], ao3[:, ic, :]) for ic in range(4)],
                       tag=f"pj{qb}.{oc}")
                    ev(plan, a, lambda e, PP=PP, osb=osb: e.copy(osb[:], PP[:]), reads=[PP], writes=[osb])
                    plan.dma(g, out_dram3[oc, :, q0:q0 + 512], osb[:], tag=f"st{qb}.{oc}")

        plan.barrier_outputs([d_out])
        plan.build(es)
    return nc




def _rope_tables_fm():
    T = SEQ
    j = np.arange(32)
    f = 1.0 / (10000.0 ** (2 * j / 64.0))
    ang = np.arange(T)[None, :] * f[:, None]
    c = np.cos(ang); sn = np.sin(ang)
    C = np.concatenate([c, c], 0)
    S = np.concatenate([-sn, sn], 0)
    return (np.concatenate([C, C], 0).astype(np.float32),
            np.concatenate([S, S], 0).astype(np.float32))


def _swap_cols(w):
    o = w.reshape(w.shape[:-1] + (-1, 2, 32))
    o = o[..., ::-1, :]
    return np.ascontiguousarray(o.reshape(w.shape))


def _causal_masks():
    km = np.arange(128)[:, None]
    qm = np.arange(512)[None, :]
    m = np.zeros((128, 4 * 512), np.float32)
    for mi in range(4):
        m[:, mi * 512:(mi + 1) * 512] = ((km + 128 * mi) <= qm).astype(np.float32)
    return m


_CONSTS = {}


def _common_consts():
    if not _CONSTS:
        oh = np.zeros((8, 8 * 128), np.float32)
        for h in range(8):
            oh[h, h * 128:(h + 1) * 128] = 1.0
        cst = np.zeros((128, 8), np.float32)
        cst[:, 0] = 1e-5
        cst[:, 1] = 1.0 + 1e-6
        cst[:, 2] = -1.0
        cst[:, 5] = 1.0
        cst[:, 6] = 1e-6
        o2c = np.zeros((128, 2), np.float32)
        o2c[0:64, 0] = 1.0
        o2c[64:128, 1] = 1.0
        _CONSTS.update(dict(
            idm=np.eye(128, dtype=np.float32), oh=oh,
            o1=np.ones((1, 64), np.float32), cst=cst, o2c=o2c,
            msk=_causal_masks()))
    return _CONSTS


def prep_attn(kind, inputs):
    cc = _common_consts()
    C, S = _rope_tables_fm()
    in_maps = []
    for c in range(N_CORES):
        b = c // 2
        hg = c % 2
        sl = slice(hg * 512, (hg + 1) * 512)
        m = {"idm": cc["idm"], "oh": cc["oh"], "o1": cc["o1"], "cst": cc["cst"],
             "o2c": cc["o2c"]}
        if kind == "l1a":
            W = np.ascontiguousarray(inputs["L1_attn_W_qkv"])
            perm = np.concatenate([np.arange(0, 64, 2), np.arange(1, 64, 2)])
            def permcols(w):
                o = w.reshape(1024, 8, 64)[:, :, perm]
                return np.ascontiguousarray(o.reshape(1024, 512))
            wq = permcols(W[:, 0 * 1024 + hg * 512:0 * 1024 + (hg + 1) * 512])
            wk = permcols(W[:, 1 * 1024 + hg * 512:1 * 1024 + (hg + 1) * 512])
            m.update({
                "x": np.ascontiguousarray(inputs["L1_attn_x_staged"][b]),
                "wq": wq, "wqs": _swap_cols(wq),
                "wk": wk, "wks": _swap_cols(wk),
                "wv": np.ascontiguousarray(W[:, 2 * 1024 + hg * 512:2 * 1024 + (hg + 1) * 512]),
                "wp": np.ascontiguousarray(inputs["L1_attn_W_out"][sl, :]),
                "cfm": C, "sfm": S, "msk": cc["msk"],
            })
        elif kind == "l1b":
            m.update({
                "x": np.ascontiguousarray(inputs["L1_ffn_x_staged"][b]),
                "wq": np.ascontiguousarray(inputs["L1_ffn_Wq"][:, sl]),
                "wqs": _swap_cols(np.ascontiguousarray(inputs["L1_ffn_Wq"][:, sl])),
                "wk": np.ascontiguousarray(inputs["L1_ffn_Wk"][:, sl]),
                "wks": _swap_cols(np.ascontiguousarray(inputs["L1_ffn_Wk"][:, sl])),
                "wv": np.ascontiguousarray(inputs["L1_ffn_Wv"][:, sl]),
                "wp": np.ascontiguousarray(inputs["L1_ffn_Wo"][sl, :]),
                "bq": inputs["L1_ffn_bq"][sl][None, :],
                "bqs": _swap_cols(inputs["L1_ffn_bq"][sl][None, :]),
                "bk": inputs["L1_ffn_bk"][sl][None, :],
                "bks": _swap_cols(inputs["L1_ffn_bk"][sl][None, :]),
                "bv": inputs["L1_ffn_bv"][sl][None, :],
                "cfm": C, "sfm": S,
            })
        elif kind == "l2":
            m.update({
                "x": np.ascontiguousarray(inputs["L2_attn_x_staged"][b]),
                "wq": np.ascontiguousarray(inputs["L2_attn_Wq"][:, sl]),
                "wk": np.ascontiguousarray(inputs["L2_attn_Wk"][:, sl]),
                "wv": np.ascontiguousarray(inputs["L2_attn_Wv"][:, sl]),
                "wp": np.ascontiguousarray(inputs["L2_attn_Wo"][sl, :]),
                "wqp": np.ascontiguousarray(inputs["L2_attn_Wqp"][:, sl]),
                "wkp": np.ascontiguousarray(inputs["L2_attn_Wkp"][:, sl]),
                "lam": np.ascontiguousarray(inputs["L2_attn_lambda_raw"][hg * 8:(hg + 1) * 8][None, :]),
                "msk": cc["msk"],
            })
        elif kind == "l3":
            W1 = np.ascontiguousarray(inputs["L3_attn_W1"])
            W2 = np.ascontiguousarray(inputs["L3_attn_W2"])
            osl = slice(hg * 512, (hg + 1) * 512)
            eta = np.ones(1024, np.float32); eta[0] = -1.0
            sgn = np.ones(1024, np.float32); sgn[0] = -1.0
            m.update({
                "x": np.ascontiguousarray(inputs["L3_attn_x_staged"][b]),
                "wq": np.ascontiguousarray(inputs["L3_attn_Wq"][osl, :].T),
                "wk": np.ascontiguousarray(inputs["L3_attn_Wk"][osl, :].T),
                "wv": np.ascontiguousarray(inputs["L3_attn_Wv"][osl, :].T),
                "wp": np.ascontiguousarray(inputs["L3_attn_Wo"].T[osl, :]),
                "w1": np.ascontiguousarray(W1.T),
                "w2": np.ascontiguousarray(W2[osl, :].T),
                "b1c": np.ascontiguousarray(inputs["L3_attn_b1"].reshape(16, 128).T),
                "w1c0": np.ascontiguousarray(W1[:, 0].reshape(16, 128).T),
                "b2c": np.ascontiguousarray(inputs["L3_attn_b2"][osl].reshape(4, 128).T),
                "etac": np.ascontiguousarray(eta[osl].reshape(4, 128).T),
                "sgnc": np.ascontiguousarray(sgn[osl].reshape(4, 128).T),
            })
        in_maps.append(m)
    return in_maps


def build_f0(dbg=False):
    nc = bass.Bass("TRN2", target_bir_lowering=False, debug=False)
    TC = T_CORE // 2
    NB = TC // 128
    NT = TC // 512
    E = 9
    HID = 2048
    NHC = HID // 128

    d_x = nc.dram_tensor("x", [TC, DM], F32, kind="ExternalInput").ap()
    d_a0 = nc.dram_tensor("a0", [TC, DM], F32, kind="ExternalInput").ap()
    d_a1 = nc.dram_tensor("a1", [TC, DM], F32, kind="ExternalInput").ap()
    d_pb = nc.dram_tensor("pb", [128, DM], F32, kind="ExternalInput").ap()
    d_rw = nc.dram_tensor("rw", [DM, 8], F32, kind="ExternalInput").ap()
    d_eb = nc.dram_tensor("eb", [1, 8], F32, kind="ExternalInput").ap()
    d_w1 = nc.dram_tensor("w1", [E * DM, HID], F32, kind="ExternalInput").ap()
    d_w3 = nc.dram_tensor("w3", [E * DM, HID], F32, kind="ExternalInput").ap()
    d_w2 = nc.dram_tensor("w2", [E * HID, DM], F32, kind="ExternalInput").ap()
    d_id = nc.dram_tensor("idm", [128, 128], F32, kind="ExternalInput").ap()
    d_oh = nc.dram_tensor("oh", [8, 8 * 128], F32, kind="ExternalInput").ap()
    d_cst = nc.dram_tensor("cst", [128, 8], F32, kind="ExternalInput").ap()
    d_out = nc.dram_tensor("out", [TC, DM], F32, kind="ExternalOutput").ap()
    if dbg:
        d_dcw = nc.dram_tensor("dcw", [8, 1024], BF16, kind="ExternalOutput").ap()
        d_dacc = nc.dram_tensor("dacc", [128, 8 * 1024], F32, kind="ExternalOutput").ap()
        d_dxn = nc.dram_tensor("dxn", [128, 8 * 1024], BF16, kind="ExternalOutput").ap()
        d_deo = nc.dram_tensor("deo", [128, 16 * 512], BF16, kind="ExternalOutput").ap()

    es = ExitStack()
    sb = lambda n, s, dt=F32: es.enter_context(nc.sbuf_tensor(n, s, dt))
    ps = lambda n, s, dt=F32: es.enter_context(nc.psum_tensor(n, s, dt))
    with es:
        xt = sb("s_xt", [128, 2 * 1024])
        xn_tm = sb("s_xn_tm", [128, 2 * 1024], BF16)
        xn_fm = sb("s_xn_fm", [128, 8 * TC], BF16)
        w1 = sb("s_w1", [128, 8 * 2048], BF16)
        w3 = sb("s_w3", [128, 8 * 2048], BF16)
        w2 = sb("s_w2", [128, 16 * 1024], BF16)
        rw = sb("s_rw", [128, 8 * 8], BF16)
        ebr = sb("s_ebr", [1, 8], BF16)
        cwt = sb("s_cwt", [8, TC], BF16)
        cwb_sb = sb("s_cwb", [128, 512], BF16)
        eo = sb("s_eo", [128, 16 * 512], BF16)
        acc = sb("s_acc", [128, 8 * TC // 512 * 512])
        idm = sb("s_idm", [128, 128], BF16)
        oh8 = sb("s_oh8", [8, 8 * 128], BF16)
        cst = sb("s_cst", [128, 8])
        onesrow = sb("s_onesrow", [1, 512], BF16)
        lg = sb("s_lg", [128, 8])
        lgm = sb("s_lgm", [128, 8])
        m1c = sb("s_m1c", [128, 1]); m2c = sb("s_m2c", [128, 1])
        zrc = sb("s_zrc", [128, 1])
        cw_tm = sb("s_cw_tm", [128, 8], BF16)
        u1 = sb("s_u1", [128, 512])
        u2 = sb("s_u2", [128, 512])
        res = sb("s_res", [128, 1024])
        pbt = sb("s_pbt", [128, 1024])
        st6 = sb("s_st6", [128, 12]); st2 = sb("s_st2", [128, 2])
        rstd1 = sb("s_rstd1", [128, 1]); b2c_ = sb("s_b2c", [128, 1])
        stats = dict(s6=st6, s2=st2, rstd=rstd1, b2=b2c_, eps=cst[:, 0:1])

        PA = ps("PA", [128, 512]); PB = ps("PB", [128, 512])
        PC = ps("PC", [128, 512]); PD = ps("PD", [128, 512])
        PEb = ps("PEb", [128, 512])
        PG1 = ps("PG1", [128, 1024], BF16)
        PG2 = ps("PG2", [128, 1024], BF16)

        plan = Plan(nc)
        g, s, v, a = "gpsimd", "sync", "vector", "scalar"

        plan.dma(g, rw[:].rearrange("p (c o) -> p c o", c=8), d_rw.rearrange("(c p) o -> p c o", p=128))
        plan.dma(g, ebr[:], d_eb)
        plan.dma(g, idm[:], d_id)
        plan.dma(g, oh8[:], d_oh)
        plan.dma(s, cst[:], d_cst)
        plan.dma(s, pbt[:], d_pb)
        ev(plan, v, lambda e: e.memset(onesrow[:], 1.0), writes=[onesrow])
        ev(plan, v, lambda e: e.memset(cwt[:], 1.0), writes=[cwt])

        xt3 = xt[:].rearrange("p (b f) -> p b f", b=2)
        xn_tm3 = xn_tm[:].rearrange("p (b f) -> p b f", b=2)
        xn_fm3 = xn_fm[:].rearrange("p (c t) -> p c t", c=8)
        w13 = w1[:].rearrange("p (c o) -> p c o", c=8)
        w33 = w3[:].rearrange("p (c o) -> p c o", c=8)
        w23 = w2[:].rearrange("p (c o) -> p c o", c=16)
        rw3 = rw[:].rearrange("p (c o) -> p c o", c=8)
        eo3 = eo[:].rearrange("p (c t) -> p c t", c=16)
        acc3 = acc[:].rearrange("p (c t) -> p c t", c=8)
        x_dram3 = d_x.rearrange("(b p) f -> p b f", p=128)
        a0_dram3 = d_a0.rearrange("(b p) f -> p b f", p=128)
        a1_dram3 = d_a1.rearrange("(b p) f -> p b f", p=128)
        out_dram3 = d_out.rearrange("(b p) f -> p b f", p=128)
        d_w1_4 = d_w1.rearrange("(e c p) o -> e p c o", e=E, p=128)
        d_w3_4 = d_w3.rearrange("(e c p) o -> e p c o", e=E, p=128)
        d_w2_4 = d_w2.rearrange("(e c p) o -> e p c o", e=E, p=128)

        for ch in range(NB // 2):
            plan.dma(s, xt3[:, :, :], x_dram3[:, 2 * ch:2 * ch + 2, :], tag=f"ldx{ch}")
            for j in range(2):
                blk = 2 * ch + j
                ln_plain_tile(plan, None, xn_tm3[:, j, :], xt3[:, j, :], stats, tag=f"ln{blk}")
                for cc in range(8):
                    PG = PG1 if cc % 2 == 0 else PG2
                    transpose_step(plan, PG[:, 0:128], xn_tm3[:, j, cc * 128:(cc + 1) * 128], idm[:],
                                   tag=f"xtr{blk}.{cc}")
                    ev(plan, a, lambda e, cc=cc, blk=blk, PG=PG: e.copy(
                        xn_fm3[:, cc, blk * 128:(blk + 1) * 128], PG[:, 0:128]),
                       reads=[PG], writes=[xn_fm], tag=f"xtre{blk}.{cc}")
        for blk in range(NB):
            mm(plan, PA[:, 0:8], [(xn_fm3[:, ic, blk * 128:(blk + 1) * 128], rw3[:, ic, :]) for ic in range(8)]
               + [(onesrow[0:1, 0:128], ebr[:])], tag=f"rt{blk}")
            ev(plan, a, lambda e: e.copy(lg[:], PA[:, 0:8]), reads=[PA], writes=[lg])
            ev(plan, v, lambda e: e.reduce_max(m1c[:], lg[:], axis=AX.X), reads=[lg], writes=[m1c])
            ev(plan, v, lambda e: e.tensor_scalar(lgm[:], lg[:], m1c[:], None, ALU.is_lt),
               reads=[lg, m1c], writes=[lgm])
            ev(plan, v, lambda e: e.tensor_scalar(lgm[:], lgm[:], 1e30, -1e30, ALU.mult, ALU.add),
               reads=[lgm], writes=[lgm])
            ev(plan, v, lambda e: e.tensor_add(lgm[:], lgm[:], lg[:]), reads=[lgm, lg], writes=[lgm])
            ev(plan, v, lambda e: e.reduce_max(m2c[:], lgm[:], axis=AX.X), reads=[lgm], writes=[m2c])
            ev(plan, v, lambda e: e.tensor_scalar(lgm[:], lg[:], m2c[:], None, ALU.is_ge),
               reads=[lg, m2c], writes=[lgm])
            ev(plan, v, lambda e: e.tensor_scalar_mul(m1c[:], m1c[:], -1.0), reads=[m1c], writes=[m1c])
            ev(plan, a, lambda e: e.activation(lg[:], lg[:], AF.Exp, bias=m1c[:]),
               reads=[lg, m1c], writes=[lg])
            ev(plan, v, lambda e: e.tensor_mul(lg[:], lg[:], lgm[:]), reads=[lg, lgm], writes=[lg])
            ev(plan, v, lambda e: e.reduce_sum(zrc[:], lg[:], axis=AX.X), reads=[lg], writes=[zrc])
            ev(plan, v, lambda e: e.reciprocal(zrc[:], zrc[:]), reads=[zrc], writes=[zrc])
            ev(plan, v, lambda e: e.tensor_scalar_mul(cw_tm[:], lg[:], zrc[:]), reads=[lg, zrc], writes=[cw_tm])
            transpose_step(plan, PG1[0:8, 0:128], cw_tm[:, 0:8], idm[:], tag=f"cwtr{blk}")
            ev(plan, a, lambda e, blk=blk: e.copy(cwt[0:8, blk * 128:(blk + 1) * 128], PG1[0:8, 0:128]),
               reads=[PG1], writes=[cwt], tag=f"cwte{blk}")

        for e_ in range(E):
            plan.dma(g, w13[:, :, :], d_w1_4[e_], tag=f"lw1{e_}")
            plan.dma(g, w33[:, :, :], d_w3_4[e_], tag=f"lw3{e_}")
            plan.dma(g, w23[:, :, :], d_w2_4[e_], tag=f"lw2{e_}")
            for tb in range(NT):
                tsl = slice(tb * 512, (tb + 1) * 512)
                if e_ < 8:
                    mm(plan, PEb[:], [(oh8[:, e_ * 128:(e_ + 1) * 128], cwt[0:8, tsl])], tag=f"cb{e_}.{tb}")
                    ev(plan, a, lambda e: e.copy(cwb_sb[:], PEb[:]), reads=[PEb], writes=[cwb_sb])
                for hc in range(NHC):
                    mm(plan, PA[:], [(w13[:, ic, hc * 128:(hc + 1) * 128], xn_fm3[:, ic, tsl]) for ic in range(8)],
                       tag=f"g{e_}.{tb}.{hc}")
                    mm(plan, PB[:], [(w33[:, ic, hc * 128:(hc + 1) * 128], xn_fm3[:, ic, tsl]) for ic in range(8)],
                       tag=f"u{e_}.{tb}.{hc}")
                    ev(plan, a, lambda e: e.activation(u1[:], PA[:], AF.Silu), reads=[PA], writes=[u1])
                    ev(plan, v, lambda e, hc=hc: e.tensor_tensor(eo3[:, hc, :], u1[:], PB[:], ALU.mult),
                       reads=[u1, PB], writes=[eo], tag=f"eo{e_}.{tb}.{hc}")
                for oc in range(8):
                    PY = PC if oc % 2 == 0 else PD
                    mm(plan, PY[:], [(w23[:, hc, oc * 128:(oc + 1) * 128], eo3[:, hc, :]) for hc in range(NHC)],
                       tag=f"y{e_}.{tb}.{oc}")
                    asl = acc3[:, oc, tb * 512:(tb + 1) * 512]
                    if e_ == 0:
                        ev(plan, v, lambda e, PY=PY, asl=asl: e.tensor_tensor(
                            asl, PY[:], cwb_sb[:], ALU.mult),
                           reads=[PY, cwb_sb], writes=[acc], tag=f"ya{e_}.{tb}.{oc}")
                    elif e_ < 8:
                        ev(plan, v, lambda e, PY=PY: e.tensor_tensor(u2[:], PY[:], cwb_sb[:], ALU.mult),
                           reads=[PY, cwb_sb], writes=[u2], tag=f"ys{e_}.{tb}.{oc}")
                        ev(plan, v, lambda e, asl=asl: e.tensor_add(asl, asl, u2[:]),
                           reads=[acc, u2], writes=[acc], tag=f"ya{e_}.{tb}.{oc}")
                    else:
                        ev(plan, v, lambda e, PY=PY, asl=asl: e.tensor_add(asl, asl, PY[:]),
                           reads=[acc, PY], writes=[acc], tag=f"ya{e_}.{tb}.{oc}")
        for blk in range(NB):
            plan.dma(s, xt3[:, 0, :], x_dram3[:, blk, :], tag=f"rx{blk}")
            plan.dma(s, res[:, 0:1024], a0_dram3[:, blk, :], tag=f"ra0{blk}")
            plan.dma(s, xt3[:, 1, :], a1_dram3[:, blk, :], tag=f"ra1{blk}")
            ev(plan, v, lambda e, blk=blk: e.tensor_copy(
                eo[:, 4096:5120].rearrange("p (c t) -> p c t", c=8),
                acc3[:, :, blk * 128:(blk + 1) * 128]),
               reads=[acc], writes=[eo], tag=f"ac{blk}")
            eob = eo[:, 4096:5120].rearrange("p (c t) -> p c t", c=8)
            for cc in range(8):
                PG = PG1 if cc % 2 == 0 else PG2
                transpose_step(plan, PG[:, 128:256], eob[:, cc, :], idm[:], tag=f"atr{blk}.{cc}")
                ev(plan, a, lambda e, cc=cc, PG=PG: e.copy(eo[:, cc * 128:(cc + 1) * 128], PG[:, 128:256]),
                   reads=[PG], writes=[eo], tag=f"atre{blk}.{cc}")
            ev(plan, v, lambda e: e.tensor_add(res[:, 0:1024], res[:, 0:1024], xt3[:, 0, :]),
               reads=[res, xt], writes=[res])
            ev(plan, v, lambda e: e.tensor_add(res[:, 0:1024], res[:, 0:1024], xt3[:, 1, :]),
               reads=[res, xt], writes=[res])
            ev(plan, v, lambda e: e.tensor_add(res[:, 0:1024], res[:, 0:1024], pbt[:]),
               reads=[res, pbt], writes=[res])
            ev(plan, v, lambda e: e.tensor_add(res[:, 0:1024], res[:, 0:1024], eo[:, 0:1024]),
               reads=[res, eo], writes=[res])
            plan.dma(g, out_dram3[:, blk, :], res[:, 0:1024], tag=f"so{blk}")

        if dbg:
            plan.dma(g, d_deo, eo[:], tag="deo")
            plan.dma(g, d_dcw, cwt[0:8, :], tag="dcw")
            plan.dma(g, d_dacc, acc[:], tag="dacc")
            plan.dma(g, d_dxn, xn_fm[:], tag="dxn")
            plan.barrier_outputs([d_out, d_dcw, d_dacc, d_dxn, d_deo])
        else:
            plan.barrier_outputs([d_out])
        plan.build(es)
    return nc


def prep_f0(inputs, x_full, a0_full, a1_full, half):
    cc = _common_consts()
    E = 9
    w1 = np.concatenate([inputs["L0_ffn_w1"], inputs["L0_ffn_sw1"]], 0)
    w3 = np.concatenate([inputs["L0_ffn_w3"], inputs["L0_ffn_sw3"]], 0)
    w2 = np.concatenate([inputs["L0_ffn_w2"], inputs["L0_ffn_sw2"]], 0)
    w1 = np.ascontiguousarray(w1.reshape(E * DM, 2048))
    w3 = np.ascontiguousarray(w3.reshape(E * DM, 2048))
    w2 = np.ascontiguousarray(w2.reshape(E * 2048, DM))
    rw = np.ascontiguousarray(inputs["L0_ffn_router_w"].T)
    pb = np.broadcast_to(inputs["L0_attn_proj_b"], (128, DM)).copy()
    a0f = a0_full.reshape(T_ALL, DM)
    a1f = a1_full.reshape(T_ALL, DM)
    in_maps = []
    TCH = T_CORE // 2
    for c in range(N_CORES):
        tsl = slice(c * T_CORE + half * TCH, c * T_CORE + (half + 1) * TCH)
        in_maps.append({
            "x": np.ascontiguousarray(x_full[tsl]),
            "a0": np.ascontiguousarray(a0f[tsl]),
            "a1": np.ascontiguousarray(a1f[tsl]),
            "pb": pb, "rw": rw,
            "eb": inputs["L0_ffn_experts_bias"][None, :],
            "w1": w1, "w3": w3, "w2": w2,
            "idm": cc["idm"], "oh": cc["oh"], "cst": cc["cst"],
        })
    return in_maps


def build_f1():
    nc = bass.Bass("TRN2", target_bir_lowering=False, debug=False)
    TC = T_CORE // 2
    NB = TC // 128
    NT = TC // 512
    E = 8
    HID = 4096
    NHC = HID // 128

    d_x = nc.dram_tensor("x", [TC, DM], F32, kind="ExternalInput").ap()
    d_aA0 = nc.dram_tensor("aA0", [TC, DM], F32, kind="ExternalInput").ap()
    d_aA1 = nc.dram_tensor("aA1", [TC, DM], F32, kind="ExternalInput").ap()
    d_aB0 = nc.dram_tensor("aB0", [TC, DM], F32, kind="ExternalInput").ap()
    d_aB1 = nc.dram_tensor("aB1", [TC, DM], F32, kind="ExternalInput").ap()
    d_bo = nc.dram_tensor("bo", [128, DM], F32, kind="ExternalInput").ap()
    d_g1 = nc.dram_tensor("g1", [128, DM], F32, kind="ExternalInput").ap()
    d_be1 = nc.dram_tensor("be1", [128, DM], F32, kind="ExternalInput").ap()
    d_g2 = nc.dram_tensor("g2", [128, DM], F32, kind="ExternalInput").ap()
    d_be2 = nc.dram_tensor("be2", [128, DM], F32, kind="ExternalInput").ap()
    d_wg = nc.dram_tensor("wg", [DM, 8], F32, kind="ExternalInput").ap()
    d_bg = nc.dram_tensor("bg", [1, 8], F32, kind="ExternalInput").ap()
    d_w1 = nc.dram_tensor("w1", [E * DM, HID], F32, kind="ExternalInput").ap()
    d_b1 = nc.dram_tensor("b1", [E, HID], F32, kind="ExternalInput").ap()
    d_w2 = nc.dram_tensor("w2", [E * HID, DM], F32, kind="ExternalInput").ap()
    d_b2 = nc.dram_tensor("b2", [E, DM], F32, kind="ExternalInput").ap()
    d_id = nc.dram_tensor("idm", [128, 128], F32, kind="ExternalInput").ap()
    d_oh = nc.dram_tensor("oh", [8, 8 * 128], F32, kind="ExternalInput").ap()
    d_cst = nc.dram_tensor("cst", [128, 8], F32, kind="ExternalInput").ap()
    d_out = nc.dram_tensor("out", [TC, DM], F32, kind="ExternalOutput").ap()

    es = ExitStack()
    sb = lambda n, s, dt=F32: es.enter_context(nc.sbuf_tensor(n, s, dt))
    ps = lambda n, s, dt=F32: es.enter_context(nc.psum_tensor(n, s, dt))
    with es:
        xt = sb("s_xt", [128, 1024])
        x1_tm = sb("s_x1_tm", [128, 8 * 1024], BF16)
        xn_fm = sb("s_xn_fm", [128, 8 * TC], BF16)
        w1 = sb("s_w1", [128, 8 * HID], BF16)
        w2c = sb("s_w2c", [128, 2 * 32 * 128], BF16)
        b1r = sb("s_b1r", [1, HID], BF16)
        b2r = sb("s_b2r", [1, DM], BF16)
        wg = sb("s_wg", [128, 8 * 8], BF16)
        bgr = sb("s_bgr", [1, 8], BF16)
        cwt = sb("s_cwt", [8, TC], BF16)
        cwb_sb = sb("s_cwb", [128, 512], BF16)
        eo = sb("s_eo", [128, 32 * 512], BF16)
        acc = sb("s_acc", [128, 8 * TC], BF16)
        bo_t = sb("s_bo", [128, 1024], BF16)
        g1t = sb("s_g1t", [128, 1024], BF16); be1t = sb("s_be1t", [128, 1024], BF16)
        g2t = sb("s_g2t", [128, 1024], BF16); be2t = sb("s_be2t", [128, 1024], BF16)
        idm = sb("s_idm", [128, 128], BF16)
        oh8 = sb("s_oh8", [8, 8 * 128], BF16)
        cst = sb("s_cst", [128, 8])
        onesrow = sb("s_onesrow", [1, 512], BF16)
        lg = sb("s_lg", [128, 8]); lgm = sb("s_lgm", [128, 8])
        m1c = sb("s_m1c", [128, 1]); m2c = sb("s_m2c", [128, 1])
        zrc = sb("s_zrc", [128, 1])
        cw_tm = sb("s_cw_tm", [128, 8], BF16)
        u1 = sb("s_u1", [128, 1024])
        u2 = sb("s_u2", [128, 512])
        res = sb("s_res", [128, 1024])
        st6 = sb("s_st6", [128, 12]); st2 = sb("s_st2", [128, 2])
        rstd1 = sb("s_rstd1", [128, 1]); b2c_ = sb("s_b2c", [128, 1])
        stats = dict(s6=st6, s2=st2, rstd=rstd1, b2=b2c_, eps=cst[:, 0:1])

        PA = ps("PA", [128, 512]); PB = ps("PB", [128, 512])
        PC = ps("PC", [128, 512]); PD = ps("PD", [128, 512])
        PEb = ps("PEb", [128, 512])
        PG1 = ps("PG1", [128, 1024], BF16)
        PG2 = ps("PG2", [128, 1024], BF16)

        plan = Plan(nc)
        g, s, v, a = "gpsimd", "sync", "vector", "scalar"

        plan.dma(g, wg[:].rearrange("p (c o) -> p c o", c=8), d_wg.rearrange("(c p) o -> p c o", p=128))
        plan.dma(g, bgr[:], d_bg)
        plan.dma(g, idm[:], d_id)
        plan.dma(g, oh8[:], d_oh)
        plan.dma(s, cst[:], d_cst)
        plan.dma(g, bo_t[:], d_bo)
        plan.dma(g, g1t[:], d_g1)
        plan.dma(g, be1t[:], d_be1)
        plan.dma(g, g2t[:], d_g2)
        plan.dma(g, be2t[:], d_be2)
        ev(plan, v, lambda e: e.memset(onesrow[:], 1.0), writes=[onesrow])

        x1_tm3 = x1_tm[:].rearrange("p (b f) -> p b f", b=8)
        xn_fm3 = xn_fm[:].rearrange("p (c t) -> p c t", c=8)
        w13 = w1[:].rearrange("p (c o) -> p c o", c=8)
        w2c3 = w2c[:].rearrange("p (b c o) -> p b c o", b=2, c=32)
        wg3 = wg[:].rearrange("p (c o) -> p c o", c=8)
        eo3 = eo[:].rearrange("p (c t) -> p c t", c=32)
        acc3 = acc[:].rearrange("p (c t) -> p c t", c=8)
        x_dram3 = d_x.rearrange("(b p) f -> p b f", p=128)
        aA0_3 = d_aA0.rearrange("(b p) f -> p b f", p=128)
        aA1_3 = d_aA1.rearrange("(b p) f -> p b f", p=128)
        aB0_3 = d_aB0.rearrange("(b p) f -> p b f", p=128)
        aB1_3 = d_aB1.rearrange("(b p) f -> p b f", p=128)
        out_dram3 = d_out.rearrange("(b p) f -> p b f", p=128)
        d_w1_4 = d_w1.rearrange("(e c p) o -> e p c o", e=E, p=128)
        d_w2_4 = d_w2.rearrange("(e c p) o -> e p c o", e=E, p=128)

        for blk in range(NB):
            plan.dma(s, xt[:], x_dram3[:, blk, :], tag=f"ldx{blk}")
            plan.dma(s, res[:], aB0_3[:, blk, :], tag=f"lb0{blk}")
            ln_plain_tile(plan, None, u1[:], xt[:], stats, tag=f"ln{blk}")
            ev(plan, v, lambda e: e.tensor_add(res[:], res[:], u1[:]), reads=[res, u1], writes=[res])
            plan.dma(s, u1[:], aB1_3[:, blk, :], tag=f"lb1{blk}")
            ev(plan, v, lambda e: e.tensor_add(res[:], res[:], u1[:]), reads=[res, u1], writes=[res])
            ev(plan, v, lambda e: e.tensor_add(res[:], res[:], bo_t[:]), reads=[res, bo_t], writes=[res])
            ln_plain_tile(plan, None, u1[:], res[:], stats, tag=f"lna{blk}")
            ev(plan, v, lambda e: e.tensor_mul(u1[:], u1[:], g1t[:]), reads=[u1, g1t], writes=[u1])
            ev(plan, v, lambda e, blk=blk: e.tensor_add(x1_tm3[:, blk, :], u1[:], be1t[:]),
               reads=[u1, be1t], writes=[x1_tm], tag=f"x1{blk}")
            for cc in range(8):
                PG = PG1 if cc % 2 == 0 else PG2
                transpose_step(plan, PG[:, 0:128], x1_tm3[:, blk, cc * 128:(cc + 1) * 128], idm[:],
                               tag=f"xtr{blk}.{cc}")
                ev(plan, a, lambda e, cc=cc, blk=blk, PG=PG: e.copy(
                    xn_fm3[:, cc, blk * 128:(blk + 1) * 128], PG[:, 0:128]),
                   reads=[PG], writes=[xn_fm], tag=f"xtre{blk}.{cc}")
        for blk in range(NB):
            mm(plan, PA[:, 0:8], [(xn_fm3[:, ic, blk * 128:(blk + 1) * 128], wg3[:, ic, :]) for ic in range(8)]
               + [(onesrow[0:1, 0:128], bgr[:])], tag=f"rt{blk}")
            ev(plan, a, lambda e: e.copy(lg[:], PA[:, 0:8]), reads=[PA], writes=[lg])
            ev(plan, v, lambda e: e.reduce_max(m1c[:], lg[:], axis=AX.X), reads=[lg], writes=[m1c])
            ev(plan, v, lambda e: e.tensor_scalar(lgm[:], lg[:], m1c[:], None, ALU.is_lt),
               reads=[lg, m1c], writes=[lgm])
            ev(plan, v, lambda e: e.tensor_scalar(lgm[:], lgm[:], 1e30, -1e30, ALU.mult, ALU.add),
               reads=[lgm], writes=[lgm])
            ev(plan, v, lambda e: e.tensor_add(lgm[:], lgm[:], lg[:]), reads=[lgm, lg], writes=[lgm])
            ev(plan, v, lambda e: e.reduce_max(m2c[:], lgm[:], axis=AX.X), reads=[lgm], writes=[m2c])
            ev(plan, v, lambda e: e.tensor_scalar(lgm[:], lg[:], m2c[:], None, ALU.is_ge),
               reads=[lg, m2c], writes=[lgm])
            ev(plan, v, lambda e: e.tensor_scalar_mul(m1c[:], m1c[:], -1.0), reads=[m1c], writes=[m1c])
            ev(plan, a, lambda e: e.activation(lg[:], lg[:], AF.Exp, bias=m1c[:]),
               reads=[lg, m1c], writes=[lg])
            ev(plan, v, lambda e: e.tensor_mul(lg[:], lg[:], lgm[:]), reads=[lg, lgm], writes=[lg])
            ev(plan, v, lambda e: e.reduce_sum(zrc[:], lg[:], axis=AX.X), reads=[lg], writes=[zrc])
            ev(plan, v, lambda e: e.reciprocal(zrc[:], zrc[:]), reads=[zrc], writes=[zrc])
            ev(plan, v, lambda e: e.tensor_scalar_mul(cw_tm[:], lg[:], zrc[:]), reads=[lg, zrc], writes=[cw_tm])
            transpose_step(plan, PG1[0:8, 0:128], cw_tm[:, 0:8], idm[:], tag=f"cwtr{blk}")
            ev(plan, a, lambda e, blk=blk: e.copy(cwt[0:8, blk * 128:(blk + 1) * 128], PG1[0:8, 0:128]),
               reads=[PG1], writes=[cwt], tag=f"cwte{blk}")

        for e_ in range(E):
            plan.dma(g, w13[:, :, :], d_w1_4[e_], tag=f"lw1{e_}")
            plan.dma(g, b1r[:], d_b1[e_:e_ + 1, :], tag=f"lb1r{e_}")
            plan.dma(g, b2r[:], d_b2[e_:e_ + 1, :], tag=f"lb2r{e_}")
            for tb in range(NT):
                tsl = slice(tb * 512, (tb + 1) * 512)
                mm(plan, PEb[:], [(oh8[:, e_ * 128:(e_ + 1) * 128], cwt[0:8, tsl])], tag=f"cb{e_}.{tb}")
                ev(plan, a, lambda e: e.copy(cwb_sb[:], PEb[:]), reads=[PEb], writes=[cwb_sb])
                for hc in range(NHC):
                    mm(plan, PA[:], [(w13[:, ic, hc * 128:(hc + 1) * 128], xn_fm3[:, ic, tsl]) for ic in range(8)]
                       + [(b1r[:, hc * 128:(hc + 1) * 128], onesrow[:])], tag=f"g{e_}.{tb}.{hc}")
                    ev(plan, a, lambda e, hc=hc: e.activation(eo3[:, hc, :], PA[:], AF.Gelu),
                       reads=[PA], writes=[eo], tag=f"eo{e_}.{tb}.{hc}")
                for oc in range(8):
                    wb = oc % 2
                    plan.dma(g, w2c3[:, wb, :, :], d_w2_4[e_][:, :, oc * 128:(oc + 1) * 128],
                             tag=f"lw2{e_}.{tb}.{oc}")
                    PY = PC if oc % 2 == 0 else PD
                    mm(plan, PY[:], [(w2c3[:, wb, hc, :], eo3[:, hc, :]) for hc in range(NHC)]
                       + [(b2r[:, oc * 128:(oc + 1) * 128], onesrow[:])], tag=f"y{e_}.{tb}.{oc}")
                    asl = acc3[:, oc, tsl]
                    if e_ == 0:
                        ev(plan, v, lambda e, PY=PY, asl=asl: e.tensor_tensor(
                            asl, PY[:], cwb_sb[:], ALU.mult),
                           reads=[PY, cwb_sb], writes=[acc], tag=f"ya{e_}.{tb}.{oc}")
                    else:
                        ev(plan, v, lambda e, PY=PY: e.tensor_tensor(u2[:], PY[:], cwb_sb[:], ALU.mult),
                           reads=[PY, cwb_sb], writes=[u2], tag=f"ys{e_}.{tb}.{oc}")
                        ev(plan, v, lambda e, asl=asl: e.tensor_add(asl, asl, u2[:]),
                           reads=[acc, u2], writes=[acc], tag=f"ya{e_}.{tb}.{oc}")

        for blk in range(NB):
            ev(plan, v, lambda e, blk=blk: e.tensor_copy(
                eo[:, 4096:5120].rearrange("p (c t) -> p c t", c=8),
                acc3[:, :, blk * 128:(blk + 1) * 128]),
               reads=[acc], writes=[eo], tag=f"ac{blk}")
            eob = eo[:, 4096:5120].rearrange("p (c t) -> p c t", c=8)
            for cc in range(8):
                PG = PG1 if cc % 2 == 0 else PG2
                transpose_step(plan, PG[:, 128:256], eob[:, cc, :], idm[:], tag=f"atr{blk}.{cc}")
                ev(plan, a, lambda e, cc=cc, PG=PG: e.copy(eo[:, cc * 128:(cc + 1) * 128], PG[:, 128:256]),
                   reads=[PG], writes=[eo], tag=f"atre{blk}.{cc}")
            ev(plan, v, lambda e, blk=blk: e.tensor_add(u1[:], x1_tm3[:, blk, :], eo[:, 0:1024]),
               reads=[x1_tm, eo], writes=[u1])
            ln_plain_tile(plan, None, res[:], u1[:], stats, tag=f"lnf{blk}")
            ev(plan, v, lambda e: e.tensor_mul(res[:], res[:], g2t[:]), reads=[res, g2t], writes=[res])
            ev(plan, v, lambda e: e.tensor_add(res[:], res[:], be2t[:]), reads=[res, be2t], writes=[res])
            plan.dma(s, xt[:], x_dram3[:, blk, :], tag=f"rx{blk}")
            ev(plan, v, lambda e: e.tensor_add(res[:], res[:], xt[:]), reads=[res, xt], writes=[res])
            plan.dma(s, xt[:], aA0_3[:, blk, :], tag=f"rA0{blk}")
            ev(plan, v, lambda e: e.tensor_add(res[:], res[:], xt[:]), reads=[res, xt], writes=[res])
            plan.dma(s, xt[:], aA1_3[:, blk, :], tag=f"rA1{blk}")
            ev(plan, v, lambda e: e.tensor_add(res[:], res[:], xt[:]), reads=[res, xt], writes=[res])
            plan.dma(g, out_dram3[:, blk, :], res[:], tag=f"so{blk}")

        plan.barrier_outputs([d_out])
        plan.build(es)
    return nc


def prep_f1(inputs, x_full, aA0, aA1, aB0, aB1, half):
    cc = _common_consts()
    E = 8
    w1 = np.ascontiguousarray(inputs["L1_ffn_W1"].reshape(E * DM, 4096))
    w2 = np.ascontiguousarray(inputs["L1_ffn_W2"].reshape(E * 4096, DM))
    tile = lambda vec: np.broadcast_to(vec, (128, DM)).copy()
    TCH = T_CORE // 2
    in_maps = []
    for c in range(N_CORES):
        tsl = slice(c * T_CORE + half * TCH, c * T_CORE + (half + 1) * TCH)
        in_maps.append({
            "x": np.ascontiguousarray(x_full[tsl]),
            "aA0": np.ascontiguousarray(aA0.reshape(T_ALL, DM)[tsl]),
            "aA1": np.ascontiguousarray(aA1.reshape(T_ALL, DM)[tsl]),
            "aB0": np.ascontiguousarray(aB0.reshape(T_ALL, DM)[tsl]),
            "aB1": np.ascontiguousarray(aB1.reshape(T_ALL, DM)[tsl]),
            "bo": tile(inputs["L1_ffn_bo"]), "g1": tile(inputs["L1_ffn_g1"]),
            "be1": tile(inputs["L1_ffn_be1"]), "g2": tile(inputs["L1_ffn_g2"]),
            "be2": tile(inputs["L1_ffn_be2"]),
            "wg": np.ascontiguousarray(inputs["L1_ffn_Wg"]),
            "bg": inputs["L1_ffn_bg"][None, :],
            "w1": w1, "b1": inputs["L1_ffn_b1"], "w2": w2, "b2": inputs["L1_ffn_b2"],
            "idm": cc["idm"], "oh": cc["oh"], "cst": cc["cst"],
        })
    return in_maps


def build_f2a():
    nc = bass.Bass("TRN2", target_bir_lowering=False, debug=False)
    TC = T_CORE
    NB = TC // 128
    NT = TC // 512
    QD = 2048
    NQC = QD // 128

    d_x = nc.dram_tensor("x", [TC, DM], F32, kind="ExternalInput").ap()
    d_wq = nc.dram_tensor("wq", [DM, QD], F32, kind="ExternalInput").ap()
    d_bq = nc.dram_tensor("bq", [1, QD], F32, kind="ExternalInput").ap()
    d_id = nc.dram_tensor("idm", [128, 128], F32, kind="ExternalInput").ap()
    d_cst = nc.dram_tensor("cst", [128, 8], F32, kind="ExternalInput").ap()
    d_q = nc.dram_tensor("qout", [QD, TC], BF16, kind="ExternalOutput").ap()
    d_st = nc.dram_tensor("stout", [128, 4], F32, kind="ExternalOutput").ap()

    es = ExitStack()
    sb = lambda n, s, dt=F32: es.enter_context(nc.sbuf_tensor(n, s, dt))
    ps = lambda n, s, dt=F32: es.enter_context(nc.psum_tensor(n, s, dt))
    with es:
        xt = sb("s_xt", [128, 2 * 1024])
        xn_tm = sb("s_xn_tm", [128, 2 * 1024], BF16)
        xn_fm = sb("s_xn_fm", [128, 8 * 512], BF16)
        wq = sb("s_wq", [128, 8 * QD], BF16)
        bqr = sb("s_bqr", [1, QD], BF16)
        qc_sb = sb("s_qc", [128, 2 * 512], BF16)
        sq = sb("s_sq", [128, 512], BF16)
        stat = sb("s_stat", [128, 4])
        t1 = sb("s_t1", [128, 1])
        onesrow = sb("s_onesrow", [1, 512], BF16)
        idm = sb("s_idm", [128, 128], BF16)
        cst = sb("s_cst", [128, 8])
        st6 = sb("s_st6", [128, 12]); st2 = sb("s_st2", [128, 2])
        rstd1 = sb("s_rstd1", [128, 1]); b2c_ = sb("s_b2c", [128, 1])
        stats = dict(s6=st6, s2=st2, rstd=rstd1, b2=b2c_, eps=cst[:, 0:1])
        PA = ps("PA", [128, 512]); PB = ps("PB", [128, 512])
        PG1 = ps("PG1", [128, 1024], BF16)
        PG2 = ps("PG2", [128, 1024], BF16)

        plan = Plan(nc)
        g, s, v, a = "gpsimd", "sync", "vector", "scalar"
        plan.dma(g, wq[:].rearrange("p (c o) -> p c o", c=8), d_wq.rearrange("(c p) o -> p c o", p=128))
        plan.dma(g, bqr[:], d_bq)
        plan.dma(g, idm[:], d_id)
        plan.dma(s, cst[:], d_cst)
        ev(plan, v, lambda e: e.memset(onesrow[:], 1.0), writes=[onesrow])
        ev(plan, v, lambda e: e.memset(stat[:], 0.0), writes=[stat])

        xt3 = xt[:].rearrange("p (b f) -> p b f", b=2)
        xn_tm3 = xn_tm[:].rearrange("p (b f) -> p b f", b=2)
        xn_fm3 = xn_fm[:].rearrange("p (c t) -> p c t", c=8)
        wq3 = wq[:].rearrange("p (c o) -> p c o", c=8)
        x_dram3 = d_x.rearrange("(b p) f -> p b f", p=128)
        q_dram3 = d_q.rearrange("(c p) t -> c p t", p=128)

        for ch in range(NT):
            for half in range(2):
                plan.dma(s, xt3[:, :, :], x_dram3[:, 4 * ch + 2 * half:4 * ch + 2 * half + 2, :],
                         tag=f"ldx{ch}.{half}")
                for jj in range(2):
                    j = 2 * half + jj
                    blk = 4 * ch + j
                    ln_plain_tile(plan, None, xn_tm3[:, jj, :], xt3[:, jj, :], stats, tag=f"ln{blk}")
                    for cc in range(8):
                        PG = PG1 if cc % 2 == 0 else PG2
                        transpose_step(plan, PG[:, 0:128], xn_tm3[:, jj, cc * 128:(cc + 1) * 128], idm[:],
                                       tag=f"xtr{blk}.{cc}")
                        ev(plan, a, lambda e, cc=cc, j=j, PG=PG: e.copy(
                            xn_fm3[:, cc, j * 128:(j + 1) * 128], PG[:, 0:128]),
                           reads=[PG], writes=[xn_fm], tag=f"xtre{blk}.{cc}")
            tsl = slice(0, 512)
            for qc in range(NQC):
                P = PA if qc % 2 == 0 else PB
                qs_ = qc_sb[:, (qc % 2) * 512:(qc % 2) * 512 + 512]
                mm(plan, P[:], [(wq3[:, ic, qc * 128:(qc + 1) * 128], xn_fm3[:, ic, :]) for ic in range(8)]
                   + [(bqr[:, qc * 128:(qc + 1) * 128], onesrow[:])], tag=f"q{ch}.{qc}")
                ev(plan, a, lambda e, P=P, qs_=qs_: e.copy(qs_, P[:]), reads=[P], writes=[qc_sb])
                plan.dma(g, q_dram3[qc, :, ch * 512:(ch + 1) * 512], qs_, tag=f"sq{ch}.{qc}")
                col = qc % 2
                ev(plan, v, lambda e, qs_=qs_: e.reduce_sum(t1[:], qs_, axis=AX.X),
                   reads=[qc_sb], writes=[t1])
                ev(plan, v, lambda e, col=col: e.tensor_add(stat[:, col:col + 1], stat[:, col:col + 1], t1[:]),
                   reads=[stat, t1], writes=[stat])
                ev(plan, a, lambda e, qs_=qs_: e.activation(sq[:], qs_, AF.Square), reads=[qc_sb], writes=[sq])
                ev(plan, v, lambda e: e.reduce_sum(t1[:], sq[:], axis=AX.X), reads=[sq], writes=[t1])
                ev(plan, v, lambda e, col=col: e.tensor_add(stat[:, 2 + col:3 + col], stat[:, 2 + col:3 + col], t1[:]),
                   reads=[stat, t1], writes=[stat])
        plan.dma(g, d_st, stat[:], tag="sst")
        plan.barrier_outputs([d_q, d_st])
        plan.build(es)
    return nc


def prep_f2a(inputs, x_full):
    cc = _common_consts()
    in_maps = []
    for c in range(N_CORES):
        tsl = slice(c * T_CORE, (c + 1) * T_CORE)
        in_maps.append({
            "x": np.ascontiguousarray(x_full[tsl]),
            "wq": np.ascontiguousarray(inputs["L2_ffn_Wq"]),
            "bq": inputs["L2_ffn_bq"][None, :],
            "idm": cc["idm"], "cst": cc["cst"],
        })
    return in_maps


def build_f2b():
    nc = bass.Bass("TRN2", target_bir_lowering=False, debug=False)
    TC = T_CORE
    NB = TC // 128
    NT = TC // 512
    QD = 2048
    NQC = 16
    NTOT = float(T_ALL * 8)

    d_x = nc.dram_tensor("x", [TC, DM], F32, kind="ExternalInput").ap()
    d_ap = nc.dram_tensor("ap", [4 * TC, DM], F32, kind="ExternalInput").ap()
    d_q = nc.dram_tensor("qin", [QD, TC], BF16, kind="ExternalInput").ap()
    d_st = nc.dram_tensor("stin", [128, 32], F32, kind="ExternalInput").ap()
    d_gb = nc.dram_tensor("gb", [128, 4], F32, kind="ExternalInput").ap()
    d_k0 = nc.dram_tensor("k0", [32, 128], F32, kind="ExternalInput").ap()
    d_k1 = nc.dram_tensor("k1", [32, 128], F32, kind="ExternalInput").ap()
    d_dn = nc.dram_tensor("dn", [DM, 1024], F32, kind="ExternalInput").ap()
    d_up = nc.dram_tensor("up", [1024, 1024], F32, kind="ExternalInput").ap()
    d_r0 = nc.dram_tensor("r0", [128, 1024], F32, kind="ExternalInput").ap()
    d_r1 = nc.dram_tensor("r1", [128, 1024], F32, kind="ExternalInput").ap()
    d_o16 = nc.dram_tensor("o16", [128, 16 * 16], F32, kind="ExternalInput").ap()
    d_id = nc.dram_tensor("idm", [128, 128], F32, kind="ExternalInput").ap()
    d_cst = nc.dram_tensor("cst", [128, 8], F32, kind="ExternalInput").ap()
    d_out = nc.dram_tensor("out", [TC, DM], F32, kind="ExternalOutput").ap()

    es = ExitStack()
    sb = lambda n, s, dt=F32: es.enter_context(nc.sbuf_tensor(n, s, dt))
    ps = lambda n, s, dt=F32: es.enter_context(nc.psum_tensor(n, s, dt))
    with es:
        xt = sb("s_xt", [128, 1024])
        xn_tm = sb("s_xn_tm", [128, 1024], BF16)
        xn_fm = sb("s_xn_fm", [128, 8 * 512], BF16)
        qch = sb("s_qch", [128, 16 * 512], BF16)
        stt_ = sb("s_stt", [128, 32])
        ab4 = sb("s_ab4", [128, 4])
        gb4 = sb("s_gb4", [128, 4])
        k0n = sb("s_k0n", [32, 128])
        k1n = sb("s_k1n", [32, 128])
        k0b = sb("s_k0b", [32, 128], BF16)
        k0T = sb("s_k0T", [128, 32], BF16)
        k1T = sb("s_k1T", [128, 32], BF16)
        kn1 = sb("s_kn1", [32, 1])
        dn = sb("s_dn", [128, 8 * 1024], BF16)
        up = sb("s_up", [128, 8 * 1024], BF16)
        r0 = sb("s_r0", [128, 1024], BF16)
        r1 = sb("s_r1", [128, 1024], BF16)
        o16 = sb("s_o16", [128, 16 * 16], BF16)
        idm = sb("s_idm", [128, 128], BF16)
        cst = sb("s_cst", [128, 8])
        sq = sb("s_sq", [128, 512], BF16)
        nrm = sb("s_nrm", [16, 512])
        rn_tm = sb("s_rn", [128, 4 * 16], BF16)
        s_f = sb("s_sf", [128, 512])
        cur = sb("s_cur", [128, 512])
        m1 = sb("s_m1", [128, 16]); mi_ = sb("s_mi", [128, 16]); gei = sb("s_gei", [128, 512])
        pbf = sb("s_pbf", [128, 512], BF16)
        pT = sb("s_pT", [64, 8 * 128], BF16)
        z16 = sb("s_z16", [128, 16]); rz8 = sb("s_rz8", [128, 8])
        G_tm = sb("s_G", [128, 1024], BF16)
        A_ = sb("s_A", [128, 1024])
        AGb = sb("s_AGb", [128, 1024], BF16)
        AG_em = sb("s_AGem", [128, 8 * 128], BF16)
        rsb = sb("s_rsb", [128, 512], BF16)
        tmp = sb("s_tmp", [128, 512])
        res = sb("s_res", [128, 1024])
        ffn = sb("s_ffn", [128, 1024])
        st6 = sb("s_st6", [128, 12]); st2 = sb("s_st2", [128, 2])
        rstd1 = sb("s_rstd1", [128, 1]); b2c_ = sb("s_b2c", [128, 1])
        stats = dict(s6=st6, s2=st2, rstd=rstd1, b2=b2c_, eps=cst[:, 0:1])

        PA = ps("PA", [128, 512]); PB = ps("PB", [128, 512])
        PC = ps("PC", [128, 512]); PD = ps("PD", [128, 512])
        PEb = ps("PEb", [128, 512]); PF = ps("PF", [128, 512])
        PG1 = ps("PG1", [128, 1024], BF16)
        PG2 = ps("PG2", [128, 1024], BF16)

        plan = Plan(nc)
        g, s, v, a = "gpsimd", "sync", "vector", "scalar"
        plan.dma(g, dn[:].rearrange("p (c o) -> p c o", c=8), d_dn.rearrange("(c p) o -> p c o", p=128))
        plan.dma(g, up[:].rearrange("p (c o) -> p c o", c=8), d_up.rearrange("(c p) o -> p c o", p=128))
        plan.dma(g, r0[:], d_r0)
        plan.dma(g, r1[:], d_r1)
        plan.dma(g, o16[:], d_o16)
        plan.dma(g, idm[:], d_id)
        plan.dma(s, cst[:], d_cst)
        plan.dma(s, stt_[:], d_st)
        plan.dma(s, gb4[:], d_gb)
        plan.dma(s, k0n[:], d_k0)
        plan.dma(s, k1n[:], d_k1)

        xn_fm3 = xn_fm[:].rearrange("p (c t) -> p c t", c=8)
        qch3 = qch[:].rearrange("p (c t) -> p c t", c=16)
        dn3 = dn[:].rearrange("p (c o) -> p c o", c=8)
        up3 = up[:].rearrange("p (c o) -> p c o", c=8)
        pT3 = pT[:].rearrange("p (c t) -> p c t", c=8)
        AG3 = AG_em[:].rearrange("p (c t) -> p c t", c=8)
        rn3 = rn_tm[:].rearrange("p (b k) -> p b k", b=4)
        x_dram3 = d_x.rearrange("(b p) f -> p b f", p=128)
        ap_dram3 = d_ap.rearrange("(n b p) f -> n p b f", n=4, p=128)
        q_dram3 = d_q.rearrange("(c p) t -> p c t", p=128)
        out_dram3 = d_out.rearrange("(b p) f -> p b f", p=128)

        ev(plan, v, lambda e: e.tensor_reduce(
            ab4[:].rearrange("p (r o) -> p r o", r=4),
            stt_[:].rearrange("p (r c) -> p r c", r=4), AX.X, ALU.add),
           reads=[stt_], writes=[ab4], tag="stc")
        ev(plan, v, lambda e: e.tensor_scalar_mul(ab4[:, 0:2], ab4[:, 0:2], 1.0 / NTOT),
           reads=[ab4], writes=[ab4])
        ev(plan, v, lambda e: e.tensor_scalar_mul(ab4[:, 2:4], ab4[:, 2:4], 1.0 / NTOT),
           reads=[ab4], writes=[ab4])
        ev(plan, v, lambda e: e.tensor_mul(stt_[:, 0:2], ab4[:, 0:2], ab4[:, 0:2]),
           reads=[ab4], writes=[stt_])
        ev(plan, v, lambda e: e.tensor_sub(ab4[:, 2:4], ab4[:, 2:4], stt_[:, 0:2]),
           reads=[ab4, stt_], writes=[ab4])
        ev(plan, a, lambda e: e.activation(ab4[:, 2:4], ab4[:, 2:4], AF.Sqrt, bias=cst[:, 0:1]),
           reads=[ab4, cst], writes=[ab4])
        ev(plan, v, lambda e: e.reciprocal(ab4[:, 2:4], ab4[:, 2:4]), reads=[ab4], writes=[ab4])
        ev(plan, v, lambda e: e.tensor_mul(ab4[:, 2:4], ab4[:, 2:4], gb4[:, 0:2]),
           reads=[ab4, gb4], writes=[ab4])
        ev(plan, v, lambda e: e.tensor_mul(ab4[:, 0:2], ab4[:, 0:2], ab4[:, 2:4]),
           reads=[ab4], writes=[ab4])
        ev(plan, v, lambda e: e.tensor_sub(ab4[:, 0:2], gb4[:, 2:4], ab4[:, 0:2]),
           reads=[ab4, gb4], writes=[ab4])

        for kt, kb_, kT_ in ((k0n, k0b, k0T), (k1n, k0b, k1T)):
            ev(plan, a, lambda e, kt=kt: e.activation(kb_[:] if False else sq[0:32, 0:128], kt[:], AF.Square),
               reads=[kt], writes=[sq])
            ev(plan, v, lambda e: e.reduce_sum(kn1[:], sq[0:32, 0:128], axis=AX.X),
               reads=[sq], writes=[kn1])
            ev(plan, a, lambda e: e.activation(kn1[:], kn1[:], AF.Sqrt), reads=[kn1], writes=[kn1])
            ev(plan, v, lambda e: e.tensor_scalar_max(kn1[:], kn1[:], 1e-12), reads=[kn1], writes=[kn1])
            ev(plan, v, lambda e: e.reciprocal(kn1[:], kn1[:]), reads=[kn1], writes=[kn1])
            ev(plan, v, lambda e, kt=kt: e.tensor_scalar_mul(kb_[:], kt[:], kn1[:]),
               reads=[kt, kn1], writes=[kb_])
            transpose_step(plan, PG1[0:128, 0:32], kb_[:], idm[0:32, 0:32], tag="ktr")
            ev(plan, a, lambda e, kT_=kT_: e.copy(kT_[:], PG1[0:128, 0:32]), reads=[PG1], writes=[kT_])

        for ch in range(NT):
            plan.dma(s, qch[:].rearrange("p (c t) -> p c t", c=16), q_dram3[:, :, ch * 512:(ch + 1) * 512],
                     tag=f"ldq{ch}")
            for j in range(4):
                blk = 4 * ch + j
                plan.dma(s, xt[:], x_dram3[:, blk, :], tag=f"ldx{blk}")
                ln_plain_tile(plan, None, xn_tm[:], xt[:], stats, tag=f"ln{blk}")
                for cc in range(8):
                    PG = PG1 if cc % 2 == 0 else PG2
                    transpose_step(plan, PG[:, 0:128], xn_tm[:, cc * 128:(cc + 1) * 128], idm[:],
                                   tag=f"xtr{blk}.{cc}")
                    ev(plan, a, lambda e, cc=cc, j=j, PG=PG: e.copy(
                        xn_fm3[:, cc, j * 128:(j + 1) * 128], PG[:, 0:128]),
                       reads=[PG], writes=[xn_fm], tag=f"xtre{blk}.{cc}")
            for qc in range(NQC):
                col = qc % 2
                ev(plan, v, lambda e, qc=qc, col=col: e.tensor_scalar(
                    qch3[:, qc, :], qch3[:, qc, :], ab4[:, 2 + col:3 + col], ab4[:, col:col + 1],
                    ALU.mult, ALU.add),
                   reads=[qch, ab4], writes=[qch], tag=f"bn{ch}.{qc}")
                ev(plan, a, lambda e, qc=qc: e.activation(sq[:], qch3[:, qc, :], AF.Square),
                   reads=[qch], writes=[sq])
                mm(plan, PF[0:16, :], [(o16[:, qc * 16:(qc + 1) * 16], sq[:])],
                   start=(qc == 0), stop=(qc == 15), tag=f"nr{ch}.{qc}")
            ev(plan, a, lambda e: e.activation(nrm[:], PF[0:16, :], AF.Sqrt), reads=[PF], writes=[nrm])
            ev(plan, v, lambda e: e.tensor_scalar_max(nrm[:], nrm[:], 1e-12), reads=[nrm], writes=[nrm])
            ev(plan, v, lambda e: e.reciprocal(nrm[:], nrm[:]), reads=[nrm], writes=[nrm])
            ev(plan, v, lambda e: e.tensor_copy(sq[0:16, :], nrm[:]), reads=[nrm], writes=[sq])
            for j in range(4):
                transpose_step(plan, PG1[0:128, 0:16], sq[0:16, j * 128:(j + 1) * 128], idm[0:16, 0:16],
                               tag=f"rtr{ch}.{j}")
                ev(plan, a, lambda e, j=j: e.copy(rn3[:, j, :], PG1[0:128, 0:16]),
                   reads=[PG1], writes=[rn_tm], tag=f"rtre{ch}.{j}")
            for j in range(4):
                blk = 4 * ch + j
                for qc in range(NQC):
                    kT_ = k0T if qc % 2 == 0 else k1T
                    mm(plan, PA[:, qc * 32:(qc + 1) * 32],
                       [(qch3[:, qc, j * 128:(j + 1) * 128], kT_[:])],
                       start=(qc == 0), stop=(qc == 15), tag=f"s0{blk}.{qc}")
                bc16_32 = lambda t: t.rearrange("p (k o) -> p k o", k=16).broadcast_to([128, 16, 32])
                ev(plan, v, lambda e, j=j: e.tensor_tensor(
                    s_f[:].rearrange("p (k i) -> p k i", k=16), PA[:].rearrange("p (k i) -> p k i", k=16),
                    bc16_32(rn3[:, j, :]), ALU.mult),
                   reads=[PA, rn_tm], writes=[s_f], tag=f"ssc{blk}")
                ev(plan, v, lambda e: e.tensor_copy(cur[:], s_f[:]), reads=[s_f], writes=[cur])
                for it in range(3):
                    red = m1 if it == 0 else mi_
                    ev(plan, v, lambda e, red=red: e.tensor_reduce(
                        red[:].rearrange("p (k o) -> p k o", k=16),
                        cur[:].rearrange("p (k i) -> p k i", k=16), AX.X, ALU.max),
                       reads=[cur], writes=[red], tag=f"t4m{blk}.{it}")
                    ev(plan, v, lambda e, red=red: e.tensor_tensor(
                        gei[:].rearrange("p (k i) -> p k i", k=16),
                        cur[:].rearrange("p (k i) -> p k i", k=16), bc16_32(red[:]), ALU.is_ge),
                       reads=[cur, red], writes=[gei], tag=f"t4g{blk}.{it}")
                    ev(plan, v, lambda e: e.scalar_tensor_tensor(cur[:], gei[:], -1e30, cur[:], ALU.mult, ALU.add),
                       reads=[gei, cur], writes=[cur], tag=f"t4u{blk}.{it}")
                ev(plan, v, lambda e: e.tensor_reduce(
                    mi_[:].rearrange("p (k o) -> p k o", k=16),
                    cur[:].rearrange("p (k i) -> p k i", k=16), AX.X, ALU.max),
                   reads=[cur], writes=[mi_], tag=f"t4m4{blk}")
                ev(plan, v, lambda e: e.tensor_tensor(
                    s_f[:].rearrange("p (k i) -> p k i", k=16),
                    s_f[:].rearrange("p (k i) -> p k i", k=16), bc16_32(m1[:]), ALU.subtract),
                   reads=[s_f, m1], writes=[s_f])
                ev(plan, v, lambda e: e.tensor_tensor(
                    gei[:].rearrange("p (k i) -> p k i", k=16),
                    s_f[:].rearrange("p (k i) -> p k i", k=16),
                    bc16_32(mi_[:]), ALU.is_ge),
                   reads=[s_f, mi_], writes=[gei])
                ev(plan, v, lambda e: e.tensor_sub(mi_[:], mi_[:], m1[:]), reads=[mi_, m1], writes=[mi_])
                ev(plan, v, lambda e: e.tensor_tensor(
                    gei[:].rearrange("p (k i) -> p k i", k=16),
                    s_f[:].rearrange("p (k i) -> p k i", k=16),
                    bc16_32(mi_[:]), ALU.is_ge),
                   reads=[s_f, mi_], writes=[gei])
                ev(plan, a, lambda e: e.activation(s_f[:], s_f[:], AF.Exp), reads=[s_f], writes=[s_f])
                ev(plan, v, lambda e: e.tensor_mul(s_f[:], s_f[:], gei[:]), reads=[s_f, gei], writes=[s_f])
                ev(plan, v, lambda e: e.tensor_reduce(
                    z16[:].rearrange("p (k o) -> p k o", k=16),
                    s_f[:].rearrange("p (k i) -> p k i", k=16), AX.X, ALU.add),
                   reads=[s_f], writes=[z16], tag=f"z{blk}")
                ev(plan, v, lambda e: e.tensor_tensor(
                    rz8[:].rearrange("p (h o) -> p h o", h=8),
                    z16[:].rearrange("p (h x) -> p h x", h=8)[:, :, 0:1],
                    z16[:].rearrange("p (h x) -> p h x", h=8)[:, :, 1:2], ALU.mult),
                   reads=[z16], writes=[rz8])
                ev(plan, v, lambda e: e.reciprocal(rz8[:], rz8[:]), reads=[rz8], writes=[rz8])
                ev(plan, v, lambda e: e.tensor_tensor(
                    s_f[:].rearrange("p (h x i) -> p h x i", h=8, x=2)[:, :, 0:1, :],
                    s_f[:].rearrange("p (h x i) -> p h x i", h=8, x=2)[:, :, 0:1, :],
                    rz8[:].rearrange("p (h a o) -> p h a o", h=8, a=1).broadcast_to([128, 8, 1, 32]), ALU.mult),
                   reads=[s_f, rz8], writes=[s_f], tag=f"rzf{blk}")
                ev(plan, v, lambda e: e.tensor_copy(pbf[:], s_f[:]), reads=[s_f], writes=[pbf])
                for cc in range(8):
                    PG = PG1 if cc % 2 == 0 else PG2
                    transpose_step(plan, PG[0:64, 128:256], pbf[:, cc * 64:(cc + 1) * 64], idm[:],
                                   tag=f"ptr{blk}.{cc}")
                    ev(plan, a, lambda e, cc=cc, PG=PG: e.copy(pT3[:, cc, :], PG[0:64, 128:256]),
                       reads=[PG], writes=[pT], tag=f"ptre{blk}.{cc}")
                for gh in range(2):
                    P = PB if gh == 0 else PC
                    mm(plan, P[:], [(xn_fm3[:, ic, j * 128:(j + 1) * 128], dn3[:, ic, gh * 512:(gh + 1) * 512])
                                    for ic in range(8)], tag=f"G{blk}.{gh}")
                    ev(plan, a, lambda e, gh=gh, P=P: e.activation(G_tm[:, gh * 512:(gh + 1) * 512], P[:], AF.Gelu),
                       reads=[P], writes=[G_tm], tag=f"Ge{blk}.{gh}")
                ev(plan, v, lambda e: e.memset(A_[:], 0.0), writes=[A_], tag=f"Az{blk}")
                for h in range(8):
                    for eh in range(2):
                        esl = slice(eh * 512, (eh + 1) * 512)
                        mm(plan, PD[:], [(pT3[0:32, h, :], r0[0:32, esl])], tag=f"r0{blk}.{h}.{eh}")
                        mm(plan, PEb[:], [(pT3[32:64, h, :], r1[32:64, esl])], tag=f"r1{blk}.{h}.{eh}")
                        ev(plan, a, lambda e: e.copy(rsb[:], PD[:]), reads=[PD], writes=[rsb])
                        ev(plan, v, lambda e: e.tensor_tensor(tmp[:], rsb[:], PEb[:], ALU.mult),
                           reads=[rsb, PEb], writes=[tmp])
                        ev(plan, v, lambda e, esl=esl: e.tensor_add(A_[:, esl], A_[:, esl], tmp[:]),
                           reads=[A_, tmp], writes=[A_], tag=f"Aa{blk}.{h}.{eh}")
                ev(plan, v, lambda e: e.tensor_mul(AGb[:], A_[:], G_tm[:]), reads=[A_, G_tm], writes=[AGb])
                for cc in range(8):
                    PG = PG1 if cc % 2 == 0 else PG2
                    transpose_step(plan, PG[:, 128:256], AGb[:, cc * 128:(cc + 1) * 128], idm[:],
                                   tag=f"agt{blk}.{cc}")
                    ev(plan, a, lambda e, cc=cc, PG=PG: e.copy(AG3[:, cc, :], PG[:, 128:256]),
                       reads=[PG], writes=[AG_em], tag=f"agte{blk}.{cc}")
                for oh_ in range(2):
                    P = PB if oh_ == 0 else PC
                    mm(plan, P[:], [(AG3[:, ec, :], up3[:, ec, oh_ * 512:(oh_ + 1) * 512]) for ec in range(8)],
                       tag=f"o{blk}.{oh_}")
                    ev(plan, a, lambda e, oh_=oh_, P=P: e.copy(ffn[:, oh_ * 512:(oh_ + 1) * 512], P[:]),
                       reads=[P], writes=[ffn], tag=f"oe{blk}.{oh_}")
                ev(plan, v, lambda e: e.tensor_copy(res[:], ffn[:]), reads=[ffn], writes=[res])
                for nn in range(4):
                    plan.dma(s, xt[:], ap_dram3[nn, :, blk, :], tag=f"rap{blk}.{nn}")
                    ev(plan, v, lambda e: e.tensor_add(res[:], res[:], xt[:]), reads=[res, xt], writes=[res])
                plan.dma(s, xt[:], x_dram3[:, blk, :], tag=f"rx{blk}")
                ev(plan, v, lambda e: e.tensor_add(res[:], res[:], xt[:]), reads=[res, xt], writes=[res])
                plan.dma(g, out_dram3[:, blk, :], res[:], tag=f"so{blk}")

        plan.barrier_outputs([d_out])
        plan.build(es)
    return nc


def prep_f2b(inputs, x_full, aps, q_parts, st_parts):
    cc = _common_consts()
    gam = inputs["L2_ffn_bn_gamma"].reshape(2, 128).T
    bet = inputs["L2_ffn_bn_beta"].reshape(2, 128).T
    gb = np.concatenate([gam, bet], 1).astype(np.float32)
    st_all = np.zeros((128, 32), np.float32)
    for c in range(8):
        for jj in range(4):
            st_all[:, jj * 8 + c] = st_parts[c][:, jj]
    ee = np.arange(1024)
    r0 = np.zeros((128, 1024), np.float32)
    r1 = np.zeros((128, 1024), np.float32)
    for p in range(128):
        r0[p] = (ee // 32 == p % 32)
        r1[p] = (ee % 32 == p % 32)
    o16 = np.zeros((128, 256), np.float32)
    for qc in range(16):
        o16[:, qc * 16 + qc] = 1.0
    in_maps = []
    for c in range(N_CORES):
        tsl = slice(c * T_CORE, (c + 1) * T_CORE)
        in_maps.append({
            "x": np.ascontiguousarray(x_full[tsl]),
            "ap": np.ascontiguousarray(np.stack([p[tsl] for p in aps], 0).reshape(4 * T_CORE, DM)),
            "qin": q_parts[c],
            "stin": st_all, "gb": gb,
            "k0": inputs["L2_ffn_sub_key0"], "k1": inputs["L2_ffn_sub_key1"],
            "dn": np.ascontiguousarray(inputs["L2_ffn_expert_down"].T),
            "up": np.ascontiguousarray(inputs["L2_ffn_expert_up"]),
            "r0": r0, "r1": r1, "o16": o16,
            "idm": cc["idm"], "cst": cc["cst"],
        })
    return in_maps


def emit_fp4_quant(plan, out_bf, v_ap, W, scr, tag="q"):
    v, a = "vector", "scalar"
    NBK = W // 16
    ab, u, r2, r4, ist, ri, bm, bi = (scr[k] for k in ("ab", "u", "r2", "r4", "ist", "ri", "bm", "bi"))
    b3 = lambda t: t[:, 0:NBK].rearrange("p (n o) -> p n o", n=NBK).broadcast_to([128, NBK, 16])
    ev(plan, a, lambda e: e.activation(ab[:, 0:W], v_ap, AF.Abs), reads=[v_ap], writes=[ab], tag=tag + ".ab")
    ev(plan, v, lambda e: e.tensor_reduce(
        bm[:, 0:NBK].rearrange("p (n o) -> p n o", n=NBK),
        ab[:, 0:W].rearrange("p (n i) -> p n i", n=NBK), AX.X, ALU.max),
       reads=[ab], writes=[bm], tag=tag + ".bm")
    ev(plan, v, lambda e: e.tensor_scalar(bm[:, 0:NBK], bm[:, 0:NBK], 1.0 / 6, 1e-12, ALU.mult, ALU.max),
       reads=[bm], writes=[bm], tag=tag + ".s")
    ev(plan, v, lambda e: e.reciprocal(bi[:, 0:NBK], bm[:, 0:NBK]), reads=[bm], writes=[bi], tag=tag + ".ri")
    ev(plan, v, lambda e: e.tensor_tensor(
        u[:, 0:W].rearrange("p (n i) -> p n i", n=NBK),
        ab[:, 0:W].rearrange("p (n i) -> p n i", n=NBK), b3(bi), ALU.mult),
       reads=[ab, bi], writes=[u], tag=tag + ".u")
    ev(plan, v, lambda e: e.tensor_scalar(r2[:, 0:W], u[:, 0:W], 2.0, None, ALU.is_ge),
       reads=[u], writes=[r2], tag=tag + ".r2")
    ev(plan, v, lambda e: e.tensor_scalar(r4[:, 0:W], u[:, 0:W], 4.0, None, ALU.is_ge),
       reads=[u], writes=[r4], tag=tag + ".r4")
    ev(plan, v, lambda e: e.tensor_scalar(ist[:, 0:W], r2[:, 0:W], -1.0, 2.0, ALU.mult, ALU.add),
       reads=[r2], writes=[ist], tag=tag + ".i1")
    ev(plan, v, lambda e: e.scalar_tensor_tensor(ist[:, 0:W], r4[:, 0:W], -0.5, ist[:, 0:W], ALU.mult, ALU.add),
       reads=[r4, ist], writes=[ist], tag=tag + ".i2")
    ev(plan, v, lambda e: e.tensor_tensor(
        ab[:, 0:W].rearrange("p (n i) -> p n i", n=NBK),
        v_ap.rearrange("p (n i) -> p n i", n=NBK), b3(bi), ALU.mult),
       reads=[v_ap, bi], writes=[ab], tag=tag + ".t1")
    ev(plan, v, lambda e: e.tensor_mul(ab[:, 0:W], ab[:, 0:W], ist[:, 0:W]),
       reads=[ab, ist], writes=[ab], tag=tag + ".t2")
    ev(plan, v, lambda e: e.tensor_copy(ri[:, 0:W], ab[:, 0:W]), reads=[ab], writes=[ri], tag=tag + ".c1")
    ev(plan, v, lambda e: e.tensor_copy(ab[:, 0:W], ri[:, 0:W]), reads=[ri], writes=[ab], tag=tag + ".c2")
    ev(plan, v, lambda e: e.tensor_scalar(ist[:, 0:W], r2[:, 0:W], 0.5, 0.5, ALU.mult, ALU.add),
       reads=[r2], writes=[ist], tag=tag + ".s1")
    ev(plan, v, lambda e: e.tensor_add(ist[:, 0:W], ist[:, 0:W], r4[:, 0:W]),
       reads=[ist, r4], writes=[ist], tag=tag + ".s2")
    ev(plan, v, lambda e: e.tensor_mul(ab[:, 0:W], ab[:, 0:W], ist[:, 0:W]),
       reads=[ab, ist], writes=[ab], tag=tag + ".q1")
    ev(plan, v, lambda e: e.tensor_tensor(
        out_bf.rearrange("p (n i) -> p n i", n=NBK),
        ab[:, 0:W].rearrange("p (n i) -> p n i", n=NBK), b3(bm), ALU.mult),
       reads=[ab, bm], writes=[out_bf], tag=tag + ".q2")


def build_f3a():
    nc = bass.Bass("TRN2", target_bir_lowering=False, debug=False)
    I3 = 2816
    GRPS = 3
    d_wg = nc.dram_tensor("wg", [GRPS * 128, DM], F32, kind="ExternalInput").ap()
    d_wu = nc.dram_tensor("wu", [GRPS * 128, DM], F32, kind="ExternalInput").ap()
    d_wd = nc.dram_tensor("wd", [128, I3], F32, kind="ExternalInput").ap()
    d_id = nc.dram_tensor("idm", [128, 128], F32, kind="ExternalInput").ap()
    d_gT = nc.dram_tensor("gT", [DM, GRPS * 128], BF16, kind="ExternalOutput").ap()
    d_uT = nc.dram_tensor("uT", [DM, GRPS * 128], BF16, kind="ExternalOutput").ap()
    d_dT = nc.dram_tensor("dT", [I3, 128], BF16, kind="ExternalOutput").ap()

    es = ExitStack()
    sb = lambda n, s, dt=F32: es.enter_context(nc.sbuf_tensor(n, s, dt))
    ps = lambda n, s, dt=F32: es.enter_context(nc.psum_tensor(n, s, dt))
    with es:
        wrow = sb("s_wrow", [128, 2816])
        wq = sb("s_wq", [128, 2816], BF16)
        wt = sb("s_wt", [128, 2816], BF16)
        ab = sb("s_ab", [128, 2816]); u = sb("s_u", [128, 2816], BF16)
        r2 = sb("s_r2", [128, 2816], BF16); r4 = sb("s_r4", [128, 2816], BF16)
        ist = sb("s_ist", [128, 2816])
        ri = es.enter_context(nc.sbuf_tensor("s_ri", [128, 2816], mybir.dt.int32))
        bm = sb("s_bm", [128, 176]); bi = sb("s_bi", [128, 176])
        idm = sb("s_idm", [128, 128], BF16)
        PG1 = ps("PG1", [128, 1024], BF16)
        PG2 = ps("PG2", [128, 1024], BF16)
        scr = dict(ab=ab, u=u, r2=r2, r4=r4, ist=ist, ri=ri, bm=bm, bi=bi)

        plan = Plan(nc)
        g, s, v, a = "gpsimd", "sync", "vector", "scalar"
        plan.dma(g, idm[:], d_id)
        d_wg4 = d_wg.rearrange("(c p) o -> p c o", p=128)
        d_wu4 = d_wu.rearrange("(c p) o -> p c o", p=128)
        d_gT3 = d_gT.rearrange("(c p) o -> p c o", p=128)
        d_uT3 = d_uT.rearrange("(c p) o -> p c o", p=128)
        d_dT3 = d_dT.rearrange("(c p) o -> p c o", p=128)
        for which, (din, dout, W_, n_in) in enumerate(
                ((d_wg4, d_gT3, 1024, GRPS), (d_wu4, d_uT3, 1024, GRPS), (None, d_dT3, 2816, 1))):
            for rg in range(n_in):
                if which < 2:
                    plan.dma(s, wrow[:, 0:W_], din[:, rg, :], tag=f"lw{which}.{rg}")
                else:
                    plan.dma(s, wrow[:, 0:W_], d_wd, tag=f"lw{which}.{rg}")
                emit_fp4_quant(plan, wq[:, 0:W_], wrow[:, 0:W_], W_, scr, tag=f"qw{which}.{rg}")
                for cc in range(W_ // 128):
                    PG = PG1 if cc % 2 == 0 else PG2
                    transpose_step(plan, PG[:, 0:128], wq[:, cc * 128:(cc + 1) * 128], idm[:],
                                   tag=f"wtr{which}.{rg}.{cc}")
                    ev(plan, a, lambda e, cc=cc, PG=PG: e.copy(wt[:, cc * 128:(cc + 1) * 128], PG[:, 0:128]),
                       reads=[PG], writes=[wt], tag=f"wtre{which}.{rg}.{cc}")
                for cc in range(W_ // 128):
                    plan.dma(g, dout[:, cc, rg * 128:(rg + 1) * 128], wt[:, cc * 128:(cc + 1) * 128],
                             tag=f"sw{which}.{rg}.{cc}")
        plan.barrier_outputs([d_gT, d_uT, d_dT])
        plan.build(es)
    return nc


def prep_f3a(inputs):
    cc = _common_consts()
    wg = inputs["L3_ffn_w_gate"]; wu = inputs["L3_ffn_w_up"]; wd = inputs["L3_ffn_w_down"]
    in_maps = []
    for c in range(N_CORES):
        grps = [c, c + 8, min(c + 16, 21)]
        in_maps.append({
            "wg": np.ascontiguousarray(np.concatenate([wg[g_ * 128:(g_ + 1) * 128] for g_ in grps], 0)),
            "wu": np.ascontiguousarray(np.concatenate([wu[g_ * 128:(g_ + 1) * 128] for g_ in grps], 0)),
            "wd": np.ascontiguousarray(wd[c * 128:(c + 1) * 128]),
            "idm": cc["idm"],
        })
    return in_maps


def gather_f3a(inputs, results):
    I3 = 2816
    gT = np.zeros((DM, I3), dtype=BF16NP)
    uT = np.zeros((DM, I3), dtype=BF16NP)
    dT = np.zeros((I3, DM), dtype=BF16NP)
    for c in range(N_CORES):
        grps = [c, c + 8, min(c + 16, 21)]
        for rg, g_ in enumerate(grps):
            gT[:, g_ * 128:(g_ + 1) * 128] = results[c]["gT"][:, rg * 128:(rg + 1) * 128]
            uT[:, g_ * 128:(g_ + 1) * 128] = results[c]["uT"][:, rg * 128:(rg + 1) * 128]
        dT[:, c * 128:(c + 1) * 128] = results[c]["dT"]
    return gT, uT, dT


def build_f3b():
    nc = bass.Bass("TRN2", target_bir_lowering=False, debug=False)
    TC = T_CORE
    NB = TC // 128
    I3 = 2816
    NIC = 22

    d_x = nc.dram_tensor("x", [TC, DM], F32, kind="ExternalInput").ap()
    d_a0 = nc.dram_tensor("a0", [TC, DM], F32, kind="ExternalInput").ap()
    d_a1 = nc.dram_tensor("a1", [TC, DM], F32, kind="ExternalInput").ap()
    d_gT = nc.dram_tensor("gT", [DM, I3], BF16, kind="ExternalInput").ap()
    d_uT = nc.dram_tensor("uT", [DM, I3], BF16, kind="ExternalInput").ap()
    d_dT = nc.dram_tensor("dT", [I3, DM], BF16, kind="ExternalInput").ap()
    d_id = nc.dram_tensor("idm", [128, 128], F32, kind="ExternalInput").ap()
    d_cst = nc.dram_tensor("cst", [128, 8], F32, kind="ExternalInput").ap()
    d_out = nc.dram_tensor("out", [TC, DM], F32, kind="ExternalOutput").ap()

    es = ExitStack()
    sb = lambda n, s, dt=F32: es.enter_context(nc.sbuf_tensor(n, s, dt))
    ps = lambda n, s, dt=F32: es.enter_context(nc.psum_tensor(n, s, dt))
    with es:
        wg = sb("s_wg", [128, 8 * I3], BF16)
        wu = sb("s_wu", [128, 8 * I3], BF16)
        wd = sb("s_wd", [128, NIC * 1024], BF16)
        xt = sb("s_xt", [128, 1024])
        xn_tm = sb("s_xn_tm", [128, 1024], BF16)
        xq_tm = sb("s_xq", [128, 1024], BF16)
        xq_fm = sb("s_xqfm", [128, 2 * 8 * 128], BF16)
        h_tm = sb("s_h", [128, I3])
        hq = sb("s_hq", [128, I3], BF16)
        hqT = sb("s_hqT", [128, NIC * 128], BF16)
        ab = sb("s_ab", [128, I3]); u = sb("s_u", [128, I3], BF16)
        r2 = sb("s_r2", [128, I3], BF16); r4 = sb("s_r4", [128, I3], BF16)
        ist = sb("s_ist", [128, I3], BF16)
        ri = ab[:].bitcast(mybir.dt.int32)
        bm = sb("s_bm", [128, 176]); bi = sb("s_bi", [128, 176])
        res = sb("s_res", [128, 1024])
        att = None
        c1 = sb("s_c1", [128, 1]); c2 = sb("s_c2", [128, 1]); c3 = sb("s_c3", [128, 1])
        idm = sb("s_idm", [128, 128], BF16)
        cst = sb("s_cst", [128, 8])
        st6 = sb("s_st6", [128, 12]); st2 = sb("s_st2", [128, 2])
        rstd1 = sb("s_rstd1", [128, 1]); b2c_ = sb("s_b2c", [128, 1])
        stats = dict(s6=st6, s2=st2, rstd=rstd1, b2=b2c_, eps=cst[:, 0:1])
        scr = dict(ab=ab, u=u, r2=r2, r4=r4, ist=ist, ri=ri, bm=bm, bi=bi)

        PA = ps("PA", [128, 512]); PB = ps("PB", [128, 512])
        PC = ps("PC", [128, 512]); PD = ps("PD", [128, 512])
        PG1 = ps("PG1", [128, 1024], BF16)
        PG2 = ps("PG2", [128, 1024], BF16)

        plan = Plan(nc)
        g, s, v, a = "gpsimd", "sync", "vector", "scalar"
        plan.dma(g, wg[:].rearrange("p (c o) -> p c o", c=8), d_gT.rearrange("(c p) o -> p c o", p=128))
        plan.dma(g, wu[:].rearrange("p (c o) -> p c o", c=8), d_uT.rearrange("(c p) o -> p c o", p=128))
        plan.dma(g, wd[:].rearrange("p (c o) -> p c o", c=NIC), d_dT.rearrange("(c p) o -> p c o", p=128))
        plan.dma(g, idm[:], d_id)
        plan.dma(s, cst[:], d_cst)
        wg3 = wg[:].rearrange("p (c o) -> p c o", c=8)
        wu3 = wu[:].rearrange("p (c o) -> p c o", c=8)
        wd3 = wd[:].rearrange("p (c o) -> p c o", c=NIC)
        xq3 = xq_fm[:].rearrange("p (b c t) -> p b c t", b=2, c=8)
        hqT3 = hqT[:].rearrange("p (c t) -> p c t", c=NIC)
        x_dram3 = d_x.rearrange("(b p) f -> p b f", p=128)
        a0_dram3 = d_a0.rearrange("(b p) f -> p b f", p=128)
        a1_dram3 = d_a1.rearrange("(b p) f -> p b f", p=128)
        out_dram3 = d_out.rearrange("(b p) f -> p b f", p=128)
        SQ1E = float(np.sqrt(1.0 + 1e-6))

        for blk in range(NB):
            wb = blk % 2
            plan.dma(s, xt[:], x_dram3[:, blk, :], tag=f"ldx{blk}")
            ln_plain_tile(plan, None, xn_tm[:], xt[:], stats, tag=f"ln{blk}")
            emit_fp4_quant(plan, xq_tm[:], xn_tm[:], 1024, scr, tag=f"qx{blk}")
            for cc in range(8):
                PG = PG1 if cc % 2 == 0 else PG2
                transpose_step(plan, PG[:, 0:128], xq_tm[:, cc * 128:(cc + 1) * 128], idm[:],
                               tag=f"xtr{blk}.{cc}")
                ev(plan, a, lambda e, cc=cc, wb=wb, PG=PG: e.copy(xq3[:, wb, cc, :], PG[:, 0:128]),
                   reads=[PG], writes=[xq_fm], tag=f"xtre{blk}.{cc}")
            for io in range(6):
                W_ = 512 if io < 5 else 256
                isl = slice(io * 512, io * 512 + W_)
                mm(plan, PA[:, 0:W_], [(xq3[:, wb, ic, :], wg3[:, ic, isl]) for ic in range(8)],
                   tag=f"g{blk}.{io}")
                mm(plan, PB[:, 0:W_], [(xq3[:, wb, ic, :], wu3[:, ic, isl]) for ic in range(8)],
                   tag=f"u{blk}.{io}")
                ev(plan, a, lambda e, W_=W_, isl=isl: e.activation(ab[:, 0:W_], PA[:, 0:W_], AF.Silu),
                   reads=[PA], writes=[ab], tag=f"sl{blk}.{io}")
                ev(plan, v, lambda e, W_=W_, isl=isl: e.tensor_tensor(
                    h_tm[:, isl], ab[:, 0:W_], PB[:, 0:W_], ALU.mult),
                   reads=[ab, PB], writes=[h_tm], tag=f"h{blk}.{io}")
            emit_fp4_quant(plan, hq[:], h_tm[:], I3, scr, tag=f"qh{blk}")
            for cc in range(NIC):
                PG = PG1 if cc % 2 == 0 else PG2
                transpose_step(plan, PG[:, 0:128], hq[:, cc * 128:(cc + 1) * 128], idm[:],
                               tag=f"htr{blk}.{cc}")
                ev(plan, a, lambda e, cc=cc, PG=PG: e.copy(hqT3[:, cc, :], PG[:, 0:128]),
                   reads=[PG], writes=[hqT], tag=f"htre{blk}.{cc}")
            for oh_ in range(2):
                P = PC if oh_ == 0 else PD
                mm(plan, P[:], [(hqT3[:, ic, :], wd3[:, ic, oh_ * 512:(oh_ + 1) * 512]) for ic in range(NIC)],
                   tag=f"d{blk}.{oh_}")
                ev(plan, a, lambda e, oh_=oh_, P=P: e.copy(res[:, oh_ * 512:(oh_ + 1) * 512], P[:]),
                   reads=[P], writes=[res], tag=f"de{blk}.{oh_}")
            att = h_tm[:, 1024:2048]
            plan.dma(s, att, a0_dram3[:, blk, :], tag=f"la0{blk}")
            plan.dma(s, xt[:], a1_dram3[:, blk, :], tag=f"la1{blk}")
            ev(plan, v, lambda e: e.tensor_add(att, att, xt[:]), reads=[h_tm, xt], writes=[h_tm])
            ev(plan, v, lambda e: e.tensor_scalar_mul(att, att, 0.1), reads=[h_tm], writes=[h_tm])
            ev(plan, v, lambda e: e.tensor_scalar_add(att[:, 0:1], att[:, 0:1], SQ1E),
               reads=[h_tm], writes=[h_tm])
            ev(plan, v, lambda e: e.tensor_scalar(att[:, 1:1024], att[:, 1:1024], -8.0, 8.0, ALU.max, ALU.min),
               reads=[h_tm], writes=[h_tm])
            ev(plan, a, lambda e: e.activation(ab[:, 0:1024], att, AF.Square, accum_out=c1[:]),
               reads=[h_tm], writes=[ab, c1])
            ev(plan, v, lambda e: e.tensor_mul(c2[:], att[:, 0:1], att[:, 0:1]), reads=[h_tm], writes=[c2])
            ev(plan, v, lambda e: e.tensor_sub(c1[:], c1[:], c2[:]), reads=[c1, c2], writes=[c1])
            ev(plan, a, lambda e: e.activation(c3[:], c1[:], AF.Sqrt, bias=cst[:, 1:2]),
               reads=[c1, cst], writes=[c3])
            ev(plan, v, lambda e: e.tensor_copy(att[:, 0:1], c3[:]), reads=[c3], writes=[h_tm])
            plan.dma(s, xt[:], x_dram3[:, blk, :], tag=f"rx{blk}")
            ev(plan, v, lambda e: e.tensor_add(res[:], res[:], att), reads=[res, h_tm], writes=[res])
            ev(plan, v, lambda e: e.tensor_add(res[:], res[:], xt[:]), reads=[res, xt], writes=[res])
            plan.dma(g, out_dram3[:, blk, :], res[:], tag=f"so{blk}")

        plan.barrier_outputs([d_out])
        plan.build(es)
    return nc


def prep_f3b(inputs, x_full, a0, a1, gT, uT, dT):
    cc = _common_consts()
    in_maps = []
    for c in range(N_CORES):
        tsl = slice(c * T_CORE, (c + 1) * T_CORE)
        in_maps.append({
            "x": np.ascontiguousarray(x_full[tsl]),
            "a0": np.ascontiguousarray(a0.reshape(T_ALL, DM)[tsl]),
            "a1": np.ascontiguousarray(a1.reshape(T_ALL, DM)[tsl]),
            "gT": gT, "uT": uT, "dT": dT,
            "idm": cc["idm"], "cst": cc["cst"],
        })
    return in_maps


def _fm_to_tm(outs):
    p0 = np.concatenate([np.ascontiguousarray(outs[2 * b].T) for b in range(4)], 0)
    p1 = np.concatenate([np.ascontiguousarray(outs[2 * b + 1].T) for b in range(4)], 0)
    return p0, p1


def kernel(**inputs):
    inputs = {k: np.asarray(v) for k, v in inputs.items()}
    cores = list(range(N_CORES))
    run = run_bass_kernel_spmd

    x0 = np.ascontiguousarray(inputs["L0_attn_x"].reshape(T_ALL, DM).astype(np.float32))

    r = run(build_a0(), prep_a0(inputs), core_ids=cores)
    a0p0, a0p1 = _fm_to_tm([res["out"] for res in r.results])
    nc_f0 = build_f0()
    x1_parts = []
    for half in (0, 1):
        r = run(nc_f0, prep_f0(inputs, x0, a0p0, a0p1, half), core_ids=cores)
        x1_parts.append([res["out"] for res in r.results])
    x1 = np.concatenate([np.concatenate([x1_parts[0][c], x1_parts[1][c]], 0) for c in range(8)], 0)

    inputs["L1_attn_x_staged"] = x1.reshape(B, SEQ, DM)
    inputs["L1_ffn_x_staged"] = inputs["L1_attn_x_staged"]
    r = run(build_attn("l1a"), prep_attn("l1a", inputs), core_ids=cores)
    aA0, aA1 = _fm_to_tm([res["out"] for res in r.results])
    r = run(build_attn("l1b"), prep_attn("l1b", inputs), core_ids=cores)
    aB0, aB1 = _fm_to_tm([res["out"] for res in r.results])
    nc_f1 = build_f1()
    x2_parts = []
    for half in (0, 1):
        r = run(nc_f1, prep_f1(inputs, x1, aA0, aA1, aB0, aB1, half), core_ids=cores)
        x2_parts.append([res["out"] for res in r.results])
    x2 = np.concatenate([np.concatenate([x2_parts[0][c], x2_parts[1][c]], 0) for c in range(8)], 0)

    inputs["L2_attn_x_staged"] = x2.reshape(B, SEQ, DM)
    r = run(build_attn("l2"), prep_attn("l2", inputs), core_ids=cores)
    l2outs = [res["out"] for res in r.results]
    aps = []
    for part in range(2):
        for hg in range(2):
            aps.append(np.concatenate(
                [np.ascontiguousarray(l2outs[2 * b + hg][part * DM:(part + 1) * DM].T) for b in range(4)], 0))
    r = run(build_f2a(), prep_f2a(inputs, x2), core_ids=cores)
    q_parts = [res["qout"] for res in r.results]
    st_parts = [res["stout"] for res in r.results]
    r = run(build_f2b(), prep_f2b(inputs, x2, aps, q_parts, st_parts), core_ids=cores)
    x3 = np.concatenate([res["out"] for res in r.results], 0)

    inputs["L3_attn_x_staged"] = x3.reshape(B, SEQ, DM)
    r = run(build_attn("l3"), prep_attn("l3", inputs), core_ids=cores)
    a3p0, a3p1 = _fm_to_tm([res["out"] for res in r.results])
    r = run(build_f3a(), prep_f3a(inputs), core_ids=cores)
    gT, uT, dT = gather_f3a(inputs, [res for res in r.results])
    r = run(build_f3b(), prep_f3b(inputs, x3, a3p0, a3p1, gT, uT, dT), core_ids=cores)
    x4 = np.concatenate([res["out"] for res in r.results], 0)

    return np.ascontiguousarray(x4.reshape(B, SEQ, DM).astype(np.float32))


# version 03 (final)
# speedup vs baseline: 1.1642x; 1.1642x over previous
import numpy as np
import ml_dtypes
from contextlib import ExitStack
import concourse.bass as bass
import concourse.mybir as mybir
from concourse.bass_utils import run_bass_kernel_spmd

F32 = mybir.dt.float32
BF16 = mybir.dt.bfloat16
AF = mybir.ActivationFunctionType
ALU = mybir.AluOpType
AX = mybir.AxisListType
BF16NP = ml_dtypes.bfloat16

N_CORES = 8
B, SEQ, DM = 4, 4096, 1024
T_ALL = B * SEQ
T_CORE = T_ALL // N_CORES


class Plan:
    COMPUTE = ("tensor", "vector", "scalar", "gpsimd", "sync")

    def __init__(self, nc):
        self.nc = nc
        self.steps = []

    @staticmethod
    def _h(x):
        t = getattr(x, "tensor", None)
        return id(t) if t is not None else id(x)

    def step(self, eng, emit, reads=(), writes=(), dma=False, tag=""):
        self.steps.append(dict(
            eng=eng, emit=emit,
            reads=[self._h(r) for r in reads],
            writes=[self._h(w) for w in writes],
            dma=dma, tag=tag))

    def dma(self, q, out, in_, tag=""):
        def emit(e, out=out, in_=in_):
            return e.dma_start(out=out, in_=in_)
        self.step(q, emit, reads=[in_], writes=[out], dma=True, tag=tag)

    def barrier_outputs(self, bufs):
        def emit(e):
            return e.nop()
        self.step("sync", emit, reads=list(bufs), writes=(), tag="final_barrier")

    DMA_POOL = 6

    def build(self, es):
        nc = self.nc
        vengs = set()
        qcount = {}
        for st in self.steps:
            if st["dma"]:
                q = "q_" + st["eng"]
                i = qcount.get(q, 0)
                qcount[q] = i + 1
                st["veng"] = f"{q}.{i % self.DMA_POOL}"
                st["slot_round"] = i // self.DMA_POOL
            else:
                st["veng"] = st["eng"]
            vengs.add(st["veng"])
            vengs.add(st["eng"])
        sems = {v: es.enter_context(nc.semaphore("sem_" + v.replace(".", "_")))
                for v in sorted(vengs)}
        seq = {v: 0 for v in vengs}
        last_w = {}
        readers = {}
        for st in self.steps:
            v = st["veng"]
            inc = 16 if st["dma"] else 1
            s = seq[v] + inc
            deps = {}

            def add_dep(dv, ds):
                deps[dv] = max(deps.get(dv, 0), ds)

            for b in st["reads"]:
                lw = last_w.get(b)
                if lw:
                    add_dep(*lw)
            for b in st["writes"]:
                lw = last_w.get(b)
                if lw:
                    add_dep(*lw)
                for dv, ds in readers.get(b, {}).items():
                    add_dep(dv, ds)
            if st["dma"] and st["slot_round"] > 0:
                add_dep(v, 16 * st["slot_round"])
            deps.pop(None, None)
            deps = {dv: ds for dv, ds in deps.items()
                    if not (dv == v and not st["dma"] and ds >= s)}
            st["deps"] = deps
            st["seqno"] = s
            seq[v] = s
            for b in st["reads"]:
                readers.setdefault(b, {})[v] = s
            for b in st["writes"]:
                last_w[b] = (v, s)
                readers[b] = {}
        for v_ in sorted(vengs):
            nc.gpsimd.sem_clear(sems[v_])
        nc.all_engine_barrier()
        block = es.enter_context(nc.Block())
        for eng_name in self.COMPUTE:
            mysteps = [st for st in self.steps if st["eng"] == eng_name]
            if not mysteps:
                continue

            def body(e, mysteps=mysteps, eng_name=eng_name):
                wm = {}
                for st in mysteps:
                    for dv, val in sorted(st["deps"].items()):
                        if wm.get(dv, 0) < val:
                            e.wait_ge(sems[dv], val)
                            wm[dv] = val
                    inst = st["emit"](e)
                    if inst is not None:
                        inst.then_inc(sems[st["veng"]], 16 if st["dma"] else 1)
                    else:
                        raise RuntimeError(f"step emit returned None: {st['tag']}")
            getattr(block, eng_name)(body)


def ev(plan, eng, fn, reads=(), writes=(), tag=""):
    plan.step(eng, fn, reads=reads, writes=writes, tag=tag)


def mm(plan, out, pairs, start=True, stop=True, tag="", extra_reads=()):
    rd = list(extra_reads)
    for l, r in pairs:
        rd += [l, r]

    def emit(e, out=out, pairs=tuple(pairs), start=start, stop=stop):
        n = len(pairs)
        inst = None
        for i, (l, r) in enumerate(pairs):
            inst = e.matmul(out, l, r, start=(start and i == 0),
                            stop=(stop and i == n - 1))
        return inst
    plan.step("tensor", emit, reads=rd, writes=[out], tag=tag)


def transpose_step(plan, out_psum, in_sb, ident, tag=""):
    def emit(e, out=out_psum, in_=in_sb, ident=ident):
        return e.transpose(out, in_, ident)
    plan.step("tensor", emit, reads=[in_sb, ident], writes=[out_psum], tag=tag)


def ln_plain_tile(plan, P, out_bf, x_f32, stats, eps=1e-5, tag="ln"):
    v = "vector"
    s6, s2, rstd, b2 = stats["s6"], stats["s2"], stats["rstd"], stats["b2"]
    W = x_f32.shape[-1]
    nh = (W + 511) // 512
    for i in range(nh):
        ev(plan, v, lambda e, i=i: e.bn_stats(s6[:, 6 * i:6 * i + 6], x_f32[:, 512 * i:512 * (i + 1)]),
           reads=[x_f32], writes=[s6], tag=tag + f".bs{i}")
    ev(plan, v, lambda e: e.bn_aggr(s2[:], s6[:, 0:6 * nh]), reads=[s6], writes=[s2], tag=tag + ".ba")
    ev(plan, "scalar", lambda e: e.activation(rstd[:], s2[:, 1:2], AF.Sqrt, bias=stats["eps"], scale=1.0),
       reads=[s2], writes=[rstd], tag=tag + ".sq")
    ev(plan, v, lambda e: e.reciprocal(rstd[:], rstd[:]), reads=[rstd], writes=[rstd], tag=tag + ".rc")
    ev(plan, v, lambda e: e.scalar_tensor_tensor(b2[:], s2[:, 0:1], -1.0, rstd[:], ALU.mult, ALU.mult),
       reads=[s2, rstd], writes=[b2], tag=tag + ".b2")
    ev(plan, v, lambda e: e.tensor_scalar(out_bf, x_f32, rstd[:], b2[:], ALU.mult, ALU.add),
       reads=[x_f32, rstd, b2], writes=[out_bf], tag=tag + ".ap")


def tr128(plan, psum_bf, src_bf, ident, dst_bf, tag="tr"):
    transpose_step(plan, psum_bf, src_bf, ident, tag=tag + ".t")
    ev(plan, "scalar", lambda e: e.copy(dst_bf, psum_bf), reads=[psum_bf], writes=[dst_bf],
       tag=tag + ".e")


def build_a0(dbg=False):
    nc = bass.Bass("TRN2", target_bir_lowering=False, debug=False)
    T = SEQ
    NB = T // 128
    NQ = T // 512
    SCALE = 1.0 / 8.0
    DL = [-384, -256, -128, 0, 128, 256]

    d_x = nc.dram_tensor("x", [T, DM], F32, kind="ExternalInput").ap()
    d_wq = nc.dram_tensor("wq", [DM, 512], F32, kind="ExternalInput").ap()
    d_wk = nc.dram_tensor("wk", [DM, 512], F32, kind="ExternalInput").ap()
    d_wv = nc.dram_tensor("wv", [DM, 512], F32, kind="ExternalInput").ap()
    d_wp = nc.dram_tensor("wp", [512, DM], F32, kind="ExternalInput").ap()
    d_cos = nc.dram_tensor("cos", [T, 64], F32, kind="ExternalInput").ap()
    d_sin = nc.dram_tensor("sin", [T, 64], F32, kind="ExternalInput").ap()
    d_qn = nc.dram_tensor("qn", [128, 2048], F32, kind="ExternalInput").ap()
    d_msk = nc.dram_tensor("msk", [128, 6 * 512], F32, kind="ExternalInput").ap()
    d_sink = nc.dram_tensor("sink", [1, 8], F32, kind="ExternalInput").ap()
    d_id = nc.dram_tensor("idm", [128, 128], F32, kind="ExternalInput").ap()
    d_oh = nc.dram_tensor("oh", [8, 8 * 128], F32, kind="ExternalInput").ap()
    d_o1 = nc.dram_tensor("o1", [1, 64], F32, kind="ExternalInput").ap()
    d_cst = nc.dram_tensor("cst", [128, 8], F32, kind="ExternalInput").ap()
    d_out = nc.dram_tensor("out", [DM, T], F32, kind="ExternalOutput").ap()
    if dbg:
        d_dq = nc.dram_tensor("dq", [128, 4 * T], BF16, kind="ExternalOutput").ap()
        d_dk = nc.dram_tensor("dk", [128, 4 * T], BF16, kind="ExternalOutput").ap()
        d_dv = nc.dram_tensor("dv", [128, 32 * 8 * 66], BF16, kind="ExternalOutput").ap()
        d_dm = nc.dram_tensor("dmx", [8, 8], F32, kind="ExternalOutput").ap()
        d_db = nc.dram_tensor("dbc", [128, 8], F32, kind="ExternalOutput").ap()
        d_dao = nc.dram_tensor("dao", [128, 4 * 512], BF16, kind="ExternalOutput").ap()

    es = ExitStack()
    sb = lambda n, s, dt=F32: es.enter_context(nc.sbuf_tensor(n, s, dt))
    ps = lambda n, s, dt=F32: es.enter_context(nc.psum_tensor(n, s, dt))
    with es:
        xt = sb("xt", [128, 4 * 1024])
        xn_tm = sb("xn_tm", [128, 4 * 1024], BF16)
        xn_fm = sb("xn_fm", [128, 8 * 512], BF16)
        q_fm = sb("q_fm", [128, 4 * T], BF16)
        k_fm = sb("k_fm", [128, 4 * T], BF16)
        v_tm = sb("v_tm", [128, NB * 8 * 66], BF16)
        ao = sb("ao", [128, 4 * 512], BF16)
        wq = sb("s_wq", [128, 8 * 512], BF16)
        wk = sb("s_wk", [128, 8 * 512], BF16)
        wv = sb("s_wv", [128, 8 * 512], BF16)
        wp = sb("s_wp", [128, 4 * 1024], BF16)
        ctab = sb("s_ctab", [128, NB * 64], BF16)
        stab = sb("s_stab", [128, NB * 64], BF16)
        qn4 = sb("qn4", [128, 2048])
        msk = sb("s_msk", [128, 6 * 512], BF16)
        snk = sb("snk", [1, 8])
        idm = sb("s_idm", [128, 128], BF16)
        oh8 = sb("oh8", [8, 8 * 128], BF16)
        o1 = sb("s_o1", [1, 64], BF16)
        qraw = sb("qraw", [128, 512])
        sq = sb("sq", [128, 512])
        u1 = sb("u1", [128, 512])
        u2 = sb("u2", [128, 512])
        qs8 = sb("qs8", [128, 8]); qss = sb("qss", [128, 8])
        mu8 = sb("mu8", [128, 8]); var8 = sb("var8", [128, 8])
        rstd8 = sb("rstd8", [128, 8]); b28 = sb("b28", [128, 8])
        nq8 = sb("nq8", [128, 8])
        runq = sb("runq", [128, 8]); runk = sb("runk", [128, 8])
        runb = sb("runb", [128, 8], BF16)
        mx8 = sb("mx8", [8, 8])
        csb = sb("csb", [8, 1], BF16)
        bias_col = sb("bias_col", [128, 8])
        sink_e = sb("sink_e", [1, 8])
        ptA = sb("ptA", [128, 512], BF16)
        ptB = sb("ptB", [128, 512], BF16)
        den = sb("den", [1, 512])
        denb = sb("denb", [1, 512], BF16)
        rb_sb = None
        outsb1 = None
        outsb2 = None
        st6 = sb("st6", [128, 12]); st2 = sb("st2", [128, 2])
        cst = sb("s_cst", [128, 8])
        rstd1 = sb("rstd1", [128, 1]); b2c = sb("b2c", [128, 1])
        stats = dict(s6=st6, s2=st2, rstd=rstd1, b2=b2c, eps=cst[:, 0:1])

        PA = ps("PA", [128, 512]); PB = ps("PB", [128, 512])
        PC = ps("PC", [128, 512])
        PEb = ps("PEb", [128, 512])
        PF = ps("PF", [128, 512]); PH = ps("PH", [128, 512])
        PG1 = ps("PG1", [128, 1024], BF16)
        PG2 = ps("PG2", [128, 1024], BF16)

        plan = Plan(nc)
        g, s, v, a = "gpsimd", "sync", "vector", "scalar"

        plan.dma(g, wq[:].rearrange("p (c o) -> p c o", c=8), d_wq.rearrange("(c p) o -> p c o", p=128))
        plan.dma(g, wk[:].rearrange("p (c o) -> p c o", c=8), d_wk.rearrange("(c p) o -> p c o", p=128))
        plan.dma(g, wv[:].rearrange("p (c o) -> p c o", c=8), d_wv.rearrange("(c p) o -> p c o", p=128))
        plan.dma(g, wp[:].rearrange("p (c o) -> p c o", c=4), d_wp.rearrange("(c p) o -> p c o", p=128))
        plan.dma(g, ctab[:].rearrange("p (b d) -> p b d", b=NB), d_cos.rearrange("(b p) d -> p b d", p=128))
        plan.dma(g, stab[:].rearrange("p (b d) -> p b d", b=NB), d_sin.rearrange("(b p) d -> p b d", p=128))
        plan.dma(g, qn4[:], d_qn)
        plan.dma(g, msk[:], d_msk)
        plan.dma(s, snk[:], d_sink)
        plan.dma(g, idm[:], d_id)
        plan.dma(g, oh8[:], d_oh)
        plan.dma(g, o1[:], d_o1)
        plan.dma(s, cst[:], d_cst)
        ev(plan, v, lambda e: e.memset(
            v_tm[:].rearrange("p (b h f) -> p b h f", b=NB, h=8)[:, :, :, 64:66], 1.0),
            writes=[v_tm], tag="vones")
        ev(plan, v, lambda e: e.memset(runq[:], 0.0), writes=[runq])
        ev(plan, v, lambda e: e.memset(runk[:], 0.0), writes=[runk])

        q_fm3 = q_fm[:].rearrange("p (hp t) -> p hp t", hp=4)
        k_fm3 = k_fm[:].rearrange("p (hp t) -> p hp t", hp=4)
        v_tm4 = v_tm[:].rearrange("p (b h f) -> p b h f", b=NB, h=8)
        xt3 = xt[:].rearrange("p (b f) -> p b f", b=4)
        xn_tm3 = xn_tm[:].rearrange("p (b f) -> p b f", b=4)
        xn_fm3 = xn_fm[:].rearrange("p (c t) -> p c t", c=8)
        ct3 = ctab[:].rearrange("p (b d) -> p b d", b=NB)
        st3 = stab[:].rearrange("p (b d) -> p b d", b=NB)
        msk3 = msk[:].rearrange("p (m t) -> p m t", m=6)
        wq3 = wq[:].rearrange("p (c o) -> p c o", c=8)
        wk3 = wk[:].rearrange("p (c o) -> p c o", c=8)
        wv3 = wv[:].rearrange("p (c o) -> p c o", c=8)
        wp3 = wp[:].rearrange("p (c o) -> p c o", c=4)
        x_dram3 = d_x.rearrange("(b p) f -> p b f", p=128)
        out_dram3 = d_out.rearrange("(c p) t -> c p t", p=128)
        ao2 = ao[:].rearrange("p (c t) -> p c t", c=4)[:, 0:2, :]
        ao3 = ao[:].rearrange("p (c t) -> p c t", c=4)

        def bc64(ap3):
            return ap3.rearrange("p (h o) -> p h o", h=8).broadcast_to([128, 8, 64])

        for ch in range(NQ):
            plan.dma(s, xt3[:, :, :], x_dram3[:, 4 * ch:4 * ch + 4, :], tag=f"ldx{ch}")
            for j in range(4):
                blk = 4 * ch + j
                ln_plain_tile(plan, None, xn_tm3[:, j, :], xt3[:, j, :], stats, tag=f"ln{blk}")
                for cc in range(8):
                    PG, PGo = (PG1, 0) if cc % 2 == 0 else (PG2, 0)
                    transpose_step(plan, PG[:, 0:128], xn_tm3[:, j, cc * 128:(cc + 1) * 128], idm[:],
                                   tag=f"xtr{blk}.{cc}")
                    ev(plan, a, lambda e, cc=cc, j=j, PG=PG: e.copy(
                        xn_fm3[:, cc, j * 128:(j + 1) * 128], PG[:, 0:128]),
                       reads=[PG], writes=[xn_fm], tag=f"xtre{blk}.{cc}")
            for j in range(4):
                blk = 4 * ch + j
                cstep = ct3[:, blk, :].rearrange("p (x d) -> p x d", x=1).broadcast_to([128, 8, 64])
                sstep = st3[:, blk, :].rearrange("p (x d) -> p x d", x=1).broadcast_to([128, 8, 64])
                mm(plan, PA[:], [(xn_fm3[:, ic, j * 128:(j + 1) * 128], wv3[:, ic, :]) for ic in range(8)],
                   tag=f"v{blk}")
                ev(plan, v, lambda e, blk=blk: e.tensor_copy(
                    v_tm4[:, blk, :, 0:64], PA[:].rearrange("p (h d) -> p h d", h=8)),
                   reads=[PA], writes=[v_tm], tag=f"ve{blk}")
                for which, (wmat, dst3, qoff, run) in enumerate(
                        ((wq3, q_fm3, 0, runq), (wk3, k_fm3, 1024, runk))):
                    P = PB
                    mm(plan, P[:], [(xn_fm3[:, ic, j * 128:(j + 1) * 128], wmat[:, ic, :]) for ic in range(8)],
                       tag=f"qk{blk}.{which}")
                    ev(plan, a, lambda e, P=P: e.copy(qraw[:], P[:]), reads=[P], writes=[qraw])
                    ev(plan, a, lambda e, P=P: e.activation(sq[:], P[:], AF.Square), reads=[P], writes=[sq])
                    qraw3 = qraw[:].rearrange("p (h d) -> p h d", h=8)
                    sq3 = sq[:].rearrange("p (h d) -> p h d", h=8)
                    ev(plan, v, lambda e, qraw3=qraw3: e.tensor_reduce(
                        qs8[:].rearrange("p (h o) -> p h o", h=8), qraw3, AX.X, ALU.add),
                       reads=[qraw], writes=[qs8])
                    ev(plan, v, lambda e, sq3=sq3: e.tensor_reduce(
                        qss[:].rearrange("p (h o) -> p h o", h=8), sq3, AX.X, ALU.add),
                       reads=[sq], writes=[qss])
                    ev(plan, v, lambda e: e.tensor_scalar_mul(mu8[:], qs8[:], 1.0 / 64),
                       reads=[qs8], writes=[mu8])
                    ev(plan, v, lambda e: e.scalar_tensor_tensor(var8[:], mu8[:], -1.0, mu8[:], ALU.mult, ALU.mult),
                       reads=[mu8], writes=[var8])
                    ev(plan, v, lambda e: e.scalar_tensor_tensor(var8[:], qss[:], 1.0 / 64, var8[:], ALU.mult, ALU.add),
                       reads=[qss, var8], writes=[var8])
                    ev(plan, a, lambda e: e.activation(rstd8[:], var8[:], AF.Sqrt, bias=cst[:, 0:1], scale=1.0),
                       reads=[var8, cst], writes=[rstd8])
                    ev(plan, v, lambda e: e.reciprocal(rstd8[:], rstd8[:]), reads=[rstd8], writes=[rstd8])
                    ev(plan, v, lambda e: e.scalar_tensor_tensor(b28[:], mu8[:], -1.0, rstd8[:], ALU.mult, ALU.mult),
                       reads=[mu8, rstd8], writes=[b28])
                    u13 = u1[:].rearrange("p (h d) -> p h d", h=8)
                    ev(plan, v, lambda e, qraw3=qraw3, u13=u13: e.tensor_tensor(
                        u13, qraw3, bc64(rstd8[:]), ALU.mult), reads=[qraw, rstd8], writes=[u1])
                    ev(plan, v, lambda e, u13=u13: e.tensor_tensor(
                        u13, u13, bc64(b28[:]), ALU.add), reads=[u1, b28], writes=[u1])
                    ev(plan, v, lambda e, qoff=qoff: e.tensor_mul(u1[:], u1[:], qn4[:, qoff:qoff + 512]),
                       reads=[u1, qn4], writes=[u1])
                    ev(plan, v, lambda e, qoff=qoff: e.tensor_add(u1[:], u1[:], qn4[:, qoff + 512:qoff + 1024]),
                       reads=[u1, qn4], writes=[u1])
                    ev(plan, a, lambda e: e.activation(sq[:], u1[:], AF.Square), reads=[u1], writes=[sq])
                    ev(plan, v, lambda e, sq3=sq3: e.tensor_reduce(
                        nq8[:].rearrange("p (h o) -> p h o", h=8), sq3, AX.X, ALU.add),
                       reads=[sq], writes=[nq8])
                    ev(plan, v, lambda e, run=run: e.tensor_max(run[:], run[:], nq8[:]),
                       reads=[run, nq8], writes=[run])
                    u23 = u2[:].rearrange("p (h d) -> p h d", h=8)
                    u14 = u1[:].rearrange("p (h x d) -> p h x d", h=8, x=2)
                    sq4 = sq[:].rearrange("p (h x d) -> p h x d", h=8, x=2)
                    ct4 = ct3[:, blk, :].rearrange("p (x d) -> p x d", x=1).broadcast_to([128, 8, 64])
                    st4a = st3[:, blk, 0:32].rearrange("p (x d) -> p x d", x=1).broadcast_to([128, 8, 32])
                    st4b = st3[:, blk, 32:64].rearrange("p (x d) -> p x d", x=1).broadcast_to([128, 8, 32])
                    ev(plan, v, lambda e, u23=u23, u13=u13, ct4=ct4: e.tensor_tensor(
                        u23, u13, ct4, ALU.mult), reads=[u1, ctab], writes=[u2])
                    ev(plan, v, lambda e, sq4=sq4, u14=u14, st4a=st4a: e.tensor_tensor(
                        sq4[:, :, 0, :], u14[:, :, 1, :], st4a, ALU.mult), reads=[u1, stab], writes=[sq])
                    ev(plan, v, lambda e, sq4=sq4, u14=u14, st4b=st4b: e.tensor_tensor(
                        sq4[:, :, 1, :], u14[:, :, 0, :], st4b, ALU.mult), reads=[u1, stab], writes=[sq])
                    ev(plan, v, lambda e, u23=u23, sq3=sq3: e.tensor_tensor(
                        u23, u23, sq3, ALU.add), reads=[u2, sq], writes=[u2])
                    ev(plan, v, lambda e: e.tensor_copy(ptA[:], u2[:]), reads=[u2], writes=[ptA])
                    for hp in range(4):
                        PG = PG1 if hp % 2 == 0 else PG2
                        transpose_step(plan, PG[:, 128:256], ptA[:, hp * 128:(hp + 1) * 128], idm[:],
                                       tag=f"qtr{blk}.{which}.{hp}")
                        ev(plan, a, lambda e, dst3=dst3, hp=hp, blk=blk, PG=PG: e.copy(
                            dst3[:, hp, blk * 128:(blk + 1) * 128], PG[:, 128:256]),
                           reads=[PG], writes=[dst3], tag=f"qtre{blk}.{which}.{hp}")

        for run, col in ((runq, 0), (runk, 1)):
            ev(plan, v, lambda e, run=run: e.tensor_copy(runb[:], run[:]), reads=[run], writes=[runb])
            transpose_step(plan, PG1[0:8, 0:128], runb[:, 0:8], idm[:], tag=f"ntr{col}")
            ev(plan, a, lambda e: e.copy(ptB[0:8, 0:128], PG1[0:8, 0:128]), reads=[PG1], writes=[ptB])
            ev(plan, v, lambda e, col=col: e.tensor_reduce(
                mx8[:, col:col + 1].rearrange("p (a o) -> p a o", a=1),
                ptB[0:8, 0:128].rearrange("p (a t) -> p a t", a=1), AX.X, ALU.max),
               reads=[ptB], writes=[mx8])
        ev(plan, v, lambda e: e.tensor_mul(mx8[:, 2:3], mx8[:, 0:1], mx8[:, 1:2]),
           reads=[mx8], writes=[mx8])
        ev(plan, a, lambda e: e.activation(mx8[:, 3:4], mx8[:, 2:3], AF.Sqrt), reads=[mx8], writes=[mx8])
        ev(plan, v, lambda e: e.tensor_scalar(csb[:], mx8[0:8, 3:4], -SCALE, None, ALU.mult),
           reads=[mx8], writes=[csb])
        for h in range(8):
            mm(plan, PEb[:, 0:1], [(oh8[:, h * 128:(h + 1) * 128], csb[:, 0:1])], tag=f"bb{h}")
            ev(plan, a, lambda e, h=h: e.copy(bias_col[:, h:h + 1], PEb[:, 0:1]),
               reads=[PEb], writes=[bias_col])
            ev(plan, a, lambda e, h=h: e.activation(
                sink_e[:, h:h + 1], snk[:, h:h + 1], AF.Exp, bias=bias_col[0:1, h:h + 1], scale=1.0),
               reads=[snk, bias_col], writes=[sink_e])

        for qb in range(NQ):
            q0 = qb * 512
            for h in range(8):
                hp = h // 2
                p0 = (h % 2) * 64
                kbs = [(q0 - d) // 128 for d in DL if 0 <= (q0 - d) // 128 < NB and (q0 - d) % 128 == 0]
                kbs = sorted(set(kbs))
                first = True
                for i, kb in enumerate(kbs):
                    d = q0 - kb * 128
                    PS = PA if i % 2 == 0 else PB
                    pt_t = ptA if i % 2 == 0 else ptB
                    mm(plan, PS[:], [(k_fm3[p0:p0 + 64, hp, kb * 128:(kb + 1) * 128],
                                      q_fm3[p0:p0 + 64, hp, q0:q0 + 512])], tag=f"s{qb}.{h}.{kb}")
                    ev(plan, a, lambda e, PS=PS, pt_t=pt_t, h=h: e.activation(
                        pt_t[:], PS[:], AF.Exp, bias=bias_col[:, h:h + 1], scale=SCALE),
                       reads=[PS, bias_col], writes=[pt_t], tag=f"e{qb}.{h}.{kb}")
                    mi = DL.index(d)
                    ev(plan, v, lambda e, pt_t=pt_t, mi=mi: e.tensor_mul(pt_t[:], pt_t[:], msk3[:, mi, :]),
                       reads=[pt_t, msk], writes=[pt_t], tag=f"m{qb}.{h}.{kb}")
                    mm(plan, PC[0:65, :], [(v_tm4[:, kb, h, 0:65], pt_t[:])],
                       start=first, stop=(i == len(kbs) - 1), tag=f"pv{qb}.{h}.{kb}")
                    first = False
                ev(plan, v, lambda e, h=h: e.tensor_scalar(
                    den[:], PC[64:65, :], sink_e[0:1, h:h + 1], 1e-30, ALU.add, ALU.max),
                   reads=[PC, sink_e], writes=[den], tag=f"dn{qb}.{h}")
                ev(plan, v, lambda e: e.reciprocal(den[:], den[:]), reads=[den], writes=[den])
                ev(plan, v, lambda e: e.tensor_copy(denb[:], den[:]), reads=[den], writes=[denb])
                mm(plan, PEb[0:64, :], [(o1[:], denb[:])], tag=f"rb{qb}.{h}")
                ev(plan, a, lambda e: e.copy(u2[0:64, :], PEb[0:64, :]), reads=[PEb], writes=[u2])
                ev(plan, v, lambda e, h=h, hp=hp, p0=p0: e.tensor_tensor(
                    ao3[p0:p0 + 64, hp, :], PC[0:64, :], u2[0:64, :], ALU.mult),
                   reads=[PC, u2], writes=[ao], tag=f"ao{qb}.{h}")
            for oc in range(8):
                PP = PF if oc % 2 == 0 else PH
                osb = qraw if oc % 2 == 0 else u1
                mm(plan, PP[:], [(wp3[:, ic, oc * 128:(oc + 1) * 128], ao3[:, ic, :]) for ic in range(4)],
                   tag=f"pj{qb}.{oc}")
                ev(plan, a, lambda e, PP=PP, osb=osb: e.copy(osb[:], PP[:]), reads=[PP], writes=[osb])
                plan.dma(g, out_dram3[oc, :, q0:q0 + 512], osb[:], tag=f"st{qb}.{oc}")

        if dbg:
            def dump(dst, src, tagn):
                n = src.shape[-1] if len(src.shape) == 2 else None
                plan.dma(g, dst, src, tag=tagn)
            dump(d_dq, q_fm[:], "dq")
            dump(d_dk, k_fm[:], "dk")
            dump(d_dv, v_tm[:], "dv")
            plan.dma(g, d_dm, mx8[:], tag="dmx")
            plan.dma(g, d_db, bias_col[:], tag="dbc")
            dump(d_dao, ao[:], "dao")
            plan.barrier_outputs([d_out, d_dq, d_dk, d_dv, d_dm, d_db, d_dao])
        else:
            plan.barrier_outputs([d_out])
        plan.build(es)
    return nc


def prep_a0(inputs):
    qkv_w = np.ascontiguousarray(inputs["L0_attn_qkv_w"])
    proj_w = np.ascontiguousarray(inputs["L0_attn_proj_w"])
    x = np.ascontiguousarray(inputs["L0_attn_x"])
    qn_w = inputs["L0_attn_qn_w"]; qn_b = inputs["L0_attn_qn_b"]
    kn_w = inputs["L0_attn_kn_w"]; kn_b = inputs["L0_attn_kn_b"]
    sinks = inputs["L0_attn_sinks"]
    T = SEQ
    j = np.arange(32)
    f = 1.0 / (10000.0 ** (2 * j / 64.0))
    ang = np.arange(T)[:, None] * f[None, :]
    cos = np.concatenate([np.cos(ang), np.cos(ang)], 1).astype(np.float32)
    sin = np.concatenate([-np.sin(ang), np.sin(ang)], 1).astype(np.float32)
    DL = [-384, -256, -128, 0, 128, 256]
    km = np.arange(128)[:, None]
    qm = np.arange(512)[None, :]
    msk = np.zeros((128, 6 * 512), np.float32)
    for m, d in enumerate(DL):
        r = km - qm - d
        msk[:, m * 512:(m + 1) * 512] = ((r <= 0) & (r > -256)).astype(np.float32)
    qn4 = np.zeros((128, 2048), np.float32)
    qn4[:, 0:512] = np.tile(qn_w, 8)[None, :]
    qn4[:, 512:1024] = np.tile(qn_b, 8)[None, :]
    qn4[:, 1024:1536] = np.tile(kn_w, 8)[None, :]
    qn4[:, 1536:2048] = np.tile(kn_b, 8)[None, :]
    oh = np.zeros((8, 8 * 128), np.float32)
    for h in range(8):
        oh[h, h * 128:(h + 1) * 128] = 1.0
    o1 = np.ones((1, 64), np.float32)
    idm = np.eye(128, dtype=np.float32)
    in_maps = []
    for c in range(N_CORES):
        b = c // 2
        hg = c % 2
        sl = slice(hg * 512, (hg + 1) * 512)
        in_maps.append({
            "x": np.ascontiguousarray(x[b]),
            "wq": np.ascontiguousarray(qkv_w[0 * 1024:1 * 1024][sl].T),
            "wk": np.ascontiguousarray(qkv_w[1 * 1024:2 * 1024][sl].T),
            "wv": np.ascontiguousarray(qkv_w[2 * 1024:3 * 1024][sl].T),
            "wp": np.ascontiguousarray(proj_w[:, sl].T),
            "cos": cos, "sin": sin, "qn": qn4, "msk": msk,
            "sink": np.ascontiguousarray(sinks[hg * 8:(hg + 1) * 8][None, :]),
            "idm": idm, "oh": oh, "o1": o1,
            "cst": np.broadcast_to(np.array([1e-5, 0, 0, 0, 0, 0, 0, 0], np.float32), (128, 8)).copy(),
        })
    return in_maps
def build_attn(kind):
    nc = bass.Bass("TRN2", target_bir_lowering=False, debug=False)
    T = SEQ
    NB = T // 128
    NQ = T // 512
    SCALE = 1.0 / 8.0
    causal = kind in ("l1a", "l2")
    has_rope = kind in ("l1a", "l1b")
    has_bias = kind == "l1b"

    d_x = nc.dram_tensor("x", [T, DM], F32, kind="ExternalInput").ap()
    d_wq = nc.dram_tensor("wq", [DM, 512], F32, kind="ExternalInput").ap()
    d_wk = nc.dram_tensor("wk", [DM, 512], F32, kind="ExternalInput").ap()
    d_wv = nc.dram_tensor("wv", [DM, 512], F32, kind="ExternalInput").ap()
    d_wp = nc.dram_tensor("wp", [512, DM], F32, kind="ExternalInput").ap()
    d_id = nc.dram_tensor("idm", [128, 128], F32, kind="ExternalInput").ap()
    d_oh = nc.dram_tensor("oh", [8, 8 * 128], F32, kind="ExternalInput").ap()
    d_o1 = nc.dram_tensor("o1", [1, 64], F32, kind="ExternalInput").ap()
    d_cst = nc.dram_tensor("cst", [128, 8], F32, kind="ExternalInput").ap()
    d_o2c = nc.dram_tensor("o2c", [128, 2], F32, kind="ExternalInput").ap()
    if has_rope:
        d_wqs = nc.dram_tensor("wqs", [DM, 512], F32, kind="ExternalInput").ap()
        d_wks = nc.dram_tensor("wks", [DM, 512], F32, kind="ExternalInput").ap()
        d_cfm = nc.dram_tensor("cfm", [128, T], F32, kind="ExternalInput").ap()
        d_sfm = nc.dram_tensor("sfm", [128, T], F32, kind="ExternalInput").ap()
    if has_bias:
        d_bq = nc.dram_tensor("bq", [1, 512], F32, kind="ExternalInput").ap()
        d_bqs = nc.dram_tensor("bqs", [1, 512], F32, kind="ExternalInput").ap()
        d_bk = nc.dram_tensor("bk", [1, 512], F32, kind="ExternalInput").ap()
        d_bks = nc.dram_tensor("bks", [1, 512], F32, kind="ExternalInput").ap()
        d_bv = nc.dram_tensor("bv", [1, 512], F32, kind="ExternalInput").ap()
    if causal:
        d_msk = nc.dram_tensor("msk", [128, 4 * 512], F32, kind="ExternalInput").ap()
    if kind == "l2":
        d_wqp = nc.dram_tensor("wqp", [DM, 512], F32, kind="ExternalInput").ap()
        d_wkp = nc.dram_tensor("wkp", [DM, 512], F32, kind="ExternalInput").ap()
        d_lam = nc.dram_tensor("lam", [1, 8], F32, kind="ExternalInput").ap()
    if kind == "l3":
        d_w1 = nc.dram_tensor("w1", [DM, 2048], F32, kind="ExternalInput").ap()
        d_w2 = nc.dram_tensor("w2", [2048, 512], F32, kind="ExternalInput").ap()
        d_b1c = nc.dram_tensor("b1c", [128, 16], F32, kind="ExternalInput").ap()
        d_w1c0 = nc.dram_tensor("w1c0", [128, 16], F32, kind="ExternalInput").ap()
        d_b2c = nc.dram_tensor("b2c", [128, 4], F32, kind="ExternalInput").ap()
        d_etac = nc.dram_tensor("etac", [128, 4], F32, kind="ExternalInput").ap()
        d_sgnc = nc.dram_tensor("sgnc", [128, 4], F32, kind="ExternalInput").ap()
    d_out = nc.dram_tensor("out", [2 * DM if kind == "l2" else DM, T], F32,
                           kind="ExternalOutput").ap()

    es = ExitStack()
    sb = lambda n, s, dt=F32: es.enter_context(nc.sbuf_tensor(n, s, dt))
    ps = lambda n, s, dt=F32: es.enter_context(nc.psum_tensor(n, s, dt))
    with es:
        xt = sb("s_xt", [128, 2 * 1024])
        xn_tm = sb("s_xn_tm", [128, 4 * 1024], BF16)
        xn_fm = sb("s_xn_fm", [128, 8 * 512], BF16)
        q_fm = sb("s_q_fm", [128, 2 * 512] if kind == "l2" else [128, 4 * T], BF16)
        k_fm = sb("s_k_fm", [128, 2 * T] if kind == "l2" else [128, 4 * T], BF16)
        v_tm = sb("s_v_tm", [128, NB * 8 * 66], BF16)
        ao = sb("s_ao", [128, 4 * 512], BF16)
        wq = sb("s_wq", [128, 8 * 512], BF16)
        wk = sb("s_wk", [128, 8 * 512], BF16)
        wv = sb("s_wv", [128, 8 * 512], BF16)
        wp = sb("s_wp", [128, 4 * 1024], BF16)
        idm = sb("s_idm", [128, 128], BF16)
        oh8 = sb("s_oh8", [8, 8 * 128], BF16)
        o1 = sb("s_o1", [1, 64], BF16)
        cst = sb("s_cst", [128, 8])
        qraw = sb("s_qraw", [128, 512])
        sq = sb("s_sq", [128, 512], BF16)
        u1 = sb("s_u1", [128, 512])
        u2 = sb("s_u2", [128, 512])
        nq8 = sb("s_nq8", [128, 8])
        runq = sb("s_runq", [128, 8]); runk = sb("s_runk", [128, 8])
        runb = sb("s_runb", [128, 8], BF16)
        mx8 = sb("s_mx8", [8, 8])
        csb = sb("s_csb", [8, 2], BF16)
        bias_col = sb("s_bias_col", [128, 8])
        ptA = sb("s_ptA", [128, 512], BF16)
        ptB = sb("s_ptB", [128, 512], BF16)
        den = sb("s_den", [1, 512])
        denb = sb("s_denb", [1, 512], BF16)
        st6 = sb("s_st6", [128, 12]); st2 = sb("s_st2", [128, 2])
        rstd1 = sb("s_rstd1", [128, 1]); b2c_ = sb("s_b2c", [128, 1])
        stats = dict(s6=st6, s2=st2, rstd=rstd1, b2=b2c_, eps=cst[:, 0:1])
        onesrow = sb("s_onesrow", [1, 512], BF16)
        o2c = sb("s_o2c", [128, 2], BF16)
        if has_rope:
            wqs = sb("s_wqs", [128, 8 * 512], BF16)
            wks = sb("s_wks", [128, 8 * 512], BF16)
            cfm = sb("s_cfm", [128, 512], BF16)
            sfm = sb("s_sfm", [128, 512], BF16)
        if has_bias:
            bq = sb("s_bq", [1, 512], BF16); bqs = sb("s_bqs", [1, 512], BF16)
            bk = sb("s_bk", [1, 512], BF16); bks = sb("s_bks", [1, 512], BF16)
            bv = sb("s_bv", [1, 512], BF16)
        if causal:
            msk = sb("s_msk", [128, 4 * 512], BF16)
        if kind == "l2":
            wqp = sb("s_wqp", [128, 8 * 512], BF16)
            wkp = sb("s_wkp", [128, 8 * 512], BF16)
            lamr = sb("s_lamr", [1, 8])
            lam_t = sb("s_lam_t", [128, 512], BF16)
            kc_fm = sb("s_kc_fm", [128, 2 * T], BF16)
            ks_fm = sb("s_ks_fm", [128, 2 * T], BF16)
            qc_fm = sb("s_qc_fm", [128, 2 * 512], BF16)
            qs_fm = sb("s_qs_fm", [128, 2 * 512], BF16)
            u3 = sb("s_u3", [128, 512])
            u4 = sb("s_u4", [128, 512])
        if kind == "l3":
            w1 = sb("s_w1", [128, 2 * 8 * 128], BF16)
            w2 = sb("s_w2", [128, 2 * 512], BF16)
            b1c = sb("s_b1c", [128, 16]); w1c0 = sb("s_w1c0", [128, 16])
            b2cc = sb("s_b2cc", [128, 4]); etac = sb("s_etac", [128, 4])
            sgnc = sb("s_sgnc", [128, 4])
            h1_fm = sb("s_h1_fm", [128, 2 * 512], BF16)
            g_fm = sb("s_g_fm", [128, 4 * 512], BF16)
            vm_tm = sb("s_vm_tm", [128, 4 * 1024], BF16)
            vm_fm = sb("s_vm_fm", [128, 8 * 512], BF16)

        PA = ps("PA", [128, 512]); PB = ps("PB", [128, 512])
        PC = ps("PC", [128, 512])
        PEb = ps("PEb", [128, 512])
        PF = ps("PF", [128, 512]); PH = ps("PH", [128, 512])
        PG1 = ps("PG1", [128, 1024], BF16)
        PG2 = ps("PG2", [128, 1024], BF16)

        plan = Plan(nc)
        g, s, v, a = "gpsimd", "sync", "vector", "scalar"

        def ld(dst, src, q=g):
            plan.dma(q, dst, src)

        ld(wq[:].rearrange("p (c o) -> p c o", c=8), d_wq.rearrange("(c p) o -> p c o", p=128))
        ld(wk[:].rearrange("p (c o) -> p c o", c=8), d_wk.rearrange("(c p) o -> p c o", p=128))
        ld(wv[:].rearrange("p (c o) -> p c o", c=8), d_wv.rearrange("(c p) o -> p c o", p=128))
        ld(wp[:].rearrange("p (c o) -> p c o", c=4), d_wp.rearrange("(c p) o -> p c o", p=128))
        ld(idm[:], d_id)
        ld(oh8[:], d_oh)
        ld(o1[:], d_o1)
        ld(cst[:], d_cst, q=s)
        ld(o2c[:], d_o2c)
        if has_rope:
            ld(wqs[:].rearrange("p (c o) -> p c o", c=8), d_wqs.rearrange("(c p) o -> p c o", p=128))
            ld(wks[:].rearrange("p (c o) -> p c o", c=8), d_wks.rearrange("(c p) o -> p c o", p=128))
        if has_bias:
            ld(bq[:], d_bq); ld(bqs[:], d_bqs); ld(bk[:], d_bk); ld(bks[:], d_bks); ld(bv[:], d_bv)
        if causal:
            ld(msk[:], d_msk)
        if kind == "l2":
            ld(wqp[:].rearrange("p (c o) -> p c o", c=8), d_wqp.rearrange("(c p) o -> p c o", p=128))
            ld(wkp[:].rearrange("p (c o) -> p c o", c=8), d_wkp.rearrange("(c p) o -> p c o", p=128))
            ld(lamr[:], d_lam, q=s)
        if kind == "l3":
            ld(b1c[:], d_b1c, q=s); ld(w1c0[:], d_w1c0, q=s)
            ld(b2cc[:], d_b2c, q=s); ld(etac[:], d_etac, q=s); ld(sgnc[:], d_sgnc, q=s)

        ev(plan, v, lambda e: e.memset(
            v_tm[:].rearrange("p (b h f) -> p b h f", b=NB, h=8)[:, :, :, 64:66], 1.0),
            writes=[v_tm], tag="vones")
        ev(plan, v, lambda e: e.memset(runq[:], 0.0), writes=[runq])
        ev(plan, v, lambda e: e.memset(runk[:], 0.0), writes=[runk])
        ev(plan, v, lambda e: e.memset(onesrow[:], 1.0), writes=[onesrow])

        nhp = 2 if kind == "l2" else 4
        q_fm3 = q_fm[:].rearrange("p (hp t) -> p hp t", hp=nhp)
        k_fm3 = k_fm[:].rearrange("p (hp t) -> p hp t", hp=nhp)
        v_tm4 = v_tm[:].rearrange("p (b h f) -> p b h f", b=NB, h=8)
        xt3 = xt[:].rearrange("p (b f) -> p b f", b=2)
        xn_tm3 = xn_tm[:].rearrange("p (b f) -> p b f", b=4)
        xn_fm3 = xn_fm[:].rearrange("p (c t) -> p c t", c=8)
        wq3 = wq[:].rearrange("p (c o) -> p c o", c=8)
        wk3 = wk[:].rearrange("p (c o) -> p c o", c=8)
        wv3 = wv[:].rearrange("p (c o) -> p c o", c=8)
        wp3 = wp[:].rearrange("p (c o) -> p c o", c=4)
        x_dram3 = d_x.rearrange("(b p) f -> p b f", p=128)
        out_dram3 = d_out.rearrange("(c p) t -> c p t", p=128)
        ao2 = ao[:].rearrange("p (c t) -> p c t", c=4)[:, 0:2, :]
        ao3 = ao[:].rearrange("p (c t) -> p c t", c=4)
        if has_rope:
            wqs3 = wqs[:].rearrange("p (c o) -> p c o", c=8)
            wks3 = wks[:].rearrange("p (c o) -> p c o", c=8)
        if kind == "l2":
            wqp3 = wqp[:].rearrange("p (c o) -> p c o", c=8)
            wkp3 = wkp[:].rearrange("p (c o) -> p c o", c=8)
            kc_fm3 = kc_fm[:].rearrange("p (hp t) -> p hp t", hp=2)
            ks_fm3 = ks_fm[:].rearrange("p (hp t) -> p hp t", hp=2)
            qc_fm3 = qc_fm[:].rearrange("p (hp t) -> p hp t", hp=2)
            qs_fm3 = qs_fm[:].rearrange("p (hp t) -> p hp t", hp=2)
        if kind == "l3":
            w14 = w1[:].rearrange("p (b c o) -> p b c o", b=2, c=8)
            w23 = w2[:].rearrange("p (b o) -> p b o", b=2)
            d_w1_4 = d_w1.rearrange("(c p) o -> p c o", p=128)
            d_w2_3 = d_w2.rearrange("(c p) o -> p c o", p=128)
            h1_fm3 = h1_fm[:].rearrange("p (c o) -> p c o", c=2)
            g_fm3 = g_fm[:].rearrange("p (c o) -> p c o", c=4)
            vm_tm3 = vm_tm[:].rearrange("p (b f) -> p b f", b=4)
            vm_fm3 = vm_fm[:].rearrange("p (c t) -> p c t", c=8)

        def bc64(ap2):
            return ap2.rearrange("p (h o) -> p h o", h=8).broadcast_to([128, 8, 64])

        def ln_and_transpose(ch, dst_fm3, src_sel="xn"):
            for half in range(2):
                plan.dma(s, xt3[:, :, :], x_dram3[:, 4 * ch + 2 * half:4 * ch + 2 * half + 2, :],
                         tag=f"ldx{ch}.{half}")
                for jj in range(2):
                    j = 2 * half + jj
                    blk = 4 * ch + j
                    ln_plain_tile(plan, None, xn_tm3[:, j, :], xt3[:, jj, :], stats, tag=f"ln{blk}")
            for j in range(4):
                blk = 4 * ch + j
                src3 = xn_tm3 if src_sel == "xn" else vm_tm3
                for cc in range(8):
                    PG = PG1 if cc % 2 == 0 else PG2
                    transpose_step(plan, PG[:, 0:128], src3[:, j, cc * 128:(cc + 1) * 128], idm[:],
                                   tag=f"xtr{blk}.{cc}")
                    ev(plan, a, lambda e, cc=cc, j=j, PG=PG, dst_fm3=dst_fm3: e.copy(
                        dst_fm3[:, cc, j * 128:(j + 1) * 128], PG[:, 0:128]),
                       reads=[PG], writes=[dst_fm3], tag=f"xtre{blk}.{cc}")

        def logmap(ch):
            for j in range(4):
                blk = 4 * ch + j
                xnb = xn_tm3[:, j, :]
                ev(plan, v, lambda e, xnb=xnb: e.tensor_scalar(u1[:, 0:512], xnb[:, 0:512], -8.0, 8.0, ALU.max, ALU.min),
                   reads=[xn_tm], writes=[u1])
                ev(plan, v, lambda e, xnb=xnb: e.tensor_scalar(u2[:, 0:512], xnb[:, 512:1024], -8.0, 8.0, ALU.max, ALU.min),
                   reads=[xn_tm], writes=[u2])
                ev(plan, a, lambda e: e.activation(sq[:], u1[:], AF.Square, accum_out=nq8[:, 0:1]),
                   reads=[u1], writes=[sq, nq8])
                ev(plan, a, lambda e: e.activation(sq[:], u2[:], AF.Square, accum_out=nq8[:, 1:2]),
                   reads=[u2], writes=[sq, nq8])
                ev(plan, v, lambda e: e.tensor_mul(nq8[:, 3:4], u1[:, 0:1], u1[:, 0:1]),
                   reads=[u1], writes=[nq8])
                ev(plan, v, lambda e: e.tensor_add(nq8[:, 2:3], nq8[:, 0:1], nq8[:, 1:2]),
                   reads=[nq8], writes=[nq8])
                ev(plan, v, lambda e: e.tensor_sub(nq8[:, 2:3], nq8[:, 2:3], nq8[:, 3:4]),
                   reads=[nq8], writes=[nq8])
                ev(plan, a, lambda e: e.activation(nq8[:, 4:5], nq8[:, 2:3], AF.Sqrt, bias=cst[:, 1:2]),
                   reads=[nq8, cst], writes=[nq8])
                ev(plan, v, lambda e: e.tensor_scalar(nq8[:, 5:6], nq8[:, 4:5], float(1 + 1e-6), None, ALU.max),
                   reads=[nq8], writes=[nq8])
                ev(plan, v, lambda e: e.scalar_tensor_tensor(runb[:, 0:1], nq8[:, 5:6], 1.0, nq8[:, 5:6], ALU.mult, ALU.mult),
                   reads=[nq8], writes=[runb])
                ev(plan, a, lambda e: e.activation(qraw[:, 0:1], runb[:, 0:1], AF.Sqrt, bias=cst[:, 2:3]),
                   reads=[runb, cst], writes=[qraw])
                ev(plan, v, lambda e: e.tensor_add(qraw[:, 1:2], qraw[:, 0:1], nq8[:, 5:6]),
                   reads=[qraw, nq8], writes=[qraw])
                ev(plan, a, lambda e: e.activation(qraw[:, 2:3], qraw[:, 1:2], AF.Ln),
                   reads=[qraw], writes=[qraw])
                ev(plan, v, lambda e: e.tensor_scalar(qraw[:, 3:4], qraw[:, 2:3], 1e-3, None, ALU.max),
                   reads=[qraw], writes=[qraw])
                ev(plan, v, lambda e: e.tensor_sub(qraw[:, 4:5], nq8[:, 4:5], nq8[:, 5:6]),
                   reads=[nq8], writes=[qraw])
                ev(plan, v, lambda e: e.tensor_mul(qraw[:, 5:6], qraw[:, 4:5], qraw[:, 4:5]),
                   reads=[qraw], writes=[qraw])
                ev(plan, v, lambda e: e.tensor_sub(qraw[:, 6:7], nq8[:, 2:3], qraw[:, 5:6]),
                   reads=[nq8, qraw], writes=[qraw])
                ev(plan, v, lambda e: e.tensor_scalar(qraw[:, 6:7], qraw[:, 6:7], 1e-6, None, ALU.max),
                   reads=[qraw], writes=[qraw])
                ev(plan, a, lambda e: e.activation(qraw[:, 7:8], qraw[:, 6:7], AF.Sqrt),
                   reads=[qraw], writes=[qraw])
                ev(plan, v, lambda e: e.tensor_scalar_add(qraw[:, 7:8], qraw[:, 7:8], 1e-6),
                   reads=[qraw], writes=[qraw])
                ev(plan, v, lambda e: e.reciprocal(qraw[:, 8:9], qraw[:, 7:8]),
                   reads=[qraw], writes=[qraw])
                ev(plan, v, lambda e: e.tensor_mul(qraw[:, 9:10], qraw[:, 8:9], qraw[:, 3:4]),
                   reads=[qraw], writes=[qraw])
                ev(plan, v, lambda e, j=j: e.tensor_scalar_mul(vm_tm3[:, j, 0:512], u1[:, 0:512], qraw[:, 9:10]),
                   reads=[u1, qraw], writes=[vm_tm])
                ev(plan, v, lambda e, j=j: e.tensor_scalar_mul(vm_tm3[:, j, 512:1024], u2[:, 0:512], qraw[:, 9:10]),
                   reads=[u2, qraw], writes=[vm_tm])
                ev(plan, v, lambda e, j=j: e.tensor_mul(vm_tm3[:, j, 0:1], qraw[:, 4:5], qraw[:, 9:10]),
                   reads=[qraw], writes=[vm_tm])

        if kind in ("l1a", "l1b"):
            for ch in range(NQ):
                ln_and_transpose(ch, xn_fm3)
                tb = ch
                plan.dma(g, cfm[:], d_cfm[:, tb * 512:(tb + 1) * 512], tag=f"ldc{ch}")
                plan.dma(g, sfm[:], d_sfm[:, tb * 512:(tb + 1) * 512], tag=f"lds{ch}")
                for j in range(4):
                    blk = 4 * ch + j
                    pairs = [(xn_fm3[:, ic, j * 128:(j + 1) * 128], wv3[:, ic, :]) for ic in range(8)]
                    if has_bias:
                        pairs.append((onesrow[0:1, 0:128], bv[:]))
                    mm(plan, PA[:], pairs, tag=f"v{blk}")
                    ev(plan, v, lambda e, blk=blk: e.tensor_copy(
                        v_tm4[:, blk, :, 0:64], PA[:].rearrange("p (h d) -> p h d", h=8)),
                       reads=[PA], writes=[v_tm], tag=f"ve{blk}")
                for hp in range(4):
                    for (wmat, wsm, bias2, dst3, run) in (
                            (wq3, wqs3, ("bq", "bqs"), q_fm3, runq),
                            (wk3, wks3, ("bk", "bks"), k_fm3, runk)):
                        pairs = [(wmat[:, ic, hp * 128:(hp + 1) * 128], xn_fm3[:, ic, :]) for ic in range(8)]
                        pairs_s = [(wsm[:, ic, hp * 128:(hp + 1) * 128], xn_fm3[:, ic, :]) for ic in range(8)]
                        if has_bias:
                            bmap = dict(bq=bq, bqs=bqs, bk=bk, bks=bks)
                            brow = bmap[bias2[0]]; bsrow = bmap[bias2[1]]
                            pairs.append((brow[:, hp * 128:(hp + 1) * 128], onesrow[:]))
                            pairs_s.append((bsrow[:, hp * 128:(hp + 1) * 128], onesrow[:]))
                        mm(plan, PA[:], pairs, tag=f"qf{ch}.{hp}")
                        mm(plan, PB[:], pairs_s, tag=f"qfs{ch}.{hp}")
                        cslice = cfm[:]
                        sslice = sfm[:]
                        ev(plan, v, lambda e, cslice=cslice: e.tensor_tensor(u1[:], PA[:], cslice, ALU.mult),
                           reads=[PA, cfm], writes=[u1])
                        ev(plan, v, lambda e, sslice=sslice: e.tensor_tensor(u2[:], PB[:], sslice, ALU.mult),
                           reads=[PB, sfm], writes=[u2])
                        ev(plan, v, lambda e, dst3=dst3, hp=hp, tb=tb: e.tensor_tensor(
                            dst3[:, hp, tb * 512:(tb + 1) * 512], u1[:], u2[:], ALU.add),
                           reads=[u1, u2], writes=[dst3], tag=f"rp{ch}.{hp}")
                        ev(plan, a, lambda e, dst3=dst3, hp=hp, tb=tb: e.activation(
                            sq[:], dst3[:, hp, tb * 512:(tb + 1) * 512], AF.Square),
                           reads=[dst3], writes=[sq])
                        mm(plan, PEb[0:2, :], [(o2c[:], sq[:])], tag=f"nm{ch}.{hp}")
                        ev(plan, v, lambda e: e.reduce_max(nq8[0:2, 0:1], PEb[0:2, :], axis=AX.X),
                           reads=[PEb], writes=[nq8])
                        ev(plan, v, lambda e, run=run, hp=hp: e.tensor_max(
                            run[0:2, hp:hp + 1], run[0:2, hp:hp + 1], nq8[0:2, 0:1]),
                           reads=[run, nq8], writes=[run], tag=f"nmx{ch}.{hp}")
        elif kind == "l3":
            ev(plan, v, lambda e: e.tensor_sub(b1c[:], b1c[:], w1c0[:]), reads=[b1c, w1c0], writes=[b1c])
            for ch in range(NQ):
                ln_and_transpose(ch, xn_fm3)
                logmap(ch)
                for j in range(4):
                    blk = 4 * ch + j
                    for cc in range(8):
                        PG = PG1 if cc % 2 == 0 else PG2
                        transpose_step(plan, PG[:, 128:256], vm_tm3[:, j, cc * 128:(cc + 1) * 128], idm[:],
                                       tag=f"vtr{blk}.{cc}")
                        ev(plan, a, lambda e, cc=cc, j=j, PG=PG: e.copy(
                            vm_fm3[:, cc, j * 128:(j + 1) * 128], PG[:, 128:256]),
                           reads=[PG], writes=[vm_fm], tag=f"vtre{blk}.{cc}")
                tb = ch
                GP = [PC, PEb, PF, PH]
                for ic16 in range(16):
                    P = PA if ic16 % 2 == 0 else PB
                    hb = h1_fm3[:, ic16 % 2, :]
                    wb = ic16 % 2
                    plan.dma(g, w14[:, wb, :, :], d_w1_4[:, :, ic16 * 128:(ic16 + 1) * 128],
                             tag=f"lw1{ch}.{ic16}")
                    plan.dma(g, w23[:, wb, :], d_w2_3[:, ic16, :], tag=f"lw2{ch}.{ic16}")
                    mm(plan, P[:], [(w14[:, wb, ic, :], xn_fm3[:, ic, :]) for ic in range(8)],
                       tag=f"h1{ch}.{ic16}")
                    ev(plan, a, lambda e, ic16=ic16, P=P, hb=hb: e.activation(
                        hb, P[:], AF.Tanh, bias=b1c[:, ic16:ic16 + 1]),
                       reads=[P, b1c], writes=[h1_fm], tag=f"h1e{ch}.{ic16}")
                    for oc in range(4):
                        mm(plan, GP[oc][:], [(w23[:, wb, oc * 128:(oc + 1) * 128], hb)],
                           start=(ic16 == 0), stop=(ic16 == 15), tag=f"g{ch}.{ic16}.{oc}")
                for oc in range(4):
                    ev(plan, a, lambda e, oc=oc: e.activation(
                        u1[:], GP[oc][:], AF.Tanh, bias=b2cc[:, oc:oc + 1]),
                       reads=[GP[oc], b2cc], writes=[u1], tag=f"ge{ch}.{oc}")
                    ev(plan, v, lambda e, oc=oc: e.tensor_scalar(
                        u1[:], u1[:], 0.1, etac[:, oc:oc + 1], ALU.mult, ALU.add),
                       reads=[u1, etac], writes=[u1])
                    ev(plan, a, lambda e, oc=oc: e.activation(u1[:], u1[:], AF.Abs),
                       reads=[u1], writes=[u1])
                    ev(plan, v, lambda e, oc=oc: e.tensor_scalar_mul(
                        g_fm3[:, oc, :], u1[:], sgnc[:, oc:oc + 1]),
                       reads=[u1, sgnc], writes=[g_fm])
                for j in range(4):
                    blk = 4 * ch + j
                    mm(plan, PC[:], [(vm_fm3[:, ic, j * 128:(j + 1) * 128], wv3[:, ic, :]) for ic in range(8)],
                       tag=f"v{blk}")
                    ev(plan, v, lambda e, blk=blk: e.tensor_copy(
                        v_tm4[:, blk, :, 0:64], PC[:].rearrange("p (h d) -> p h d", h=8)),
                       reads=[PC], writes=[v_tm], tag=f"ve{blk}")
                for hp in range(4):
                    mm(plan, PA[:], [(wq3[:, ic, hp * 128:(hp + 1) * 128], vm_fm3[:, ic, :]) for ic in range(8)],
                       tag=f"q{ch}.{hp}")
                    ev(plan, v, lambda e, hp=hp, tb=tb: e.tensor_tensor(
                        q_fm3[:, hp, tb * 512:(tb + 1) * 512], PA[:], g_fm3[:, hp, :], ALU.mult),
                       reads=[PA, g_fm], writes=[q_fm], tag=f"qe{ch}.{hp}")
                    mm(plan, PB[:], [(wk3[:, ic, hp * 128:(hp + 1) * 128], vm_fm3[:, ic, :]) for ic in range(8)],
                       tag=f"k{ch}.{hp}")
                    ev(plan, a, lambda e, hp=hp, tb=tb: e.copy(
                        k_fm3[:, hp, tb * 512:(tb + 1) * 512], PB[:]),
                       reads=[PB], writes=[k_fm], tag=f"ke{ch}.{hp}")
                    for dst3, run in ((q_fm3, runq), (k_fm3, runk)):
                        ev(plan, a, lambda e, dst3=dst3, hp=hp, tb=tb: e.activation(
                            sq[:], dst3[:, hp, tb * 512:(tb + 1) * 512], AF.Square),
                           reads=[dst3], writes=[sq])
                        mm(plan, PEb[0:2, :], [(o2c[:], sq[:])], tag=f"nm{ch}.{hp}")
                        ev(plan, v, lambda e: e.reduce_max(nq8[0:2, 0:1], PEb[0:2, :], axis=AX.X),
                           reads=[PEb], writes=[nq8])
                        ev(plan, v, lambda e, run=run, hp=hp: e.tensor_max(
                            run[0:2, hp:hp + 1], run[0:2, hp:hp + 1], nq8[0:2, 0:1]),
                           reads=[run, nq8], writes=[run])
        elif kind == "l2":
            ev(plan, a, lambda e: e.activation(den[0:1, 0:8], lamr[:], AF.Tanh),
               reads=[lamr], writes=[den])
            ev(plan, v, lambda e: e.tensor_copy(
                denb[0:1, 0:512].rearrange("p (h d) -> p h d", h=8),
                den[0:1, 0:8].rearrange("p (h o) -> p h o", h=8).broadcast_to([1, 8, 64])),
               reads=[den], writes=[denb])
            mm(plan, PEb[:], [(onesrow[0:1, 0:128], denb[0:1, 0:512])], tag="lamb")
            ev(plan, v, lambda e: e.tensor_scalar(lam_t[:], PEb[:], 0.5, None, ALU.mult),
               reads=[PEb], writes=[lam_t])

            def phase_block(blk, Praw, Pph, hg2, with_lam):
                W_ = slice(0, 256)
                ev(plan, a, lambda e, P=Praw: e.copy(qraw[:, W_], P[:, W_]), reads=[Praw], writes=[qraw])
                ev(plan, a, lambda e: e.activation(u1[:, W_], qraw[:, W_], AF.Abs),
                   reads=[qraw], writes=[u1])
                ev(plan, a, lambda e: e.activation(u1[:, W_], u1[:, W_], AF.Exp, scale=-1.0),
                   reads=[u1], writes=[u1])
                ev(plan, a, lambda e: e.activation(u1[:, W_], u1[:, W_], AF.Ln, bias=cst[:, 5:6]),
                   reads=[u1, cst], writes=[u1])
                ev(plan, v, lambda e: e.tensor_scalar(u2[:, W_], qraw[:, W_], 0.0, None, ALU.max),
                   reads=[qraw], writes=[u2])
                ev(plan, v, lambda e: e.tensor_add(u1[:, W_], u1[:, W_], u2[:, W_]), reads=[u1, u2], writes=[u1])
                ev(plan, a, lambda e: e.activation(sq[:, W_], u1[:, W_], AF.Square), reads=[u1], writes=[sq])
                ev(plan, v, lambda e: e.tensor_reduce(
                    nq8[:, 0:4].rearrange("p (h o) -> p h o", h=4),
                    sq[:, W_].rearrange("p (h d) -> p h d", h=4), AX.X, ALU.add),
                   reads=[sq], writes=[nq8])
                ev(plan, a, lambda e: e.activation(nq8[:, 0:4], nq8[:, 0:4], AF.Sqrt, bias=cst[:, 6:7], scale=1.0 / 64),
                   reads=[nq8, cst], writes=[nq8])
                ev(plan, v, lambda e: e.reciprocal(nq8[:, 0:4], nq8[:, 0:4]), reads=[nq8], writes=[nq8])
                ev(plan, v, lambda e: e.tensor_tensor(
                    u1[:, W_].rearrange("p (h d) -> p h d", h=4),
                    u1[:, W_].rearrange("p (h d) -> p h d", h=4),
                    nq8[:, 0:4].rearrange("p (h o) -> p h o", h=4).broadcast_to([128, 4, 64]), ALU.mult),
                   reads=[u1, nq8], writes=[u1])
                ev(plan, a, lambda e, P=Pph: e.activation(u2[:, W_], P[:, W_], AF.Tanh), reads=[Pph], writes=[u2])
                ev(plan, a, lambda e: e.activation(u3[:, W_], u2[:, W_], AF.Sin, scale=float(np.pi)),
                   reads=[u2], writes=[u3])
                ev(plan, a, lambda e: e.activation(u4[:, W_], u2[:, W_], AF.Sin, scale=float(np.pi / 2)),
                   reads=[u2], writes=[u4])
                ev(plan, a, lambda e: e.activation(u4[:, W_], u4[:, W_], AF.Square), reads=[u4], writes=[u4])
                ev(plan, v, lambda e: e.tensor_scalar(u4[:, W_], u4[:, W_], -2.0, 1.0, ALU.mult, ALU.add),
                   reads=[u4], writes=[u4])
                ev(plan, v, lambda e: e.tensor_mul(u4[:, W_], u4[:, W_], u1[:, W_]), reads=[u4, u1], writes=[u4])
                ev(plan, v, lambda e: e.tensor_mul(u3[:, W_], u3[:, W_], u1[:, W_]), reads=[u3, u1], writes=[u3])
                if with_lam:
                    lsl = lam_t[:, hg2 * 256:(hg2 + 1) * 256]
                    ev(plan, v, lambda e, lsl=lsl: e.tensor_mul(u4[:, W_], u4[:, W_], lsl), reads=[u4, lam_t], writes=[u4])
                    ev(plan, v, lambda e, lsl=lsl: e.tensor_mul(u3[:, W_], u3[:, W_], lsl), reads=[u3, lam_t], writes=[u3])
                return

            qss = sb("s_qss", [128, 8])

            def l2_norm_acc(run, nh4):
                ev(plan, a, lambda e: e.activation(sq[:, 0:256], qraw[:, 0:256], AF.Square),
                   reads=[qraw], writes=[sq])
                ev(plan, v, lambda e: e.tensor_reduce(
                    qss[:, 0:4].rearrange("p (h o) -> p h o", h=4),
                    sq[:, 0:256].rearrange("p (h d) -> p h d", h=4), AX.X, ALU.add),
                   reads=[sq], writes=[qss])
                for t_ in (u4, u3):
                    ev(plan, a, lambda e, t_=t_: e.activation(sq[:, 0:256], t_[:, 0:256], AF.Square),
                       reads=[t_], writes=[sq])
                    ev(plan, v, lambda e: e.tensor_reduce(
                        nq8[:, 0:4].rearrange("p (h o) -> p h o", h=4),
                        sq[:, 0:256].rearrange("p (h d) -> p h d", h=4), AX.X, ALU.add),
                       reads=[sq], writes=[nq8])
                    ev(plan, v, lambda e: e.tensor_add(qss[:, 0:4], qss[:, 0:4], nq8[:, 0:4]),
                       reads=[qss, nq8], writes=[qss])
                ev(plan, v, lambda e, run=run: e.tensor_max(run[:, 0:4], run[:, 0:4], qss[:, 0:4]),
                   reads=[run, qss], writes=[run])

            def l2_kside(hg2):
                osl = slice(hg2 * 256, (hg2 + 1) * 256)
                for ch in range(NQ):
                    ln_and_transpose(ch, xn_fm3)
                    for j in range(4):
                        blk = 4 * ch + j
                        if hg2 == 0:
                            mm(plan, PC[:], [(xn_fm3[:, ic, j * 128:(j + 1) * 128], wv3[:, ic, :]) for ic in range(8)],
                               tag=f"v{blk}")
                            ev(plan, v, lambda e, blk=blk: e.tensor_copy(
                                v_tm4[:, blk, :, 0:64], PC[:].rearrange("p (h d) -> p h d", h=8)),
                               reads=[PC], writes=[v_tm], tag=f"ve{blk}")
                        mm(plan, PA[:, 0:256], [(xn_fm3[:, ic, j * 128:(j + 1) * 128], wk3[:, ic, osl]) for ic in range(8)],
                           tag=f"k{blk}")
                        mm(plan, PB[:, 0:256], [(xn_fm3[:, ic, j * 128:(j + 1) * 128], wkp3[:, ic, osl]) for ic in range(8)],
                           tag=f"kp{blk}")
                        phase_block(blk, PA, PB, hg2, True)
                        l2_norm_acc(runk, 4)
                        for src, dst3, tg in ((qraw, k_fm3, "tk"), (u4, kc_fm3, "tkc"), (u3, ks_fm3, "tks")):
                            ev(plan, v, lambda e, src=src: e.tensor_copy(ptA[:, 0:256], src[:, 0:256]),
                               reads=[src], writes=[ptA])
                            for hp in range(2):
                                PG = PG1 if hp % 2 == 0 else PG2
                                transpose_step(plan, PG[:, 128:256], ptA[:, hp * 128:(hp + 1) * 128], idm[:],
                                               tag=f"{tg}{blk}.{hp}")
                                ev(plan, a, lambda e, dst3=dst3, hp=hp, blk=blk, PG=PG: e.copy(
                                    dst3[:, hp, blk * 128:(blk + 1) * 128], PG[:, 128:256]),
                                   reads=[PG], writes=[dst3], tag=f"{tg}e{blk}.{hp}")

        if kind in ("l0",):
            pass
        if kind in ("l1a", "l1b", "l3"):
            ev(plan, v, lambda e: e.memset(runb[:], 0.0), writes=[runb])
            for run, col in ((runq, 0), (runk, 1)):
                ev(plan, v, lambda e, run=run: e.tensor_copy(runb[0:2, 0:4], run[0:2, 0:4]),
                   reads=[run], writes=[runb])
                transpose_step(plan, PG1[0:4, 0:128], runb[:, 0:4], idm[:], tag=f"ntr{col}")
                ev(plan, a, lambda e, col=col: e.copy(mx8[0:4, col * 2:col * 2 + 2], PG1[0:4, 0:2]),
                   reads=[PG1], writes=[mx8])
            ev(plan, v, lambda e: e.tensor_mul(mx8[0:4, 4:6], mx8[0:4, 0:2], mx8[0:4, 2:4]),
               reads=[mx8], writes=[mx8])
            ev(plan, a, lambda e: e.activation(mx8[0:4, 6:8], mx8[0:4, 4:6], AF.Sqrt),
               reads=[mx8], writes=[mx8])
            ev(plan, v, lambda e: e.tensor_scalar(csb[0:4, 0:2], mx8[0:4, 6:8], -SCALE, None, ALU.mult),
               reads=[mx8], writes=[csb])
        def emit_bias_cols(hg2=None):
            if hg2 is None:
                for h in range(8):
                    hp = h // 2; par = h % 2
                    mm(plan, PEb[:, 0:1], [(oh8[0:4, hp * 128:(hp + 1) * 128], csb[0:4, par:par + 1])],
                       tag=f"bb{h}")
                    ev(plan, a, lambda e, h=h: e.copy(bias_col[:, h:h + 1], PEb[:, 0:1]),
                       reads=[PEb], writes=[bias_col])
            else:
                for hl in range(4):
                    h = hg2 * 4 + hl
                    mm(plan, PEb[:, 0:1], [(oh8[0:4, hl * 128:(hl + 1) * 128], csb[0:4, 0:1])],
                       tag=f"bb{h}")
                    ev(plan, a, lambda e, h=h: e.copy(bias_col[:, h:h + 1], PEb[:, 0:1]),
                       reads=[PEb], writes=[bias_col])
        if kind != "l2":
            emit_bias_cols()

        msk3 = msk[:].rearrange("p (m t) -> p m t", m=4) if causal else None

        def epilogue_head(h, p0, hpl, ao_v):
            ev(plan, v, lambda e, h=h: e.tensor_scalar(
                den[:], PC[64:65, :], 1e-30, None, ALU.max),
               reads=[PC], writes=[den], tag=f"dn{h}")
            ev(plan, v, lambda e: e.reciprocal(den[:], den[:]), reads=[den], writes=[den])
            ev(plan, v, lambda e: e.tensor_copy(denb[:], den[:]), reads=[den], writes=[denb])
            mm(plan, PEb[0:64, :], [(o1[:], denb[:])], tag=f"rb{h}")
            ev(plan, a, lambda e: e.copy(u2[0:64, :], PEb[0:64, :]), reads=[PEb], writes=[u2])
            ev(plan, v, lambda e, hpl=hpl, p0=p0, ao_v=ao_v: e.tensor_tensor(
                ao_v[p0:p0 + 64, hpl, :], PC[0:64, :], u2[0:64, :], ALU.mult),
               reads=[PC, u2], writes=[ao], tag=f"ao{h}")

        if kind == "l2":
            for hg2 in range(2):
                osl = slice(hg2 * 256, (hg2 + 1) * 256)
                ev(plan, v, lambda e: e.memset(runk[:], 0.0), writes=[runk])
                l2_kside(hg2)
                ev(plan, v, lambda e: e.memset(runb[:], 0.0), writes=[runb])
                ev(plan, v, lambda e: e.tensor_copy(runb[:, 0:4], runk[:, 0:4]), reads=[runk], writes=[runb])
                transpose_step(plan, PG1[0:4, 0:128], runb[:, 0:4], idm[:], tag=f"ntrk{hg2}")
                ev(plan, a, lambda e: e.copy(ptB[0:4, 0:128], PG1[0:4, 0:128]), reads=[PG1], writes=[ptB])
                ev(plan, v, lambda e: e.tensor_reduce(
                    mx8[0:4, 1:2].rearrange("p (a o) -> p a o", a=1),
                    ptB[0:4, 0:128].rearrange("p (a t) -> p a t", a=1), AX.X, ALU.max),
                   reads=[ptB], writes=[mx8])
                for qb in range(NQ):
                    q0 = qb * 512
                    ev(plan, v, lambda e: e.memset(runq[:], 0.0), writes=[runq])
                    ln_and_transpose(qb, xn_fm3)
                    for j in range(4):
                        blk = 4 * qb + j
                        mm(plan, PA[:, 0:256], [(xn_fm3[:, ic, j * 128:(j + 1) * 128], wq3[:, ic, osl]) for ic in range(8)],
                           tag=f"q{blk}")
                        mm(plan, PB[:, 0:256], [(xn_fm3[:, ic, j * 128:(j + 1) * 128], wqp3[:, ic, osl]) for ic in range(8)],
                           tag=f"qp{blk}")
                        phase_block(blk, PA, PB, hg2, False)
                        l2_norm_acc(runq, 4)
                        for src, dst3, tg in ((qraw, q_fm3, "tq"), (u4, qc_fm3, "tqc"), (u3, qs_fm3, "tqs")):
                            ev(plan, v, lambda e, src=src: e.tensor_copy(ptA[:, 0:256], src[:, 0:256]),
                               reads=[src], writes=[ptA])
                            for hp in range(2):
                                PG = PG1 if hp % 2 == 0 else PG2
                                transpose_step(plan, PG[:, 128:256], ptA[:, hp * 128:(hp + 1) * 128], idm[:],
                                               tag=f"{tg}{blk}.{hp}")
                                ev(plan, a, lambda e, dst3=dst3, hp=hp, j=j, PG=PG: e.copy(
                                    dst3[:, hp, j * 128:(j + 1) * 128], PG[:, 128:256]),
                                   reads=[PG], writes=[dst3], tag=f"{tg}e{blk}.{hp}")
                    ev(plan, v, lambda e: e.tensor_copy(runb[:, 0:4], runq[:, 0:4]), reads=[runq], writes=[runb])
                    transpose_step(plan, PG1[0:4, 0:128], runb[:, 0:4], idm[:], tag=f"ntrq{hg2}.{qb}")
                    ev(plan, a, lambda e: e.copy(ptB[0:4, 0:128], PG1[0:4, 0:128]), reads=[PG1], writes=[ptB])
                    ev(plan, v, lambda e: e.tensor_reduce(
                        mx8[0:4, 0:1].rearrange("p (a o) -> p a o", a=1),
                        ptB[0:4, 0:128].rearrange("p (a t) -> p a t", a=1), AX.X, ALU.max),
                       reads=[ptB], writes=[mx8])
                    ev(plan, v, lambda e: e.tensor_mul(mx8[0:4, 2:3], mx8[0:4, 0:1], mx8[0:4, 1:2]),
                       reads=[mx8], writes=[mx8])
                    ev(plan, a, lambda e: e.activation(mx8[0:4, 3:4], mx8[0:4, 2:3], AF.Sqrt),
                       reads=[mx8], writes=[mx8])
                    ev(plan, v, lambda e: e.tensor_scalar(csb[0:4, 0:1], mx8[0:4, 3:4], -SCALE, None, ALU.mult),
                       reads=[mx8], writes=[csb])
                    emit_bias_cols(hg2)
                    for hl in range(4):
                        h = hg2 * 4 + hl
                        hpl = hl // 2
                        p0 = (hl % 2) * 64
                        kbs = list(range(0, 4 * qb + 4))
                        PSL = [PA, PB, PF, PH]
                        PTL = [ptA[:], ptB[:],
                               qraw[:].bitcast(BF16)[:, 0:512], u1[:].bitcast(BF16)[:, 0:512]]
                        PTH = [ptA, ptB, qraw, u1]
                        pend = []
                        for i, kb in enumerate(kbs):
                            PS = PSL[i % 4]
                            pt_t = PTL[i % 4]
                            pt_h = PTH[i % 4]
                            pairs = [(k_fm3[p0:p0 + 64, hpl, kb * 128:(kb + 1) * 128], q_fm3[p0:p0 + 64, hpl, :]),
                                     (kc_fm3[p0:p0 + 64, hpl, kb * 128:(kb + 1) * 128], qc_fm3[p0:p0 + 64, hpl, :]),
                                     (ks_fm3[p0:p0 + 64, hpl, kb * 128:(kb + 1) * 128], qs_fm3[p0:p0 + 64, hpl, :])]
                            mm(plan, PS[:], pairs, tag=f"s{qb}.{h}.{kb}")
                            ev(plan, a, lambda e, PS=PS, pt_t=pt_t, h=h: e.activation(
                                pt_t, PS[:], AF.Exp, bias=bias_col[:, h:h + 1], scale=SCALE),
                               reads=[PS, bias_col], writes=[pt_h], tag=f"e{qb}.{h}.{kb}")
                            if kb >= 4 * qb:
                                mi = kb - 4 * qb
                                ev(plan, v, lambda e, pt_t=pt_t, mi=mi: e.tensor_mul(pt_t, pt_t, msk3[:, mi, :]),
                                   reads=[pt_h, msk], writes=[pt_h], tag=f"m{qb}.{h}.{kb}")
                            pend.append((i, kb, pt_t, pt_h))
                            if len(pend) >= 2:
                                i2, kb2, pt2, ph2 = pend.pop(0)
                                mm(plan, PC[0:65, :], [(v_tm4[:, kb2, h, 0:65], pt2)],
                                   start=(i2 == 0), stop=(i2 == len(kbs) - 1), tag=f"pv{qb}.{h}.{kb2}",
                                   extra_reads=[ph2])
                        for i2, kb2, pt2, ph2 in pend:
                            mm(plan, PC[0:65, :], [(v_tm4[:, kb2, h, 0:65], pt2)],
                               start=(i2 == 0), stop=(i2 == len(kbs) - 1), tag=f"pv{qb}.{h}.{kb2}",
                               extra_reads=[ph2])
                        epilogue_head(h, p0, hpl, ao2)
                    for oc in range(8):
                        PP = PF if oc % 2 == 0 else PH
                        osb = qraw if oc % 2 == 0 else u1
                        mm(plan, PP[:], [(wp3[:, 2 * hg2 + icl, oc * 128:(oc + 1) * 128], ao2[:, icl, :])
                                         for icl in range(2)],
                           tag=f"pj{hg2}.{qb}.{oc}")
                        ev(plan, a, lambda e, PP=PP, osb=osb: e.copy(osb[:], PP[:]), reads=[PP], writes=[osb])
                        plan.dma(g, out_dram3[hg2 * 8 + oc, :, q0:q0 + 512], osb[:], tag=f"st{hg2}.{qb}.{oc}")
        else:
            for qb in range(NQ):
                q0 = qb * 512
                for h in range(8):
                    hp = h // 2
                    p0 = (h % 2) * 64
                    if kind == "l0":
                        kbs = []
                    elif causal:
                        kbs = list(range(0, 4 * qb + 4))
                    else:
                        kbs = list(range(NB))
                    PSL = [PA, PB, PF, PH]
                    PTL = [ptA[:], ptB[:],
                           qraw[:].bitcast(BF16)[:, 0:512], u1[:].bitcast(BF16)[:, 0:512]]
                    PTH = [ptA, ptB, qraw, u1]
                    pend = []
                    for i, kb in enumerate(kbs):
                        PS = PSL[i % 4]
                        pt_t = PTL[i % 4]
                        pt_h = PTH[i % 4]
                        pairs = [(k_fm3[p0:p0 + 64, hp, kb * 128:(kb + 1) * 128],
                                  q_fm3[p0:p0 + 64, hp, q0:q0 + 512])]
                        mm(plan, PS[:], pairs, tag=f"s{qb}.{h}.{kb}")
                        ev(plan, a, lambda e, PS=PS, pt_t=pt_t, h=h: e.activation(
                            pt_t, PS[:], AF.Exp, bias=bias_col[:, h:h + 1], scale=SCALE),
                           reads=[PS, bias_col], writes=[pt_h], tag=f"e{qb}.{h}.{kb}")
                        if causal and kb >= 4 * qb:
                            mi = kb - 4 * qb
                            ev(plan, v, lambda e, pt_t=pt_t, mi=mi: e.tensor_mul(pt_t, pt_t, msk3[:, mi, :]),
                               reads=[pt_h, msk], writes=[pt_h], tag=f"m{qb}.{h}.{kb}")
                        pend.append((i, kb, pt_t, pt_h))
                        if len(pend) >= 2:
                            i2, kb2, pt2, ph2 = pend.pop(0)
                            mm(plan, PC[0:65, :], [(v_tm4[:, kb2, h, 0:65], pt2)],
                               start=(i2 == 0), stop=(i2 == len(kbs) - 1), tag=f"pv{qb}.{h}.{kb2}",
                               extra_reads=[ph2])
                    for i2, kb2, pt2, ph2 in pend:
                        mm(plan, PC[0:65, :], [(v_tm4[:, kb2, h, 0:65], pt2)],
                           start=(i2 == 0), stop=(i2 == len(kbs) - 1), tag=f"pv{qb}.{h}.{kb2}",
                           extra_reads=[ph2])
                    epilogue_head(h, p0, hp, ao3)
                for oc in range(8):
                    PP = PF if oc % 2 == 0 else PH
                    osb = qraw if oc % 2 == 0 else u1
                    mm(plan, PP[:], [(wp3[:, ic, oc * 128:(oc + 1) * 128], ao3[:, ic, :]) for ic in range(4)],
                       tag=f"pj{qb}.{oc}")
                    ev(plan, a, lambda e, PP=PP, osb=osb: e.copy(osb[:], PP[:]), reads=[PP], writes=[osb])
                    plan.dma(g, out_dram3[oc, :, q0:q0 + 512], osb[:], tag=f"st{qb}.{oc}")

        plan.barrier_outputs([d_out])
        plan.build(es)
    return nc




def _rope_tables_fm():
    T = SEQ
    j = np.arange(32)
    f = 1.0 / (10000.0 ** (2 * j / 64.0))
    ang = np.arange(T)[None, :] * f[:, None]
    c = np.cos(ang); sn = np.sin(ang)
    C = np.concatenate([c, c], 0)
    S = np.concatenate([-sn, sn], 0)
    return (np.concatenate([C, C], 0).astype(np.float32),
            np.concatenate([S, S], 0).astype(np.float32))


def _swap_cols(w):
    o = w.reshape(w.shape[:-1] + (-1, 2, 32))
    o = o[..., ::-1, :]
    return np.ascontiguousarray(o.reshape(w.shape))


def _causal_masks():
    km = np.arange(128)[:, None]
    qm = np.arange(512)[None, :]
    m = np.zeros((128, 4 * 512), np.float32)
    for mi in range(4):
        m[:, mi * 512:(mi + 1) * 512] = ((km + 128 * mi) <= qm).astype(np.float32)
    return m


_CONSTS = {}


def _common_consts():
    if not _CONSTS:
        oh = np.zeros((8, 8 * 128), np.float32)
        for h in range(8):
            oh[h, h * 128:(h + 1) * 128] = 1.0
        cst = np.zeros((128, 8), np.float32)
        cst[:, 0] = 1e-5
        cst[:, 1] = 1.0 + 1e-6
        cst[:, 2] = -1.0
        cst[:, 5] = 1.0
        cst[:, 6] = 1e-6
        o2c = np.zeros((128, 2), np.float32)
        o2c[0:64, 0] = 1.0
        o2c[64:128, 1] = 1.0
        _CONSTS.update(dict(
            idm=np.eye(128, dtype=np.float32), oh=oh,
            o1=np.ones((1, 64), np.float32), cst=cst, o2c=o2c,
            msk=_causal_masks()))
    return _CONSTS


def prep_attn(kind, inputs):
    cc = _common_consts()
    C, S = _rope_tables_fm()
    in_maps = []
    for c in range(N_CORES):
        b = c // 2
        hg = c % 2
        sl = slice(hg * 512, (hg + 1) * 512)
        m = {"idm": cc["idm"], "oh": cc["oh"], "o1": cc["o1"], "cst": cc["cst"],
             "o2c": cc["o2c"]}
        if kind == "l1a":
            W = np.ascontiguousarray(inputs["L1_attn_W_qkv"])
            perm = np.concatenate([np.arange(0, 64, 2), np.arange(1, 64, 2)])
            def permcols(w):
                o = w.reshape(1024, 8, 64)[:, :, perm]
                return np.ascontiguousarray(o.reshape(1024, 512))
            wq = permcols(W[:, 0 * 1024 + hg * 512:0 * 1024 + (hg + 1) * 512])
            wk = permcols(W[:, 1 * 1024 + hg * 512:1 * 1024 + (hg + 1) * 512])
            m.update({
                "x": np.ascontiguousarray(inputs["L1_attn_x_staged"][b]),
                "wq": wq, "wqs": _swap_cols(wq),
                "wk": wk, "wks": _swap_cols(wk),
                "wv": np.ascontiguousarray(W[:, 2 * 1024 + hg * 512:2 * 1024 + (hg + 1) * 512]),
                "wp": np.ascontiguousarray(inputs["L1_attn_W_out"][sl, :]),
                "cfm": C, "sfm": S, "msk": cc["msk"],
            })
        elif kind == "l1b":
            m.update({
                "x": np.ascontiguousarray(inputs["L1_ffn_x_staged"][b]),
                "wq": np.ascontiguousarray(inputs["L1_ffn_Wq"][:, sl]),
                "wqs": _swap_cols(np.ascontiguousarray(inputs["L1_ffn_Wq"][:, sl])),
                "wk": np.ascontiguousarray(inputs["L1_ffn_Wk"][:, sl]),
                "wks": _swap_cols(np.ascontiguousarray(inputs["L1_ffn_Wk"][:, sl])),
                "wv": np.ascontiguousarray(inputs["L1_ffn_Wv"][:, sl]),
                "wp": np.ascontiguousarray(inputs["L1_ffn_Wo"][sl, :]),
                "bq": inputs["L1_ffn_bq"][sl][None, :],
                "bqs": _swap_cols(inputs["L1_ffn_bq"][sl][None, :]),
                "bk": inputs["L1_ffn_bk"][sl][None, :],
                "bks": _swap_cols(inputs["L1_ffn_bk"][sl][None, :]),
                "bv": inputs["L1_ffn_bv"][sl][None, :],
                "cfm": C, "sfm": S,
            })
        elif kind == "l2":
            m.update({
                "x": np.ascontiguousarray(inputs["L2_attn_x_staged"][b]),
                "wq": np.ascontiguousarray(inputs["L2_attn_Wq"][:, sl]),
                "wk": np.ascontiguousarray(inputs["L2_attn_Wk"][:, sl]),
                "wv": np.ascontiguousarray(inputs["L2_attn_Wv"][:, sl]),
                "wp": np.ascontiguousarray(inputs["L2_attn_Wo"][sl, :]),
                "wqp": np.ascontiguousarray(inputs["L2_attn_Wqp"][:, sl]),
                "wkp": np.ascontiguousarray(inputs["L2_attn_Wkp"][:, sl]),
                "lam": np.ascontiguousarray(inputs["L2_attn_lambda_raw"][hg * 8:(hg + 1) * 8][None, :]),
                "msk": cc["msk"],
            })
        elif kind == "l3":
            W1 = np.ascontiguousarray(inputs["L3_attn_W1"])
            W2 = np.ascontiguousarray(inputs["L3_attn_W2"])
            osl = slice(hg * 512, (hg + 1) * 512)
            eta = np.ones(1024, np.float32); eta[0] = -1.0
            sgn = np.ones(1024, np.float32); sgn[0] = -1.0
            m.update({
                "x": np.ascontiguousarray(inputs["L3_attn_x_staged"][b]),
                "wq": np.ascontiguousarray(inputs["L3_attn_Wq"][osl, :].T),
                "wk": np.ascontiguousarray(inputs["L3_attn_Wk"][osl, :].T),
                "wv": np.ascontiguousarray(inputs["L3_attn_Wv"][osl, :].T),
                "wp": np.ascontiguousarray(inputs["L3_attn_Wo"].T[osl, :]),
                "w1": np.ascontiguousarray(W1.T),
                "w2": np.ascontiguousarray(W2[osl, :].T),
                "b1c": np.ascontiguousarray(inputs["L3_attn_b1"].reshape(16, 128).T),
                "w1c0": np.ascontiguousarray(W1[:, 0].reshape(16, 128).T),
                "b2c": np.ascontiguousarray(inputs["L3_attn_b2"][osl].reshape(4, 128).T),
                "etac": np.ascontiguousarray(eta[osl].reshape(4, 128).T),
                "sgnc": np.ascontiguousarray(sgn[osl].reshape(4, 128).T),
            })
        in_maps.append(m)
    return in_maps


def build_f0(dbg=False):
    nc = bass.Bass("TRN2", target_bir_lowering=False, debug=False)
    TC = T_CORE // 2
    NB = TC // 128
    NT = TC // 512
    E = 9
    HID = 2048
    NHC = HID // 128

    d_x = nc.dram_tensor("x", [TC, DM], F32, kind="ExternalInput").ap()
    d_a0 = nc.dram_tensor("a0", [TC, DM], F32, kind="ExternalInput").ap()
    d_a1 = nc.dram_tensor("a1", [TC, DM], F32, kind="ExternalInput").ap()
    d_pb = nc.dram_tensor("pb", [128, DM], F32, kind="ExternalInput").ap()
    d_rw = nc.dram_tensor("rw", [DM, 8], F32, kind="ExternalInput").ap()
    d_eb = nc.dram_tensor("eb", [1, 8], F32, kind="ExternalInput").ap()
    d_w1 = nc.dram_tensor("w1", [E * DM, HID], F32, kind="ExternalInput").ap()
    d_w3 = nc.dram_tensor("w3", [E * DM, HID], F32, kind="ExternalInput").ap()
    d_w2 = nc.dram_tensor("w2", [E * HID, DM], F32, kind="ExternalInput").ap()
    d_id = nc.dram_tensor("idm", [128, 128], F32, kind="ExternalInput").ap()
    d_oh = nc.dram_tensor("oh", [8, 8 * 128], F32, kind="ExternalInput").ap()
    d_cst = nc.dram_tensor("cst", [128, 8], F32, kind="ExternalInput").ap()
    d_out = nc.dram_tensor("out", [TC, DM], F32, kind="ExternalOutput").ap()
    if dbg:
        d_dcw = nc.dram_tensor("dcw", [8, 1024], BF16, kind="ExternalOutput").ap()
        d_dacc = nc.dram_tensor("dacc", [128, 8 * 1024], F32, kind="ExternalOutput").ap()
        d_dxn = nc.dram_tensor("dxn", [128, 8 * 1024], BF16, kind="ExternalOutput").ap()
        d_deo = nc.dram_tensor("deo", [128, 16 * 512], BF16, kind="ExternalOutput").ap()

    es = ExitStack()
    sb = lambda n, s, dt=F32: es.enter_context(nc.sbuf_tensor(n, s, dt))
    ps = lambda n, s, dt=F32: es.enter_context(nc.psum_tensor(n, s, dt))
    with es:
        xt = sb("s_xt", [128, 2 * 1024])
        xn_tm = sb("s_xn_tm", [128, 2 * 1024], BF16)
        xn_fm = sb("s_xn_fm", [128, 8 * TC], BF16)
        w1 = sb("s_w1", [128, 8 * 2048], BF16)
        w3 = sb("s_w3", [128, 8 * 2048], BF16)
        w2 = sb("s_w2", [128, 16 * 1024], BF16)
        rw = sb("s_rw", [128, 8 * 8], BF16)
        ebr = sb("s_ebr", [1, 8], BF16)
        cwt = sb("s_cwt", [8, TC], BF16)
        cwb_sb = sb("s_cwb", [128, 512], BF16)
        eo = sb("s_eo", [128, 16 * 512], BF16)
        acc = sb("s_acc", [128, 8 * TC // 512 * 512])
        idm = sb("s_idm", [128, 128], BF16)
        oh8 = sb("s_oh8", [8, 8 * 128], BF16)
        cst = sb("s_cst", [128, 8])
        onesrow = sb("s_onesrow", [1, 512], BF16)
        lg = sb("s_lg", [128, 8])
        lgm = sb("s_lgm", [128, 8])
        m1c = sb("s_m1c", [128, 1]); m2c = sb("s_m2c", [128, 1])
        zrc = sb("s_zrc", [128, 1])
        cw_tm = sb("s_cw_tm", [128, 8], BF16)
        u1 = sb("s_u1", [128, 512])
        u2 = sb("s_u2", [128, 512])
        res = sb("s_res", [128, 1024])
        pbt = sb("s_pbt", [128, 1024])
        st6 = sb("s_st6", [128, 12]); st2 = sb("s_st2", [128, 2])
        rstd1 = sb("s_rstd1", [128, 1]); b2c_ = sb("s_b2c", [128, 1])
        stats = dict(s6=st6, s2=st2, rstd=rstd1, b2=b2c_, eps=cst[:, 0:1])

        PA = ps("PA", [128, 512]); PB = ps("PB", [128, 512])
        PC = ps("PC", [128, 512]); PD = ps("PD", [128, 512])
        PEb = ps("PEb", [128, 512])
        PG1 = ps("PG1", [128, 1024], BF16)
        PG2 = ps("PG2", [128, 1024], BF16)

        plan = Plan(nc)
        g, s, v, a = "gpsimd", "sync", "vector", "scalar"

        plan.dma(g, rw[:].rearrange("p (c o) -> p c o", c=8), d_rw.rearrange("(c p) o -> p c o", p=128))
        plan.dma(g, ebr[:], d_eb)
        plan.dma(g, idm[:], d_id)
        plan.dma(g, oh8[:], d_oh)
        plan.dma(s, cst[:], d_cst)
        plan.dma(s, pbt[:], d_pb)
        ev(plan, v, lambda e: e.memset(onesrow[:], 1.0), writes=[onesrow])
        ev(plan, v, lambda e: e.memset(cwt[:], 1.0), writes=[cwt])

        xt3 = xt[:].rearrange("p (b f) -> p b f", b=2)
        xn_tm3 = xn_tm[:].rearrange("p (b f) -> p b f", b=2)
        xn_fm3 = xn_fm[:].rearrange("p (c t) -> p c t", c=8)
        w13 = w1[:].rearrange("p (c o) -> p c o", c=8)
        w33 = w3[:].rearrange("p (c o) -> p c o", c=8)
        w23 = w2[:].rearrange("p (c o) -> p c o", c=16)
        rw3 = rw[:].rearrange("p (c o) -> p c o", c=8)
        eo3 = eo[:].rearrange("p (c t) -> p c t", c=16)
        acc3 = acc[:].rearrange("p (c t) -> p c t", c=8)
        x_dram3 = d_x.rearrange("(b p) f -> p b f", p=128)
        a0_dram3 = d_a0.rearrange("(b p) f -> p b f", p=128)
        a1_dram3 = d_a1.rearrange("(b p) f -> p b f", p=128)
        out_dram3 = d_out.rearrange("(b p) f -> p b f", p=128)
        d_w1_4 = d_w1.rearrange("(e c p) o -> e p c o", e=E, p=128)
        d_w3_4 = d_w3.rearrange("(e c p) o -> e p c o", e=E, p=128)
        d_w2_4 = d_w2.rearrange("(e c p) o -> e p c o", e=E, p=128)

        for ch in range(NB // 2):
            plan.dma(s, xt3[:, :, :], x_dram3[:, 2 * ch:2 * ch + 2, :], tag=f"ldx{ch}")
            for j in range(2):
                blk = 2 * ch + j
                ln_plain_tile(plan, None, xn_tm3[:, j, :], xt3[:, j, :], stats, tag=f"ln{blk}")
                for cc in range(8):
                    PG = PG1 if cc % 2 == 0 else PG2
                    transpose_step(plan, PG[:, 0:128], xn_tm3[:, j, cc * 128:(cc + 1) * 128], idm[:],
                                   tag=f"xtr{blk}.{cc}")
                    ev(plan, a, lambda e, cc=cc, blk=blk, PG=PG: e.copy(
                        xn_fm3[:, cc, blk * 128:(blk + 1) * 128], PG[:, 0:128]),
                       reads=[PG], writes=[xn_fm], tag=f"xtre{blk}.{cc}")
        for blk in range(NB):
            mm(plan, PA[:, 0:8], [(xn_fm3[:, ic, blk * 128:(blk + 1) * 128], rw3[:, ic, :]) for ic in range(8)]
               + [(onesrow[0:1, 0:128], ebr[:])], tag=f"rt{blk}")
            ev(plan, a, lambda e: e.copy(lg[:], PA[:, 0:8]), reads=[PA], writes=[lg])
            ev(plan, v, lambda e: e.reduce_max(m1c[:], lg[:], axis=AX.X), reads=[lg], writes=[m1c])
            ev(plan, v, lambda e: e.tensor_scalar(lgm[:], lg[:], m1c[:], None, ALU.is_lt),
               reads=[lg, m1c], writes=[lgm])
            ev(plan, v, lambda e: e.tensor_scalar(lgm[:], lgm[:], 1e30, -1e30, ALU.mult, ALU.add),
               reads=[lgm], writes=[lgm])
            ev(plan, v, lambda e: e.tensor_add(lgm[:], lgm[:], lg[:]), reads=[lgm, lg], writes=[lgm])
            ev(plan, v, lambda e: e.reduce_max(m2c[:], lgm[:], axis=AX.X), reads=[lgm], writes=[m2c])
            ev(plan, v, lambda e: e.tensor_scalar(lgm[:], lg[:], m2c[:], None, ALU.is_ge),
               reads=[lg, m2c], writes=[lgm])
            ev(plan, v, lambda e: e.tensor_scalar_mul(m1c[:], m1c[:], -1.0), reads=[m1c], writes=[m1c])
            ev(plan, a, lambda e: e.activation(lg[:], lg[:], AF.Exp, bias=m1c[:]),
               reads=[lg, m1c], writes=[lg])
            ev(plan, v, lambda e: e.tensor_mul(lg[:], lg[:], lgm[:]), reads=[lg, lgm], writes=[lg])
            ev(plan, v, lambda e: e.reduce_sum(zrc[:], lg[:], axis=AX.X), reads=[lg], writes=[zrc])
            ev(plan, v, lambda e: e.reciprocal(zrc[:], zrc[:]), reads=[zrc], writes=[zrc])
            ev(plan, v, lambda e: e.tensor_scalar_mul(cw_tm[:], lg[:], zrc[:]), reads=[lg, zrc], writes=[cw_tm])
            transpose_step(plan, PG1[0:8, 0:128], cw_tm[:, 0:8], idm[:], tag=f"cwtr{blk}")
            ev(plan, a, lambda e, blk=blk: e.copy(cwt[0:8, blk * 128:(blk + 1) * 128], PG1[0:8, 0:128]),
               reads=[PG1], writes=[cwt], tag=f"cwte{blk}")

        for e_ in range(E):
            plan.dma(g, w13[:, :, :], d_w1_4[e_], tag=f"lw1{e_}")
            plan.dma(g, w33[:, :, :], d_w3_4[e_], tag=f"lw3{e_}")
            plan.dma(g, w23[:, :, :], d_w2_4[e_], tag=f"lw2{e_}")
            for tb in range(NT):
                tsl = slice(tb * 512, (tb + 1) * 512)
                if e_ < 8:
                    mm(plan, PEb[:], [(oh8[:, e_ * 128:(e_ + 1) * 128], cwt[0:8, tsl])], tag=f"cb{e_}.{tb}")
                    ev(plan, a, lambda e: e.copy(cwb_sb[:], PEb[:]), reads=[PEb], writes=[cwb_sb])
                for hc in range(NHC):
                    mm(plan, PA[:], [(w13[:, ic, hc * 128:(hc + 1) * 128], xn_fm3[:, ic, tsl]) for ic in range(8)],
                       tag=f"g{e_}.{tb}.{hc}")
                    mm(plan, PB[:], [(w33[:, ic, hc * 128:(hc + 1) * 128], xn_fm3[:, ic, tsl]) for ic in range(8)],
                       tag=f"u{e_}.{tb}.{hc}")
                    ev(plan, a, lambda e: e.activation(u1[:], PA[:], AF.Silu), reads=[PA], writes=[u1])
                    ev(plan, v, lambda e, hc=hc: e.tensor_tensor(eo3[:, hc, :], u1[:], PB[:], ALU.mult),
                       reads=[u1, PB], writes=[eo], tag=f"eo{e_}.{tb}.{hc}")
                for oc in range(8):
                    PY = PC if oc % 2 == 0 else PD
                    mm(plan, PY[:], [(w23[:, hc, oc * 128:(oc + 1) * 128], eo3[:, hc, :]) for hc in range(NHC)],
                       tag=f"y{e_}.{tb}.{oc}")
                    asl = acc3[:, oc, tb * 512:(tb + 1) * 512]
                    if e_ == 0:
                        ev(plan, v, lambda e, PY=PY, asl=asl: e.tensor_tensor(
                            asl, PY[:], cwb_sb[:], ALU.mult),
                           reads=[PY, cwb_sb], writes=[acc], tag=f"ya{e_}.{tb}.{oc}")
                    elif e_ < 8:
                        ev(plan, v, lambda e, PY=PY: e.tensor_tensor(u2[:], PY[:], cwb_sb[:], ALU.mult),
                           reads=[PY, cwb_sb], writes=[u2], tag=f"ys{e_}.{tb}.{oc}")
                        ev(plan, v, lambda e, asl=asl: e.tensor_add(asl, asl, u2[:]),
                           reads=[acc, u2], writes=[acc], tag=f"ya{e_}.{tb}.{oc}")
                    else:
                        ev(plan, v, lambda e, PY=PY, asl=asl: e.tensor_add(asl, asl, PY[:]),
                           reads=[acc, PY], writes=[acc], tag=f"ya{e_}.{tb}.{oc}")
        for blk in range(NB):
            plan.dma(s, xt3[:, 0, :], x_dram3[:, blk, :], tag=f"rx{blk}")
            plan.dma(s, res[:, 0:1024], a0_dram3[:, blk, :], tag=f"ra0{blk}")
            plan.dma(s, xt3[:, 1, :], a1_dram3[:, blk, :], tag=f"ra1{blk}")
            ev(plan, v, lambda e, blk=blk: e.tensor_copy(
                eo[:, 4096:5120].rearrange("p (c t) -> p c t", c=8),
                acc3[:, :, blk * 128:(blk + 1) * 128]),
               reads=[acc], writes=[eo], tag=f"ac{blk}")
            eob = eo[:, 4096:5120].rearrange("p (c t) -> p c t", c=8)
            for cc in range(8):
                PG = PG1 if cc % 2 == 0 else PG2
                transpose_step(plan, PG[:, 128:256], eob[:, cc, :], idm[:], tag=f"atr{blk}.{cc}")
                ev(plan, a, lambda e, cc=cc, PG=PG: e.copy(eo[:, cc * 128:(cc + 1) * 128], PG[:, 128:256]),
                   reads=[PG], writes=[eo], tag=f"atre{blk}.{cc}")
            ev(plan, v, lambda e: e.tensor_add(res[:, 0:1024], res[:, 0:1024], xt3[:, 0, :]),
               reads=[res, xt], writes=[res])
            ev(plan, v, lambda e: e.tensor_add(res[:, 0:1024], res[:, 0:1024], xt3[:, 1, :]),
               reads=[res, xt], writes=[res])
            ev(plan, v, lambda e: e.tensor_add(res[:, 0:1024], res[:, 0:1024], pbt[:]),
               reads=[res, pbt], writes=[res])
            ev(plan, v, lambda e: e.tensor_add(res[:, 0:1024], res[:, 0:1024], eo[:, 0:1024]),
               reads=[res, eo], writes=[res])
            plan.dma(g, out_dram3[:, blk, :], res[:, 0:1024], tag=f"so{blk}")

        if dbg:
            plan.dma(g, d_deo, eo[:], tag="deo")
            plan.dma(g, d_dcw, cwt[0:8, :], tag="dcw")
            plan.dma(g, d_dacc, acc[:], tag="dacc")
            plan.dma(g, d_dxn, xn_fm[:], tag="dxn")
            plan.barrier_outputs([d_out, d_dcw, d_dacc, d_dxn, d_deo])
        else:
            plan.barrier_outputs([d_out])
        plan.build(es)
    return nc


def prep_f0(inputs, x_full, a0_full, a1_full, half):
    cc = _common_consts()
    E = 9
    w1 = np.concatenate([inputs["L0_ffn_w1"], inputs["L0_ffn_sw1"]], 0)
    w3 = np.concatenate([inputs["L0_ffn_w3"], inputs["L0_ffn_sw3"]], 0)
    w2 = np.concatenate([inputs["L0_ffn_w2"], inputs["L0_ffn_sw2"]], 0)
    w1 = np.ascontiguousarray(w1.reshape(E * DM, 2048))
    w3 = np.ascontiguousarray(w3.reshape(E * DM, 2048))
    w2 = np.ascontiguousarray(w2.reshape(E * 2048, DM))
    rw = np.ascontiguousarray(inputs["L0_ffn_router_w"].T)
    pb = np.broadcast_to(inputs["L0_attn_proj_b"], (128, DM)).copy()
    a0f = a0_full.reshape(T_ALL, DM)
    a1f = a1_full.reshape(T_ALL, DM)
    in_maps = []
    TCH = T_CORE // 2
    for c in range(N_CORES):
        tsl = slice(c * T_CORE + half * TCH, c * T_CORE + (half + 1) * TCH)
        in_maps.append({
            "x": np.ascontiguousarray(x_full[tsl]),
            "a0": np.ascontiguousarray(a0f[tsl]),
            "a1": np.ascontiguousarray(a1f[tsl]),
            "pb": pb, "rw": rw,
            "eb": inputs["L0_ffn_experts_bias"][None, :],
            "w1": w1, "w3": w3, "w2": w2,
            "idm": cc["idm"], "oh": cc["oh"], "cst": cc["cst"],
        })
    return in_maps


def build_f1():
    nc = bass.Bass("TRN2", target_bir_lowering=False, debug=False)
    TC = T_CORE // 2
    NB = TC // 128
    NT = TC // 512
    E = 8
    HID = 4096
    NHC = HID // 128

    d_x = nc.dram_tensor("x", [TC, DM], F32, kind="ExternalInput").ap()
    d_aA0 = nc.dram_tensor("aA0", [TC, DM], F32, kind="ExternalInput").ap()
    d_aA1 = nc.dram_tensor("aA1", [TC, DM], F32, kind="ExternalInput").ap()
    d_aB0 = nc.dram_tensor("aB0", [TC, DM], F32, kind="ExternalInput").ap()
    d_aB1 = nc.dram_tensor("aB1", [TC, DM], F32, kind="ExternalInput").ap()
    d_bo = nc.dram_tensor("bo", [128, DM], F32, kind="ExternalInput").ap()
    d_g1 = nc.dram_tensor("g1", [128, DM], F32, kind="ExternalInput").ap()
    d_be1 = nc.dram_tensor("be1", [128, DM], F32, kind="ExternalInput").ap()
    d_g2 = nc.dram_tensor("g2", [128, DM], F32, kind="ExternalInput").ap()
    d_be2 = nc.dram_tensor("be2", [128, DM], F32, kind="ExternalInput").ap()
    d_wg = nc.dram_tensor("wg", [DM, 8], F32, kind="ExternalInput").ap()
    d_bg = nc.dram_tensor("bg", [1, 8], F32, kind="ExternalInput").ap()
    d_w1 = nc.dram_tensor("w1", [E * DM, HID], F32, kind="ExternalInput").ap()
    d_b1 = nc.dram_tensor("b1", [E, HID], F32, kind="ExternalInput").ap()
    d_w2 = nc.dram_tensor("w2", [E * HID, DM], F32, kind="ExternalInput").ap()
    d_b2 = nc.dram_tensor("b2", [E, DM], F32, kind="ExternalInput").ap()
    d_id = nc.dram_tensor("idm", [128, 128], F32, kind="ExternalInput").ap()
    d_oh = nc.dram_tensor("oh", [8, 8 * 128], F32, kind="ExternalInput").ap()
    d_cst = nc.dram_tensor("cst", [128, 8], F32, kind="ExternalInput").ap()
    d_out = nc.dram_tensor("out", [TC, DM], F32, kind="ExternalOutput").ap()

    es = ExitStack()
    sb = lambda n, s, dt=F32: es.enter_context(nc.sbuf_tensor(n, s, dt))
    ps = lambda n, s, dt=F32: es.enter_context(nc.psum_tensor(n, s, dt))
    with es:
        xt = sb("s_xt", [128, 1024])
        x1_tm = sb("s_x1_tm", [128, 8 * 1024], BF16)
        xn_fm = sb("s_xn_fm", [128, 8 * TC], BF16)
        w1 = sb("s_w1", [128, 8 * HID], BF16)
        w2c = sb("s_w2c", [128, 2 * 32 * 128], BF16)
        b1r = sb("s_b1r", [1, HID], BF16)
        b2r = sb("s_b2r", [1, DM], BF16)
        wg = sb("s_wg", [128, 8 * 8], BF16)
        bgr = sb("s_bgr", [1, 8], BF16)
        cwt = sb("s_cwt", [8, TC], BF16)
        cwb_sb = sb("s_cwb", [128, 512], BF16)
        eo = sb("s_eo", [128, 32 * 512], BF16)
        acc = sb("s_acc", [128, 8 * TC], BF16)
        bo_t = sb("s_bo", [128, 1024], BF16)
        g1t = sb("s_g1t", [128, 1024], BF16); be1t = sb("s_be1t", [128, 1024], BF16)
        g2t = sb("s_g2t", [128, 1024], BF16); be2t = sb("s_be2t", [128, 1024], BF16)
        idm = sb("s_idm", [128, 128], BF16)
        oh8 = sb("s_oh8", [8, 8 * 128], BF16)
        cst = sb("s_cst", [128, 8])
        onesrow = sb("s_onesrow", [1, 512], BF16)
        lg = sb("s_lg", [128, 8]); lgm = sb("s_lgm", [128, 8])
        m1c = sb("s_m1c", [128, 1]); m2c = sb("s_m2c", [128, 1])
        zrc = sb("s_zrc", [128, 1])
        cw_tm = sb("s_cw_tm", [128, 8], BF16)
        u1 = sb("s_u1", [128, 1024])
        u2 = sb("s_u2", [128, 512])
        res = sb("s_res", [128, 1024])
        st6 = sb("s_st6", [128, 12]); st2 = sb("s_st2", [128, 2])
        rstd1 = sb("s_rstd1", [128, 1]); b2c_ = sb("s_b2c", [128, 1])
        stats = dict(s6=st6, s2=st2, rstd=rstd1, b2=b2c_, eps=cst[:, 0:1])

        PA = ps("PA", [128, 512]); PB = ps("PB", [128, 512])
        PC = ps("PC", [128, 512]); PD = ps("PD", [128, 512])
        PEb = ps("PEb", [128, 512])
        PG1 = ps("PG1", [128, 1024], BF16)
        PG2 = ps("PG2", [128, 1024], BF16)

        plan = Plan(nc)
        g, s, v, a = "gpsimd", "sync", "vector", "scalar"

        plan.dma(g, wg[:].rearrange("p (c o) -> p c o", c=8), d_wg.rearrange("(c p) o -> p c o", p=128))
        plan.dma(g, bgr[:], d_bg)
        plan.dma(g, idm[:], d_id)
        plan.dma(g, oh8[:], d_oh)
        plan.dma(s, cst[:], d_cst)
        plan.dma(g, bo_t[:], d_bo)
        plan.dma(g, g1t[:], d_g1)
        plan.dma(g, be1t[:], d_be1)
        plan.dma(g, g2t[:], d_g2)
        plan.dma(g, be2t[:], d_be2)
        ev(plan, v, lambda e: e.memset(onesrow[:], 1.0), writes=[onesrow])

        x1_tm3 = x1_tm[:].rearrange("p (b f) -> p b f", b=8)
        xn_fm3 = xn_fm[:].rearrange("p (c t) -> p c t", c=8)
        w13 = w1[:].rearrange("p (c o) -> p c o", c=8)
        w2c3 = w2c[:].rearrange("p (b c o) -> p b c o", b=2, c=32)
        wg3 = wg[:].rearrange("p (c o) -> p c o", c=8)
        eo3 = eo[:].rearrange("p (c t) -> p c t", c=32)
        acc3 = acc[:].rearrange("p (c t) -> p c t", c=8)
        x_dram3 = d_x.rearrange("(b p) f -> p b f", p=128)
        aA0_3 = d_aA0.rearrange("(b p) f -> p b f", p=128)
        aA1_3 = d_aA1.rearrange("(b p) f -> p b f", p=128)
        aB0_3 = d_aB0.rearrange("(b p) f -> p b f", p=128)
        aB1_3 = d_aB1.rearrange("(b p) f -> p b f", p=128)
        out_dram3 = d_out.rearrange("(b p) f -> p b f", p=128)
        d_w1_4 = d_w1.rearrange("(e c p) o -> e p c o", e=E, p=128)
        d_w2_4 = d_w2.rearrange("(e c p) o -> e p c o", e=E, p=128)

        for blk in range(NB):
            plan.dma(s, xt[:], x_dram3[:, blk, :], tag=f"ldx{blk}")
            plan.dma(s, res[:], aB0_3[:, blk, :], tag=f"lb0{blk}")
            ln_plain_tile(plan, None, u1[:], xt[:], stats, tag=f"ln{blk}")
            ev(plan, v, lambda e: e.tensor_add(res[:], res[:], u1[:]), reads=[res, u1], writes=[res])
            plan.dma(s, u1[:], aB1_3[:, blk, :], tag=f"lb1{blk}")
            ev(plan, v, lambda e: e.tensor_add(res[:], res[:], u1[:]), reads=[res, u1], writes=[res])
            ev(plan, v, lambda e: e.tensor_add(res[:], res[:], bo_t[:]), reads=[res, bo_t], writes=[res])
            ln_plain_tile(plan, None, u1[:], res[:], stats, tag=f"lna{blk}")
            ev(plan, v, lambda e: e.tensor_mul(u1[:], u1[:], g1t[:]), reads=[u1, g1t], writes=[u1])
            ev(plan, v, lambda e, blk=blk: e.tensor_add(x1_tm3[:, blk, :], u1[:], be1t[:]),
               reads=[u1, be1t], writes=[x1_tm], tag=f"x1{blk}")
            for cc in range(8):
                PG = PG1 if cc % 2 == 0 else PG2
                transpose_step(plan, PG[:, 0:128], x1_tm3[:, blk, cc * 128:(cc + 1) * 128], idm[:],
                               tag=f"xtr{blk}.{cc}")
                ev(plan, a, lambda e, cc=cc, blk=blk, PG=PG: e.copy(
                    xn_fm3[:, cc, blk * 128:(blk + 1) * 128], PG[:, 0:128]),
                   reads=[PG], writes=[xn_fm], tag=f"xtre{blk}.{cc}")
        for blk in range(NB):
            mm(plan, PA[:, 0:8], [(xn_fm3[:, ic, blk * 128:(blk + 1) * 128], wg3[:, ic, :]) for ic in range(8)]
               + [(onesrow[0:1, 0:128], bgr[:])], tag=f"rt{blk}")
            ev(plan, a, lambda e: e.copy(lg[:], PA[:, 0:8]), reads=[PA], writes=[lg])
            ev(plan, v, lambda e: e.reduce_max(m1c[:], lg[:], axis=AX.X), reads=[lg], writes=[m1c])
            ev(plan, v, lambda e: e.tensor_scalar(lgm[:], lg[:], m1c[:], None, ALU.is_lt),
               reads=[lg, m1c], writes=[lgm])
            ev(plan, v, lambda e: e.tensor_scalar(lgm[:], lgm[:], 1e30, -1e30, ALU.mult, ALU.add),
               reads=[lgm], writes=[lgm])
            ev(plan, v, lambda e: e.tensor_add(lgm[:], lgm[:], lg[:]), reads=[lgm, lg], writes=[lgm])
            ev(plan, v, lambda e: e.reduce_max(m2c[:], lgm[:], axis=AX.X), reads=[lgm], writes=[m2c])
            ev(plan, v, lambda e: e.tensor_scalar(lgm[:], lg[:], m2c[:], None, ALU.is_ge),
               reads=[lg, m2c], writes=[lgm])
            ev(plan, v, lambda e: e.tensor_scalar_mul(m1c[:], m1c[:], -1.0), reads=[m1c], writes=[m1c])
            ev(plan, a, lambda e: e.activation(lg[:], lg[:], AF.Exp, bias=m1c[:]),
               reads=[lg, m1c], writes=[lg])
            ev(plan, v, lambda e: e.tensor_mul(lg[:], lg[:], lgm[:]), reads=[lg, lgm], writes=[lg])
            ev(plan, v, lambda e: e.reduce_sum(zrc[:], lg[:], axis=AX.X), reads=[lg], writes=[zrc])
            ev(plan, v, lambda e: e.reciprocal(zrc[:], zrc[:]), reads=[zrc], writes=[zrc])
            ev(plan, v, lambda e: e.tensor_scalar_mul(cw_tm[:], lg[:], zrc[:]), reads=[lg, zrc], writes=[cw_tm])
            transpose_step(plan, PG1[0:8, 0:128], cw_tm[:, 0:8], idm[:], tag=f"cwtr{blk}")
            ev(plan, a, lambda e, blk=blk: e.copy(cwt[0:8, blk * 128:(blk + 1) * 128], PG1[0:8, 0:128]),
               reads=[PG1], writes=[cwt], tag=f"cwte{blk}")

        for e_ in range(E):
            plan.dma(g, w13[:, :, :], d_w1_4[e_], tag=f"lw1{e_}")
            plan.dma(g, b1r[:], d_b1[e_:e_ + 1, :], tag=f"lb1r{e_}")
            plan.dma(g, b2r[:], d_b2[e_:e_ + 1, :], tag=f"lb2r{e_}")
            for tb in range(NT):
                tsl = slice(tb * 512, (tb + 1) * 512)
                mm(plan, PEb[:], [(oh8[:, e_ * 128:(e_ + 1) * 128], cwt[0:8, tsl])], tag=f"cb{e_}.{tb}")
                ev(plan, a, lambda e: e.copy(cwb_sb[:], PEb[:]), reads=[PEb], writes=[cwb_sb])
                for hc in range(NHC):
                    mm(plan, PA[:], [(w13[:, ic, hc * 128:(hc + 1) * 128], xn_fm3[:, ic, tsl]) for ic in range(8)]
                       + [(b1r[:, hc * 128:(hc + 1) * 128], onesrow[:])], tag=f"g{e_}.{tb}.{hc}")
                    ev(plan, a, lambda e, hc=hc: e.activation(eo3[:, hc, :], PA[:], AF.Gelu),
                       reads=[PA], writes=[eo], tag=f"eo{e_}.{tb}.{hc}")
                for oc in range(8):
                    wb = oc % 2
                    plan.dma(g, w2c3[:, wb, :, :], d_w2_4[e_][:, :, oc * 128:(oc + 1) * 128],
                             tag=f"lw2{e_}.{tb}.{oc}")
                    PY = PC if oc % 2 == 0 else PD
                    mm(plan, PY[:], [(w2c3[:, wb, hc, :], eo3[:, hc, :]) for hc in range(NHC)]
                       + [(b2r[:, oc * 128:(oc + 1) * 128], onesrow[:])], tag=f"y{e_}.{tb}.{oc}")
                    asl = acc3[:, oc, tsl]
                    if e_ == 0:
                        ev(plan, v, lambda e, PY=PY, asl=asl: e.tensor_tensor(
                            asl, PY[:], cwb_sb[:], ALU.mult),
                           reads=[PY, cwb_sb], writes=[acc], tag=f"ya{e_}.{tb}.{oc}")
                    else:
                        ev(plan, v, lambda e, PY=PY: e.tensor_tensor(u2[:], PY[:], cwb_sb[:], ALU.mult),
                           reads=[PY, cwb_sb], writes=[u2], tag=f"ys{e_}.{tb}.{oc}")
                        ev(plan, v, lambda e, asl=asl: e.tensor_add(asl, asl, u2[:]),
                           reads=[acc, u2], writes=[acc], tag=f"ya{e_}.{tb}.{oc}")

        for blk in range(NB):
            ev(plan, v, lambda e, blk=blk: e.tensor_copy(
                eo[:, 4096:5120].rearrange("p (c t) -> p c t", c=8),
                acc3[:, :, blk * 128:(blk + 1) * 128]),
               reads=[acc], writes=[eo], tag=f"ac{blk}")
            eob = eo[:, 4096:5120].rearrange("p (c t) -> p c t", c=8)
            for cc in range(8):
                PG = PG1 if cc % 2 == 0 else PG2
                transpose_step(plan, PG[:, 128:256], eob[:, cc, :], idm[:], tag=f"atr{blk}.{cc}")
                ev(plan, a, lambda e, cc=cc, PG=PG: e.copy(eo[:, cc * 128:(cc + 1) * 128], PG[:, 128:256]),
                   reads=[PG], writes=[eo], tag=f"atre{blk}.{cc}")
            ev(plan, v, lambda e, blk=blk: e.tensor_add(u1[:], x1_tm3[:, blk, :], eo[:, 0:1024]),
               reads=[x1_tm, eo], writes=[u1])
            ln_plain_tile(plan, None, res[:], u1[:], stats, tag=f"lnf{blk}")
            ev(plan, v, lambda e: e.tensor_mul(res[:], res[:], g2t[:]), reads=[res, g2t], writes=[res])
            ev(plan, v, lambda e: e.tensor_add(res[:], res[:], be2t[:]), reads=[res, be2t], writes=[res])
            plan.dma(s, xt[:], x_dram3[:, blk, :], tag=f"rx{blk}")
            ev(plan, v, lambda e: e.tensor_add(res[:], res[:], xt[:]), reads=[res, xt], writes=[res])
            plan.dma(s, xt[:], aA0_3[:, blk, :], tag=f"rA0{blk}")
            ev(plan, v, lambda e: e.tensor_add(res[:], res[:], xt[:]), reads=[res, xt], writes=[res])
            plan.dma(s, xt[:], aA1_3[:, blk, :], tag=f"rA1{blk}")
            ev(plan, v, lambda e: e.tensor_add(res[:], res[:], xt[:]), reads=[res, xt], writes=[res])
            plan.dma(g, out_dram3[:, blk, :], res[:], tag=f"so{blk}")

        plan.barrier_outputs([d_out])
        plan.build(es)
    return nc


def prep_f1(inputs, x_full, aA0, aA1, aB0, aB1, half):
    cc = _common_consts()
    E = 8
    w1 = np.ascontiguousarray(inputs["L1_ffn_W1"].reshape(E * DM, 4096))
    w2 = np.ascontiguousarray(inputs["L1_ffn_W2"].reshape(E * 4096, DM))
    tile = lambda vec: np.broadcast_to(vec, (128, DM)).copy()
    TCH = T_CORE // 2
    in_maps = []
    for c in range(N_CORES):
        tsl = slice(c * T_CORE + half * TCH, c * T_CORE + (half + 1) * TCH)
        in_maps.append({
            "x": np.ascontiguousarray(x_full[tsl]),
            "aA0": np.ascontiguousarray(aA0.reshape(T_ALL, DM)[tsl]),
            "aA1": np.ascontiguousarray(aA1.reshape(T_ALL, DM)[tsl]),
            "aB0": np.ascontiguousarray(aB0.reshape(T_ALL, DM)[tsl]),
            "aB1": np.ascontiguousarray(aB1.reshape(T_ALL, DM)[tsl]),
            "bo": tile(inputs["L1_ffn_bo"]), "g1": tile(inputs["L1_ffn_g1"]),
            "be1": tile(inputs["L1_ffn_be1"]), "g2": tile(inputs["L1_ffn_g2"]),
            "be2": tile(inputs["L1_ffn_be2"]),
            "wg": np.ascontiguousarray(inputs["L1_ffn_Wg"]),
            "bg": inputs["L1_ffn_bg"][None, :],
            "w1": w1, "b1": inputs["L1_ffn_b1"], "w2": w2, "b2": inputs["L1_ffn_b2"],
            "idm": cc["idm"], "oh": cc["oh"], "cst": cc["cst"],
        })
    return in_maps


def build_f2a():
    nc = bass.Bass("TRN2", target_bir_lowering=False, debug=False)
    TC = T_CORE
    NB = TC // 128
    NT = TC // 512
    QD = 2048
    NQC = QD // 128

    d_x = nc.dram_tensor("x", [TC, DM], F32, kind="ExternalInput").ap()
    d_wq = nc.dram_tensor("wq", [DM, QD], F32, kind="ExternalInput").ap()
    d_bq = nc.dram_tensor("bq", [1, QD], F32, kind="ExternalInput").ap()
    d_id = nc.dram_tensor("idm", [128, 128], F32, kind="ExternalInput").ap()
    d_cst = nc.dram_tensor("cst", [128, 8], F32, kind="ExternalInput").ap()
    d_q = nc.dram_tensor("qout", [QD, TC], BF16, kind="ExternalOutput").ap()
    d_st = nc.dram_tensor("stout", [128, 4], F32, kind="ExternalOutput").ap()

    es = ExitStack()
    sb = lambda n, s, dt=F32: es.enter_context(nc.sbuf_tensor(n, s, dt))
    ps = lambda n, s, dt=F32: es.enter_context(nc.psum_tensor(n, s, dt))
    with es:
        xt = sb("s_xt", [128, 2 * 1024])
        xn_tm = sb("s_xn_tm", [128, 2 * 1024], BF16)
        xn_fm = sb("s_xn_fm", [128, 8 * 512], BF16)
        wq = sb("s_wq", [128, 8 * QD], BF16)
        bqr = sb("s_bqr", [1, QD], BF16)
        qc_sb = sb("s_qc", [128, 2 * 512], BF16)
        sq = sb("s_sq", [128, 512], BF16)
        stat = sb("s_stat", [128, 4])
        t1 = sb("s_t1", [128, 1])
        onesrow = sb("s_onesrow", [1, 512], BF16)
        idm = sb("s_idm", [128, 128], BF16)
        cst = sb("s_cst", [128, 8])
        st6 = sb("s_st6", [128, 12]); st2 = sb("s_st2", [128, 2])
        rstd1 = sb("s_rstd1", [128, 1]); b2c_ = sb("s_b2c", [128, 1])
        stats = dict(s6=st6, s2=st2, rstd=rstd1, b2=b2c_, eps=cst[:, 0:1])
        PA = ps("PA", [128, 512]); PB = ps("PB", [128, 512])
        PG1 = ps("PG1", [128, 1024], BF16)
        PG2 = ps("PG2", [128, 1024], BF16)

        plan = Plan(nc)
        g, s, v, a = "gpsimd", "sync", "vector", "scalar"
        plan.dma(g, wq[:].rearrange("p (c o) -> p c o", c=8), d_wq.rearrange("(c p) o -> p c o", p=128))
        plan.dma(g, bqr[:], d_bq)
        plan.dma(g, idm[:], d_id)
        plan.dma(s, cst[:], d_cst)
        ev(plan, v, lambda e: e.memset(onesrow[:], 1.0), writes=[onesrow])
        ev(plan, v, lambda e: e.memset(stat[:], 0.0), writes=[stat])

        xt3 = xt[:].rearrange("p (b f) -> p b f", b=2)
        xn_tm3 = xn_tm[:].rearrange("p (b f) -> p b f", b=2)
        xn_fm3 = xn_fm[:].rearrange("p (c t) -> p c t", c=8)
        wq3 = wq[:].rearrange("p (c o) -> p c o", c=8)
        x_dram3 = d_x.rearrange("(b p) f -> p b f", p=128)
        q_dram3 = d_q.rearrange("(c p) t -> c p t", p=128)

        for ch in range(NT):
            for half in range(2):
                plan.dma(s, xt3[:, :, :], x_dram3[:, 4 * ch + 2 * half:4 * ch + 2 * half + 2, :],
                         tag=f"ldx{ch}.{half}")
                for jj in range(2):
                    j = 2 * half + jj
                    blk = 4 * ch + j
                    ln_plain_tile(plan, None, xn_tm3[:, jj, :], xt3[:, jj, :], stats, tag=f"ln{blk}")
                    for cc in range(8):
                        PG = PG1 if cc % 2 == 0 else PG2
                        transpose_step(plan, PG[:, 0:128], xn_tm3[:, jj, cc * 128:(cc + 1) * 128], idm[:],
                                       tag=f"xtr{blk}.{cc}")
                        ev(plan, a, lambda e, cc=cc, j=j, PG=PG: e.copy(
                            xn_fm3[:, cc, j * 128:(j + 1) * 128], PG[:, 0:128]),
                           reads=[PG], writes=[xn_fm], tag=f"xtre{blk}.{cc}")
            tsl = slice(0, 512)
            for qc in range(NQC):
                P = PA if qc % 2 == 0 else PB
                qs_ = qc_sb[:, (qc % 2) * 512:(qc % 2) * 512 + 512]
                mm(plan, P[:], [(wq3[:, ic, qc * 128:(qc + 1) * 128], xn_fm3[:, ic, :]) for ic in range(8)]
                   + [(bqr[:, qc * 128:(qc + 1) * 128], onesrow[:])], tag=f"q{ch}.{qc}")
                ev(plan, a, lambda e, P=P, qs_=qs_: e.copy(qs_, P[:]), reads=[P], writes=[qc_sb])
                plan.dma(g, q_dram3[qc, :, ch * 512:(ch + 1) * 512], qs_, tag=f"sq{ch}.{qc}")
                col = qc % 2
                ev(plan, v, lambda e, qs_=qs_: e.reduce_sum(t1[:], qs_, axis=AX.X),
                   reads=[qc_sb], writes=[t1])
                ev(plan, v, lambda e, col=col: e.tensor_add(stat[:, col:col + 1], stat[:, col:col + 1], t1[:]),
                   reads=[stat, t1], writes=[stat])
                ev(plan, a, lambda e, qs_=qs_: e.activation(sq[:], qs_, AF.Square), reads=[qc_sb], writes=[sq])
                ev(plan, v, lambda e: e.reduce_sum(t1[:], sq[:], axis=AX.X), reads=[sq], writes=[t1])
                ev(plan, v, lambda e, col=col: e.tensor_add(stat[:, 2 + col:3 + col], stat[:, 2 + col:3 + col], t1[:]),
                   reads=[stat, t1], writes=[stat])
        plan.dma(g, d_st, stat[:], tag="sst")
        plan.barrier_outputs([d_q, d_st])
        plan.build(es)
    return nc


def prep_f2a(inputs, x_full):
    cc = _common_consts()
    in_maps = []
    for c in range(N_CORES):
        tsl = slice(c * T_CORE, (c + 1) * T_CORE)
        in_maps.append({
            "x": np.ascontiguousarray(x_full[tsl]),
            "wq": np.ascontiguousarray(inputs["L2_ffn_Wq"]),
            "bq": inputs["L2_ffn_bq"][None, :],
            "idm": cc["idm"], "cst": cc["cst"],
        })
    return in_maps


def build_f2b():
    nc = bass.Bass("TRN2", target_bir_lowering=False, debug=False)
    TC = T_CORE
    NB = TC // 128
    NT = TC // 512
    QD = 2048
    NQC = 16
    NTOT = float(T_ALL * 8)

    d_x = nc.dram_tensor("x", [TC, DM], F32, kind="ExternalInput").ap()
    d_ap = nc.dram_tensor("ap", [4 * TC, DM], F32, kind="ExternalInput").ap()
    d_q = nc.dram_tensor("qin", [QD, TC], BF16, kind="ExternalInput").ap()
    d_st = nc.dram_tensor("stin", [128, 32], F32, kind="ExternalInput").ap()
    d_gb = nc.dram_tensor("gb", [128, 4], F32, kind="ExternalInput").ap()
    d_k0 = nc.dram_tensor("k0", [32, 128], F32, kind="ExternalInput").ap()
    d_k1 = nc.dram_tensor("k1", [32, 128], F32, kind="ExternalInput").ap()
    d_dn = nc.dram_tensor("dn", [DM, 1024], F32, kind="ExternalInput").ap()
    d_up = nc.dram_tensor("up", [1024, 1024], F32, kind="ExternalInput").ap()
    d_r0 = nc.dram_tensor("r0", [128, 1024], F32, kind="ExternalInput").ap()
    d_r1 = nc.dram_tensor("r1", [128, 1024], F32, kind="ExternalInput").ap()
    d_o16 = nc.dram_tensor("o16", [128, 16 * 16], F32, kind="ExternalInput").ap()
    d_id = nc.dram_tensor("idm", [128, 128], F32, kind="ExternalInput").ap()
    d_cst = nc.dram_tensor("cst", [128, 8], F32, kind="ExternalInput").ap()
    d_out = nc.dram_tensor("out", [TC, DM], F32, kind="ExternalOutput").ap()

    es = ExitStack()
    sb = lambda n, s, dt=F32: es.enter_context(nc.sbuf_tensor(n, s, dt))
    ps = lambda n, s, dt=F32: es.enter_context(nc.psum_tensor(n, s, dt))
    with es:
        xt = sb("s_xt", [128, 1024])
        xn_tm = sb("s_xn_tm", [128, 1024], BF16)
        xn_fm = sb("s_xn_fm", [128, 8 * 512], BF16)
        qch = sb("s_qch", [128, 16 * 512], BF16)
        stt_ = sb("s_stt", [128, 32])
        ab4 = sb("s_ab4", [128, 4])
        gb4 = sb("s_gb4", [128, 4])
        k0n = sb("s_k0n", [32, 128])
        k1n = sb("s_k1n", [32, 128])
        k0b = sb("s_k0b", [32, 128], BF16)
        k0T = sb("s_k0T", [128, 32], BF16)
        k1T = sb("s_k1T", [128, 32], BF16)
        kn1 = sb("s_kn1", [32, 1])
        dn = sb("s_dn", [128, 8 * 1024], BF16)
        up = sb("s_up", [128, 8 * 1024], BF16)
        r0 = sb("s_r0", [128, 1024], BF16)
        r1 = sb("s_r1", [128, 1024], BF16)
        o16 = sb("s_o16", [128, 16 * 16], BF16)
        idm = sb("s_idm", [128, 128], BF16)
        cst = sb("s_cst", [128, 8])
        sq = sb("s_sq", [128, 512], BF16)
        nrm = sb("s_nrm", [16, 512])
        rn_tm = sb("s_rn", [128, 4 * 16], BF16)
        s_f = sb("s_sf", [128, 512])
        cur = sb("s_cur", [128, 512])
        m1 = sb("s_m1", [128, 16]); mi_ = sb("s_mi", [128, 16]); gei = sb("s_gei", [128, 512])
        pbf = sb("s_pbf", [128, 512], BF16)
        pT = sb("s_pT", [64, 8 * 128], BF16)
        z16 = sb("s_z16", [128, 16]); rz8 = sb("s_rz8", [128, 8])
        G_tm = sb("s_G", [128, 1024], BF16)
        A_ = sb("s_A", [128, 1024])
        AGb = sb("s_AGb", [128, 1024], BF16)
        AG_em = sb("s_AGem", [128, 8 * 128], BF16)
        rsb = sb("s_rsb", [128, 512], BF16)
        tmp = sb("s_tmp", [128, 512])
        res = sb("s_res", [128, 1024])
        ffn = sb("s_ffn", [128, 1024])
        st6 = sb("s_st6", [128, 12]); st2 = sb("s_st2", [128, 2])
        rstd1 = sb("s_rstd1", [128, 1]); b2c_ = sb("s_b2c", [128, 1])
        stats = dict(s6=st6, s2=st2, rstd=rstd1, b2=b2c_, eps=cst[:, 0:1])

        PA = ps("PA", [128, 512]); PB = ps("PB", [128, 512])
        PC = ps("PC", [128, 512]); PD = ps("PD", [128, 512])
        PEb = ps("PEb", [128, 512]); PF = ps("PF", [128, 512])
        PG1 = ps("PG1", [128, 1024], BF16)
        PG2 = ps("PG2", [128, 1024], BF16)

        plan = Plan(nc)
        g, s, v, a = "gpsimd", "sync", "vector", "scalar"
        plan.dma(g, dn[:].rearrange("p (c o) -> p c o", c=8), d_dn.rearrange("(c p) o -> p c o", p=128))
        plan.dma(g, up[:].rearrange("p (c o) -> p c o", c=8), d_up.rearrange("(c p) o -> p c o", p=128))
        plan.dma(g, r0[:], d_r0)
        plan.dma(g, r1[:], d_r1)
        plan.dma(g, o16[:], d_o16)
        plan.dma(g, idm[:], d_id)
        plan.dma(s, cst[:], d_cst)
        plan.dma(s, stt_[:], d_st)
        plan.dma(s, gb4[:], d_gb)
        plan.dma(s, k0n[:], d_k0)
        plan.dma(s, k1n[:], d_k1)

        xn_fm3 = xn_fm[:].rearrange("p (c t) -> p c t", c=8)
        qch3 = qch[:].rearrange("p (c t) -> p c t", c=16)
        dn3 = dn[:].rearrange("p (c o) -> p c o", c=8)
        up3 = up[:].rearrange("p (c o) -> p c o", c=8)
        pT3 = pT[:].rearrange("p (c t) -> p c t", c=8)
        AG3 = AG_em[:].rearrange("p (c t) -> p c t", c=8)
        rn3 = rn_tm[:].rearrange("p (b k) -> p b k", b=4)
        x_dram3 = d_x.rearrange("(b p) f -> p b f", p=128)
        ap_dram3 = d_ap.rearrange("(n b p) f -> n p b f", n=4, p=128)
        q_dram3 = d_q.rearrange("(c p) t -> p c t", p=128)
        out_dram3 = d_out.rearrange("(b p) f -> p b f", p=128)

        ev(plan, v, lambda e: e.tensor_reduce(
            ab4[:].rearrange("p (r o) -> p r o", r=4),
            stt_[:].rearrange("p (r c) -> p r c", r=4), AX.X, ALU.add),
           reads=[stt_], writes=[ab4], tag="stc")
        ev(plan, v, lambda e: e.tensor_scalar_mul(ab4[:, 0:2], ab4[:, 0:2], 1.0 / NTOT),
           reads=[ab4], writes=[ab4])
        ev(plan, v, lambda e: e.tensor_scalar_mul(ab4[:, 2:4], ab4[:, 2:4], 1.0 / NTOT),
           reads=[ab4], writes=[ab4])
        ev(plan, v, lambda e: e.tensor_mul(stt_[:, 0:2], ab4[:, 0:2], ab4[:, 0:2]),
           reads=[ab4], writes=[stt_])
        ev(plan, v, lambda e: e.tensor_sub(ab4[:, 2:4], ab4[:, 2:4], stt_[:, 0:2]),
           reads=[ab4, stt_], writes=[ab4])
        ev(plan, a, lambda e: e.activation(ab4[:, 2:4], ab4[:, 2:4], AF.Sqrt, bias=cst[:, 0:1]),
           reads=[ab4, cst], writes=[ab4])
        ev(plan, v, lambda e: e.reciprocal(ab4[:, 2:4], ab4[:, 2:4]), reads=[ab4], writes=[ab4])
        ev(plan, v, lambda e: e.tensor_mul(ab4[:, 2:4], ab4[:, 2:4], gb4[:, 0:2]),
           reads=[ab4, gb4], writes=[ab4])
        ev(plan, v, lambda e: e.tensor_mul(ab4[:, 0:2], ab4[:, 0:2], ab4[:, 2:4]),
           reads=[ab4], writes=[ab4])
        ev(plan, v, lambda e: e.tensor_sub(ab4[:, 0:2], gb4[:, 2:4], ab4[:, 0:2]),
           reads=[ab4, gb4], writes=[ab4])

        for kt, kb_, kT_ in ((k0n, k0b, k0T), (k1n, k0b, k1T)):
            ev(plan, a, lambda e, kt=kt: e.activation(kb_[:] if False else sq[0:32, 0:128], kt[:], AF.Square),
               reads=[kt], writes=[sq])
            ev(plan, v, lambda e: e.reduce_sum(kn1[:], sq[0:32, 0:128], axis=AX.X),
               reads=[sq], writes=[kn1])
            ev(plan, a, lambda e: e.activation(kn1[:], kn1[:], AF.Sqrt), reads=[kn1], writes=[kn1])
            ev(plan, v, lambda e: e.tensor_scalar_max(kn1[:], kn1[:], 1e-12), reads=[kn1], writes=[kn1])
            ev(plan, v, lambda e: e.reciprocal(kn1[:], kn1[:]), reads=[kn1], writes=[kn1])
            ev(plan, v, lambda e, kt=kt: e.tensor_scalar_mul(kb_[:], kt[:], kn1[:]),
               reads=[kt, kn1], writes=[kb_])
            transpose_step(plan, PG1[0:128, 0:32], kb_[:], idm[0:32, 0:32], tag="ktr")
            ev(plan, a, lambda e, kT_=kT_: e.copy(kT_[:], PG1[0:128, 0:32]), reads=[PG1], writes=[kT_])

        for ch in range(NT):
            plan.dma(s, qch[:].rearrange("p (c t) -> p c t", c=16), q_dram3[:, :, ch * 512:(ch + 1) * 512],
                     tag=f"ldq{ch}")
            for j in range(4):
                blk = 4 * ch + j
                plan.dma(s, xt[:], x_dram3[:, blk, :], tag=f"ldx{blk}")
                ln_plain_tile(plan, None, xn_tm[:], xt[:], stats, tag=f"ln{blk}")
                for cc in range(8):
                    PG = PG1 if cc % 2 == 0 else PG2
                    transpose_step(plan, PG[:, 0:128], xn_tm[:, cc * 128:(cc + 1) * 128], idm[:],
                                   tag=f"xtr{blk}.{cc}")
                    ev(plan, a, lambda e, cc=cc, j=j, PG=PG: e.copy(
                        xn_fm3[:, cc, j * 128:(j + 1) * 128], PG[:, 0:128]),
                       reads=[PG], writes=[xn_fm], tag=f"xtre{blk}.{cc}")
            for qc in range(NQC):
                col = qc % 2
                ev(plan, v, lambda e, qc=qc, col=col: e.tensor_scalar(
                    qch3[:, qc, :], qch3[:, qc, :], ab4[:, 2 + col:3 + col], ab4[:, col:col + 1],
                    ALU.mult, ALU.add),
                   reads=[qch, ab4], writes=[qch], tag=f"bn{ch}.{qc}")
                ev(plan, a, lambda e, qc=qc: e.activation(sq[:], qch3[:, qc, :], AF.Square),
                   reads=[qch], writes=[sq])
                mm(plan, PF[0:16, :], [(o16[:, qc * 16:(qc + 1) * 16], sq[:])],
                   start=(qc == 0), stop=(qc == 15), tag=f"nr{ch}.{qc}")
            ev(plan, a, lambda e: e.activation(nrm[:], PF[0:16, :], AF.Sqrt), reads=[PF], writes=[nrm])
            ev(plan, v, lambda e: e.tensor_scalar_max(nrm[:], nrm[:], 1e-12), reads=[nrm], writes=[nrm])
            ev(plan, v, lambda e: e.reciprocal(nrm[:], nrm[:]), reads=[nrm], writes=[nrm])
            ev(plan, v, lambda e: e.tensor_copy(sq[0:16, :], nrm[:]), reads=[nrm], writes=[sq])
            for j in range(4):
                transpose_step(plan, PG1[0:128, 0:16], sq[0:16, j * 128:(j + 1) * 128], idm[0:16, 0:16],
                               tag=f"rtr{ch}.{j}")
                ev(plan, a, lambda e, j=j: e.copy(rn3[:, j, :], PG1[0:128, 0:16]),
                   reads=[PG1], writes=[rn_tm], tag=f"rtre{ch}.{j}")
            for j in range(4):
                blk = 4 * ch + j
                for qc in range(NQC):
                    kT_ = k0T if qc % 2 == 0 else k1T
                    mm(plan, PA[:, qc * 32:(qc + 1) * 32],
                       [(qch3[:, qc, j * 128:(j + 1) * 128], kT_[:])],
                       start=(qc == 0), stop=(qc == 15), tag=f"s0{blk}.{qc}")
                bc16_32 = lambda t: t.rearrange("p (k o) -> p k o", k=16).broadcast_to([128, 16, 32])
                ev(plan, v, lambda e, j=j: e.tensor_tensor(
                    s_f[:].rearrange("p (k i) -> p k i", k=16), PA[:].rearrange("p (k i) -> p k i", k=16),
                    bc16_32(rn3[:, j, :]), ALU.mult),
                   reads=[PA, rn_tm], writes=[s_f], tag=f"ssc{blk}")
                ev(plan, v, lambda e: e.tensor_copy(cur[:], s_f[:]), reads=[s_f], writes=[cur])
                for it in range(3):
                    red = m1 if it == 0 else mi_
                    ev(plan, v, lambda e, red=red: e.tensor_reduce(
                        red[:].rearrange("p (k o) -> p k o", k=16),
                        cur[:].rearrange("p (k i) -> p k i", k=16), AX.X, ALU.max),
                       reads=[cur], writes=[red], tag=f"t4m{blk}.{it}")
                    ev(plan, v, lambda e, red=red: e.tensor_tensor(
                        gei[:].rearrange("p (k i) -> p k i", k=16),
                        cur[:].rearrange("p (k i) -> p k i", k=16), bc16_32(red[:]), ALU.is_ge),
                       reads=[cur, red], writes=[gei], tag=f"t4g{blk}.{it}")
                    ev(plan, v, lambda e: e.scalar_tensor_tensor(cur[:], gei[:], -1e30, cur[:], ALU.mult, ALU.add),
                       reads=[gei, cur], writes=[cur], tag=f"t4u{blk}.{it}")
                ev(plan, v, lambda e: e.tensor_reduce(
                    mi_[:].rearrange("p (k o) -> p k o", k=16),
                    cur[:].rearrange("p (k i) -> p k i", k=16), AX.X, ALU.max),
                   reads=[cur], writes=[mi_], tag=f"t4m4{blk}")
                ev(plan, v, lambda e: e.tensor_tensor(
                    s_f[:].rearrange("p (k i) -> p k i", k=16),
                    s_f[:].rearrange("p (k i) -> p k i", k=16), bc16_32(m1[:]), ALU.subtract),
                   reads=[s_f, m1], writes=[s_f])
                ev(plan, v, lambda e: e.tensor_tensor(
                    gei[:].rearrange("p (k i) -> p k i", k=16),
                    s_f[:].rearrange("p (k i) -> p k i", k=16),
                    bc16_32(mi_[:]), ALU.is_ge),
                   reads=[s_f, mi_], writes=[gei])
                ev(plan, v, lambda e: e.tensor_sub(mi_[:], mi_[:], m1[:]), reads=[mi_, m1], writes=[mi_])
                ev(plan, v, lambda e: e.tensor_tensor(
                    gei[:].rearrange("p (k i) -> p k i", k=16),
                    s_f[:].rearrange("p (k i) -> p k i", k=16),
                    bc16_32(mi_[:]), ALU.is_ge),
                   reads=[s_f, mi_], writes=[gei])
                ev(plan, a, lambda e: e.activation(s_f[:], s_f[:], AF.Exp), reads=[s_f], writes=[s_f])
                ev(plan, v, lambda e: e.tensor_mul(s_f[:], s_f[:], gei[:]), reads=[s_f, gei], writes=[s_f])
                ev(plan, v, lambda e: e.tensor_reduce(
                    z16[:].rearrange("p (k o) -> p k o", k=16),
                    s_f[:].rearrange("p (k i) -> p k i", k=16), AX.X, ALU.add),
                   reads=[s_f], writes=[z16], tag=f"z{blk}")
                ev(plan, v, lambda e: e.tensor_tensor(
                    rz8[:].rearrange("p (h o) -> p h o", h=8),
                    z16[:].rearrange("p (h x) -> p h x", h=8)[:, :, 0:1],
                    z16[:].rearrange("p (h x) -> p h x", h=8)[:, :, 1:2], ALU.mult),
                   reads=[z16], writes=[rz8])
                ev(plan, v, lambda e: e.reciprocal(rz8[:], rz8[:]), reads=[rz8], writes=[rz8])
                ev(plan, v, lambda e: e.tensor_tensor(
                    s_f[:].rearrange("p (h x i) -> p h x i", h=8, x=2)[:, :, 0:1, :],
                    s_f[:].rearrange("p (h x i) -> p h x i", h=8, x=2)[:, :, 0:1, :],
                    rz8[:].rearrange("p (h a o) -> p h a o", h=8, a=1).broadcast_to([128, 8, 1, 32]), ALU.mult),
                   reads=[s_f, rz8], writes=[s_f], tag=f"rzf{blk}")
                ev(plan, v, lambda e: e.tensor_copy(pbf[:], s_f[:]), reads=[s_f], writes=[pbf])
                for cc in range(8):
                    PG = PG1 if cc % 2 == 0 else PG2
                    transpose_step(plan, PG[0:64, 128:256], pbf[:, cc * 64:(cc + 1) * 64], idm[:],
                                   tag=f"ptr{blk}.{cc}")
                    ev(plan, a, lambda e, cc=cc, PG=PG: e.copy(pT3[:, cc, :], PG[0:64, 128:256]),
                       reads=[PG], writes=[pT], tag=f"ptre{blk}.{cc}")
                for gh in range(2):
                    P = PB if gh == 0 else PC
                    mm(plan, P[:], [(xn_fm3[:, ic, j * 128:(j + 1) * 128], dn3[:, ic, gh * 512:(gh + 1) * 512])
                                    for ic in range(8)], tag=f"G{blk}.{gh}")
                    ev(plan, a, lambda e, gh=gh, P=P: e.activation(G_tm[:, gh * 512:(gh + 1) * 512], P[:], AF.Gelu),
                       reads=[P], writes=[G_tm], tag=f"Ge{blk}.{gh}")
                ev(plan, v, lambda e: e.memset(A_[:], 0.0), writes=[A_], tag=f"Az{blk}")
                for h in range(8):
                    for eh in range(2):
                        esl = slice(eh * 512, (eh + 1) * 512)
                        mm(plan, PD[:], [(pT3[0:32, h, :], r0[0:32, esl])], tag=f"r0{blk}.{h}.{eh}")
                        mm(plan, PEb[:], [(pT3[32:64, h, :], r1[32:64, esl])], tag=f"r1{blk}.{h}.{eh}")
                        ev(plan, a, lambda e: e.copy(rsb[:], PD[:]), reads=[PD], writes=[rsb])
                        ev(plan, v, lambda e: e.tensor_tensor(tmp[:], rsb[:], PEb[:], ALU.mult),
                           reads=[rsb, PEb], writes=[tmp])
                        ev(plan, v, lambda e, esl=esl: e.tensor_add(A_[:, esl], A_[:, esl], tmp[:]),
                           reads=[A_, tmp], writes=[A_], tag=f"Aa{blk}.{h}.{eh}")
                ev(plan, v, lambda e: e.tensor_mul(AGb[:], A_[:], G_tm[:]), reads=[A_, G_tm], writes=[AGb])
                for cc in range(8):
                    PG = PG1 if cc % 2 == 0 else PG2
                    transpose_step(plan, PG[:, 128:256], AGb[:, cc * 128:(cc + 1) * 128], idm[:],
                                   tag=f"agt{blk}.{cc}")
                    ev(plan, a, lambda e, cc=cc, PG=PG: e.copy(AG3[:, cc, :], PG[:, 128:256]),
                       reads=[PG], writes=[AG_em], tag=f"agte{blk}.{cc}")
                for oh_ in range(2):
                    P = PB if oh_ == 0 else PC
                    mm(plan, P[:], [(AG3[:, ec, :], up3[:, ec, oh_ * 512:(oh_ + 1) * 512]) for ec in range(8)],
                       tag=f"o{blk}.{oh_}")
                    ev(plan, a, lambda e, oh_=oh_, P=P: e.copy(ffn[:, oh_ * 512:(oh_ + 1) * 512], P[:]),
                       reads=[P], writes=[ffn], tag=f"oe{blk}.{oh_}")
                ev(plan, v, lambda e: e.tensor_copy(res[:], ffn[:]), reads=[ffn], writes=[res])
                for nn in range(4):
                    plan.dma(s, xt[:], ap_dram3[nn, :, blk, :], tag=f"rap{blk}.{nn}")
                    ev(plan, v, lambda e: e.tensor_add(res[:], res[:], xt[:]), reads=[res, xt], writes=[res])
                plan.dma(s, xt[:], x_dram3[:, blk, :], tag=f"rx{blk}")
                ev(plan, v, lambda e: e.tensor_add(res[:], res[:], xt[:]), reads=[res, xt], writes=[res])
                plan.dma(g, out_dram3[:, blk, :], res[:], tag=f"so{blk}")

        plan.barrier_outputs([d_out])
        plan.build(es)
    return nc


def prep_f2b(inputs, x_full, aps, q_parts, st_parts):
    cc = _common_consts()
    gam = inputs["L2_ffn_bn_gamma"].reshape(2, 128).T
    bet = inputs["L2_ffn_bn_beta"].reshape(2, 128).T
    gb = np.concatenate([gam, bet], 1).astype(np.float32)
    st_all = np.zeros((128, 32), np.float32)
    for c in range(8):
        for jj in range(4):
            st_all[:, jj * 8 + c] = st_parts[c][:, jj]
    ee = np.arange(1024)
    r0 = np.zeros((128, 1024), np.float32)
    r1 = np.zeros((128, 1024), np.float32)
    for p in range(128):
        r0[p] = (ee // 32 == p % 32)
        r1[p] = (ee % 32 == p % 32)
    o16 = np.zeros((128, 256), np.float32)
    for qc in range(16):
        o16[:, qc * 16 + qc] = 1.0
    in_maps = []
    for c in range(N_CORES):
        tsl = slice(c * T_CORE, (c + 1) * T_CORE)
        in_maps.append({
            "x": np.ascontiguousarray(x_full[tsl]),
            "ap": np.ascontiguousarray(np.stack([p[tsl] for p in aps], 0).reshape(4 * T_CORE, DM)),
            "qin": q_parts[c],
            "stin": st_all, "gb": gb,
            "k0": inputs["L2_ffn_sub_key0"], "k1": inputs["L2_ffn_sub_key1"],
            "dn": np.ascontiguousarray(inputs["L2_ffn_expert_down"].T),
            "up": np.ascontiguousarray(inputs["L2_ffn_expert_up"]),
            "r0": r0, "r1": r1, "o16": o16,
            "idm": cc["idm"], "cst": cc["cst"],
        })
    return in_maps


def emit_fp4_quant(plan, out_bf, v_ap, W, scr, tag="q"):
    v, a = "vector", "scalar"
    NBK = W // 16
    ab, u, r2, r4, ist, ri, bm, bi = (scr[k] for k in ("ab", "u", "r2", "r4", "ist", "ri", "bm", "bi"))
    b3 = lambda t: t[:, 0:NBK].rearrange("p (n o) -> p n o", n=NBK).broadcast_to([128, NBK, 16])
    ev(plan, a, lambda e: e.activation(ab[:, 0:W], v_ap, AF.Abs), reads=[v_ap], writes=[ab], tag=tag + ".ab")
    ev(plan, v, lambda e: e.tensor_reduce(
        bm[:, 0:NBK].rearrange("p (n o) -> p n o", n=NBK),
        ab[:, 0:W].rearrange("p (n i) -> p n i", n=NBK), AX.X, ALU.max),
       reads=[ab], writes=[bm], tag=tag + ".bm")
    ev(plan, v, lambda e: e.tensor_scalar(bm[:, 0:NBK], bm[:, 0:NBK], 1.0 / 6, 1e-12, ALU.mult, ALU.max),
       reads=[bm], writes=[bm], tag=tag + ".s")
    ev(plan, v, lambda e: e.reciprocal(bi[:, 0:NBK], bm[:, 0:NBK]), reads=[bm], writes=[bi], tag=tag + ".ri")
    ev(plan, v, lambda e: e.tensor_tensor(
        u[:, 0:W].rearrange("p (n i) -> p n i", n=NBK),
        ab[:, 0:W].rearrange("p (n i) -> p n i", n=NBK), b3(bi), ALU.mult),
       reads=[ab, bi], writes=[u], tag=tag + ".u")
    ev(plan, v, lambda e: e.tensor_scalar(r2[:, 0:W], u[:, 0:W], 2.0, None, ALU.is_ge),
       reads=[u], writes=[r2], tag=tag + ".r2")
    ev(plan, v, lambda e: e.tensor_scalar(r4[:, 0:W], u[:, 0:W], 4.0, None, ALU.is_ge),
       reads=[u], writes=[r4], tag=tag + ".r4")
    ev(plan, v, lambda e: e.tensor_scalar(ist[:, 0:W], r2[:, 0:W], -1.0, 2.0, ALU.mult, ALU.add),
       reads=[r2], writes=[ist], tag=tag + ".i1")
    ev(plan, v, lambda e: e.scalar_tensor_tensor(ist[:, 0:W], r4[:, 0:W], -0.5, ist[:, 0:W], ALU.mult, ALU.add),
       reads=[r4, ist], writes=[ist], tag=tag + ".i2")
    ev(plan, v, lambda e: e.tensor_tensor(
        ab[:, 0:W].rearrange("p (n i) -> p n i", n=NBK),
        v_ap.rearrange("p (n i) -> p n i", n=NBK), b3(bi), ALU.mult),
       reads=[v_ap, bi], writes=[ab], tag=tag + ".t1")
    ev(plan, v, lambda e: e.tensor_mul(ab[:, 0:W], ab[:, 0:W], ist[:, 0:W]),
       reads=[ab, ist], writes=[ab], tag=tag + ".t2")
    ev(plan, v, lambda e: e.tensor_copy(ri[:, 0:W], ab[:, 0:W]), reads=[ab], writes=[ri], tag=tag + ".c1")
    ev(plan, v, lambda e: e.tensor_copy(ab[:, 0:W], ri[:, 0:W]), reads=[ri], writes=[ab], tag=tag + ".c2")
    ev(plan, v, lambda e: e.tensor_scalar(ist[:, 0:W], r2[:, 0:W], 0.5, 0.5, ALU.mult, ALU.add),
       reads=[r2], writes=[ist], tag=tag + ".s1")
    ev(plan, v, lambda e: e.tensor_add(ist[:, 0:W], ist[:, 0:W], r4[:, 0:W]),
       reads=[ist, r4], writes=[ist], tag=tag + ".s2")
    ev(plan, v, lambda e: e.tensor_mul(ab[:, 0:W], ab[:, 0:W], ist[:, 0:W]),
       reads=[ab, ist], writes=[ab], tag=tag + ".q1")
    ev(plan, v, lambda e: e.tensor_tensor(
        out_bf.rearrange("p (n i) -> p n i", n=NBK),
        ab[:, 0:W].rearrange("p (n i) -> p n i", n=NBK), b3(bm), ALU.mult),
       reads=[ab, bm], writes=[out_bf], tag=tag + ".q2")


def build_f3a():
    nc = bass.Bass("TRN2", target_bir_lowering=False, debug=False)
    I3 = 2816
    GRPS = 3
    d_wg = nc.dram_tensor("wg", [GRPS * 128, DM], F32, kind="ExternalInput").ap()
    d_wu = nc.dram_tensor("wu", [GRPS * 128, DM], F32, kind="ExternalInput").ap()
    d_wd = nc.dram_tensor("wd", [128, I3], F32, kind="ExternalInput").ap()
    d_id = nc.dram_tensor("idm", [128, 128], F32, kind="ExternalInput").ap()
    d_gT = nc.dram_tensor("gT", [DM, GRPS * 128], BF16, kind="ExternalOutput").ap()
    d_uT = nc.dram_tensor("uT", [DM, GRPS * 128], BF16, kind="ExternalOutput").ap()
    d_dT = nc.dram_tensor("dT", [I3, 128], BF16, kind="ExternalOutput").ap()

    es = ExitStack()
    sb = lambda n, s, dt=F32: es.enter_context(nc.sbuf_tensor(n, s, dt))
    ps = lambda n, s, dt=F32: es.enter_context(nc.psum_tensor(n, s, dt))
    with es:
        wrow = sb("s_wrow", [128, 2816])
        wq = sb("s_wq", [128, 2816], BF16)
        wt = sb("s_wt", [128, 2816], BF16)
        ab = sb("s_ab", [128, 2816]); u = sb("s_u", [128, 2816], BF16)
        r2 = sb("s_r2", [128, 2816], BF16); r4 = sb("s_r4", [128, 2816], BF16)
        ist = sb("s_ist", [128, 2816])
        ri = es.enter_context(nc.sbuf_tensor("s_ri", [128, 2816], mybir.dt.int32))
        bm = sb("s_bm", [128, 176]); bi = sb("s_bi", [128, 176])
        idm = sb("s_idm", [128, 128], BF16)
        PG1 = ps("PG1", [128, 1024], BF16)
        PG2 = ps("PG2", [128, 1024], BF16)
        scr = dict(ab=ab, u=u, r2=r2, r4=r4, ist=ist, ri=ri, bm=bm, bi=bi)

        plan = Plan(nc)
        g, s, v, a = "gpsimd", "sync", "vector", "scalar"
        plan.dma(g, idm[:], d_id)
        d_wg4 = d_wg.rearrange("(c p) o -> p c o", p=128)
        d_wu4 = d_wu.rearrange("(c p) o -> p c o", p=128)
        d_gT3 = d_gT.rearrange("(c p) o -> p c o", p=128)
        d_uT3 = d_uT.rearrange("(c p) o -> p c o", p=128)
        d_dT3 = d_dT.rearrange("(c p) o -> p c o", p=128)
        for which, (din, dout, W_, n_in) in enumerate(
                ((d_wg4, d_gT3, 1024, GRPS), (d_wu4, d_uT3, 1024, GRPS), (None, d_dT3, 2816, 1))):
            for rg in range(n_in):
                if which < 2:
                    plan.dma(s, wrow[:, 0:W_], din[:, rg, :], tag=f"lw{which}.{rg}")
                else:
                    plan.dma(s, wrow[:, 0:W_], d_wd, tag=f"lw{which}.{rg}")
                emit_fp4_quant(plan, wq[:, 0:W_], wrow[:, 0:W_], W_, scr, tag=f"qw{which}.{rg}")
                for cc in range(W_ // 128):
                    PG = PG1 if cc % 2 == 0 else PG2
                    transpose_step(plan, PG[:, 0:128], wq[:, cc * 128:(cc + 1) * 128], idm[:],
                                   tag=f"wtr{which}.{rg}.{cc}")
                    ev(plan, a, lambda e, cc=cc, PG=PG: e.copy(wt[:, cc * 128:(cc + 1) * 128], PG[:, 0:128]),
                       reads=[PG], writes=[wt], tag=f"wtre{which}.{rg}.{cc}")
                for cc in range(W_ // 128):
                    plan.dma(g, dout[:, cc, rg * 128:(rg + 1) * 128], wt[:, cc * 128:(cc + 1) * 128],
                             tag=f"sw{which}.{rg}.{cc}")
        plan.barrier_outputs([d_gT, d_uT, d_dT])
        plan.build(es)
    return nc


def prep_f3a(inputs):
    cc = _common_consts()
    wg = inputs["L3_ffn_w_gate"]; wu = inputs["L3_ffn_w_up"]; wd = inputs["L3_ffn_w_down"]
    in_maps = []
    for c in range(N_CORES):
        grps = [c, c + 8, min(c + 16, 21)]
        in_maps.append({
            "wg": np.ascontiguousarray(np.concatenate([wg[g_ * 128:(g_ + 1) * 128] for g_ in grps], 0)),
            "wu": np.ascontiguousarray(np.concatenate([wu[g_ * 128:(g_ + 1) * 128] for g_ in grps], 0)),
            "wd": np.ascontiguousarray(wd[c * 128:(c + 1) * 128]),
            "idm": cc["idm"],
        })
    return in_maps


def gather_f3a(inputs, results):
    I3 = 2816
    gT = np.zeros((DM, I3), dtype=BF16NP)
    uT = np.zeros((DM, I3), dtype=BF16NP)
    dT = np.zeros((I3, DM), dtype=BF16NP)
    for c in range(N_CORES):
        grps = [c, c + 8, min(c + 16, 21)]
        for rg, g_ in enumerate(grps):
            gT[:, g_ * 128:(g_ + 1) * 128] = results[c]["gT"][:, rg * 128:(rg + 1) * 128]
            uT[:, g_ * 128:(g_ + 1) * 128] = results[c]["uT"][:, rg * 128:(rg + 1) * 128]
        dT[:, c * 128:(c + 1) * 128] = results[c]["dT"]
    return gT, uT, dT


def build_f3b():
    nc = bass.Bass("TRN2", target_bir_lowering=False, debug=False)
    TC = T_CORE
    NB = TC // 128
    I3 = 2816
    NIC = 22

    d_x = nc.dram_tensor("x", [TC, DM], F32, kind="ExternalInput").ap()
    d_a0 = nc.dram_tensor("a0", [TC, DM], F32, kind="ExternalInput").ap()
    d_a1 = nc.dram_tensor("a1", [TC, DM], F32, kind="ExternalInput").ap()
    d_gT = nc.dram_tensor("gT", [DM, I3], BF16, kind="ExternalInput").ap()
    d_uT = nc.dram_tensor("uT", [DM, I3], BF16, kind="ExternalInput").ap()
    d_dT = nc.dram_tensor("dT", [I3, DM], BF16, kind="ExternalInput").ap()
    d_id = nc.dram_tensor("idm", [128, 128], F32, kind="ExternalInput").ap()
    d_cst = nc.dram_tensor("cst", [128, 8], F32, kind="ExternalInput").ap()
    d_out = nc.dram_tensor("out", [TC, DM], F32, kind="ExternalOutput").ap()

    es = ExitStack()
    sb = lambda n, s, dt=F32: es.enter_context(nc.sbuf_tensor(n, s, dt))
    ps = lambda n, s, dt=F32: es.enter_context(nc.psum_tensor(n, s, dt))
    with es:
        wg = sb("s_wg", [128, 8 * I3], BF16)
        wu = sb("s_wu", [128, 8 * I3], BF16)
        wd = sb("s_wd", [128, NIC * 1024], BF16)
        xt = sb("s_xt", [128, 1024])
        xn_tm = sb("s_xn_tm", [128, 1024], BF16)
        xq_tm = sb("s_xq", [128, 1024], BF16)
        xq_fm = sb("s_xqfm", [128, 2 * 8 * 128], BF16)
        h_tm = sb("s_h", [128, I3])
        hq = sb("s_hq", [128, I3], BF16)
        hqT = sb("s_hqT", [128, NIC * 128], BF16)
        ab = sb("s_ab", [128, I3]); u = sb("s_u", [128, I3], BF16)
        r2 = sb("s_r2", [128, I3], BF16); r4 = sb("s_r4", [128, I3], BF16)
        ist = sb("s_ist", [128, I3], BF16)
        ri = ab[:].bitcast(mybir.dt.int32)
        bm = sb("s_bm", [128, 176]); bi = sb("s_bi", [128, 176])
        res = sb("s_res", [128, 1024])
        att = None
        c1 = sb("s_c1", [128, 1]); c2 = sb("s_c2", [128, 1]); c3 = sb("s_c3", [128, 1])
        idm = sb("s_idm", [128, 128], BF16)
        cst = sb("s_cst", [128, 8])
        st6 = sb("s_st6", [128, 12]); st2 = sb("s_st2", [128, 2])
        rstd1 = sb("s_rstd1", [128, 1]); b2c_ = sb("s_b2c", [128, 1])
        stats = dict(s6=st6, s2=st2, rstd=rstd1, b2=b2c_, eps=cst[:, 0:1])
        scr = dict(ab=ab, u=u, r2=r2, r4=r4, ist=ist, ri=ri, bm=bm, bi=bi)

        PA = ps("PA", [128, 512]); PB = ps("PB", [128, 512])
        PC = ps("PC", [128, 512]); PD = ps("PD", [128, 512])
        PG1 = ps("PG1", [128, 1024], BF16)
        PG2 = ps("PG2", [128, 1024], BF16)

        plan = Plan(nc)
        g, s, v, a = "gpsimd", "sync", "vector", "scalar"
        plan.dma(g, wg[:].rearrange("p (c o) -> p c o", c=8), d_gT.rearrange("(c p) o -> p c o", p=128))
        plan.dma(g, wu[:].rearrange("p (c o) -> p c o", c=8), d_uT.rearrange("(c p) o -> p c o", p=128))
        plan.dma(g, wd[:].rearrange("p (c o) -> p c o", c=NIC), d_dT.rearrange("(c p) o -> p c o", p=128))
        plan.dma(g, idm[:], d_id)
        plan.dma(s, cst[:], d_cst)
        wg3 = wg[:].rearrange("p (c o) -> p c o", c=8)
        wu3 = wu[:].rearrange("p (c o) -> p c o", c=8)
        wd3 = wd[:].rearrange("p (c o) -> p c o", c=NIC)
        xq3 = xq_fm[:].rearrange("p (b c t) -> p b c t", b=2, c=8)
        hqT3 = hqT[:].rearrange("p (c t) -> p c t", c=NIC)
        x_dram3 = d_x.rearrange("(b p) f -> p b f", p=128)
        a0_dram3 = d_a0.rearrange("(b p) f -> p b f", p=128)
        a1_dram3 = d_a1.rearrange("(b p) f -> p b f", p=128)
        out_dram3 = d_out.rearrange("(b p) f -> p b f", p=128)
        SQ1E = float(np.sqrt(1.0 + 1e-6))

        for blk in range(NB):
            wb = blk % 2
            plan.dma(s, xt[:], x_dram3[:, blk, :], tag=f"ldx{blk}")
            ln_plain_tile(plan, None, xn_tm[:], xt[:], stats, tag=f"ln{blk}")
            emit_fp4_quant(plan, xq_tm[:], xn_tm[:], 1024, scr, tag=f"qx{blk}")
            for cc in range(8):
                PG = PG1 if cc % 2 == 0 else PG2
                transpose_step(plan, PG[:, 0:128], xq_tm[:, cc * 128:(cc + 1) * 128], idm[:],
                               tag=f"xtr{blk}.{cc}")
                ev(plan, a, lambda e, cc=cc, wb=wb, PG=PG: e.copy(xq3[:, wb, cc, :], PG[:, 0:128]),
                   reads=[PG], writes=[xq_fm], tag=f"xtre{blk}.{cc}")
            for io in range(6):
                W_ = 512 if io < 5 else 256
                isl = slice(io * 512, io * 512 + W_)
                mm(plan, PA[:, 0:W_], [(xq3[:, wb, ic, :], wg3[:, ic, isl]) for ic in range(8)],
                   tag=f"g{blk}.{io}")
                mm(plan, PB[:, 0:W_], [(xq3[:, wb, ic, :], wu3[:, ic, isl]) for ic in range(8)],
                   tag=f"u{blk}.{io}")
                ev(plan, a, lambda e, W_=W_, isl=isl: e.activation(ab[:, 0:W_], PA[:, 0:W_], AF.Silu),
                   reads=[PA], writes=[ab], tag=f"sl{blk}.{io}")
                ev(plan, v, lambda e, W_=W_, isl=isl: e.tensor_tensor(
                    h_tm[:, isl], ab[:, 0:W_], PB[:, 0:W_], ALU.mult),
                   reads=[ab, PB], writes=[h_tm], tag=f"h{blk}.{io}")
            emit_fp4_quant(plan, hq[:], h_tm[:], I3, scr, tag=f"qh{blk}")
            for cc in range(NIC):
                PG = PG1 if cc % 2 == 0 else PG2
                transpose_step(plan, PG[:, 0:128], hq[:, cc * 128:(cc + 1) * 128], idm[:],
                               tag=f"htr{blk}.{cc}")
                ev(plan, a, lambda e, cc=cc, PG=PG: e.copy(hqT3[:, cc, :], PG[:, 0:128]),
                   reads=[PG], writes=[hqT], tag=f"htre{blk}.{cc}")
            for oh_ in range(2):
                P = PC if oh_ == 0 else PD
                mm(plan, P[:], [(hqT3[:, ic, :], wd3[:, ic, oh_ * 512:(oh_ + 1) * 512]) for ic in range(NIC)],
                   tag=f"d{blk}.{oh_}")
                ev(plan, a, lambda e, oh_=oh_, P=P: e.copy(res[:, oh_ * 512:(oh_ + 1) * 512], P[:]),
                   reads=[P], writes=[res], tag=f"de{blk}.{oh_}")
            att = h_tm[:, 1024:2048]
            plan.dma(s, att, a0_dram3[:, blk, :], tag=f"la0{blk}")
            plan.dma(s, xt[:], a1_dram3[:, blk, :], tag=f"la1{blk}")
            ev(plan, v, lambda e: e.tensor_add(att, att, xt[:]), reads=[h_tm, xt], writes=[h_tm])
            ev(plan, v, lambda e: e.tensor_scalar_mul(att, att, 0.1), reads=[h_tm], writes=[h_tm])
            ev(plan, v, lambda e: e.tensor_scalar_add(att[:, 0:1], att[:, 0:1], SQ1E),
               reads=[h_tm], writes=[h_tm])
            ev(plan, v, lambda e: e.tensor_scalar(att[:, 1:1024], att[:, 1:1024], -8.0, 8.0, ALU.max, ALU.min),
               reads=[h_tm], writes=[h_tm])
            ev(plan, a, lambda e: e.activation(ab[:, 0:1024], att, AF.Square, accum_out=c1[:]),
               reads=[h_tm], writes=[ab, c1])
            ev(plan, v, lambda e: e.tensor_mul(c2[:], att[:, 0:1], att[:, 0:1]), reads=[h_tm], writes=[c2])
            ev(plan, v, lambda e: e.tensor_sub(c1[:], c1[:], c2[:]), reads=[c1, c2], writes=[c1])
            ev(plan, a, lambda e: e.activation(c3[:], c1[:], AF.Sqrt, bias=cst[:, 1:2]),
               reads=[c1, cst], writes=[c3])
            ev(plan, v, lambda e: e.tensor_copy(att[:, 0:1], c3[:]), reads=[c3], writes=[h_tm])
            plan.dma(s, xt[:], x_dram3[:, blk, :], tag=f"rx{blk}")
            ev(plan, v, lambda e: e.tensor_add(res[:], res[:], att), reads=[res, h_tm], writes=[res])
            ev(plan, v, lambda e: e.tensor_add(res[:], res[:], xt[:]), reads=[res, xt], writes=[res])
            plan.dma(g, out_dram3[:, blk, :], res[:], tag=f"so{blk}")

        plan.barrier_outputs([d_out])
        plan.build(es)
    return nc


def prep_f3b(inputs, x_full, a0, a1, gT, uT, dT):
    cc = _common_consts()
    in_maps = []
    for c in range(N_CORES):
        tsl = slice(c * T_CORE, (c + 1) * T_CORE)
        in_maps.append({
            "x": np.ascontiguousarray(x_full[tsl]),
            "a0": np.ascontiguousarray(a0.reshape(T_ALL, DM)[tsl]),
            "a1": np.ascontiguousarray(a1.reshape(T_ALL, DM)[tsl]),
            "gT": gT, "uT": uT, "dT": dT,
            "idm": cc["idm"], "cst": cc["cst"],
        })
    return in_maps


def _fm_to_tm(outs):
    p0 = np.concatenate([np.ascontiguousarray(outs[2 * b].T) for b in range(4)], 0)
    p1 = np.concatenate([np.ascontiguousarray(outs[2 * b + 1].T) for b in range(4)], 0)
    return p0, p1


def kernel(**inputs):
    inputs = {k: np.asarray(v) for k, v in inputs.items()}
    cores = list(range(N_CORES))
    run = run_bass_kernel_spmd

    x0 = np.ascontiguousarray(inputs["L0_attn_x"].reshape(T_ALL, DM).astype(np.float32))

    r = run(build_a0(), prep_a0(inputs), core_ids=cores)
    a0p0, a0p1 = _fm_to_tm([res["out"] for res in r.results])
    nc_f0 = build_f0()
    x1_parts = []
    for half in (0, 1):
        r = run(nc_f0, prep_f0(inputs, x0, a0p0, a0p1, half), core_ids=cores)
        x1_parts.append([res["out"] for res in r.results])
    x1 = np.concatenate([np.concatenate([x1_parts[0][c], x1_parts[1][c]], 0) for c in range(8)], 0)

    inputs["L1_attn_x_staged"] = x1.reshape(B, SEQ, DM)
    inputs["L1_ffn_x_staged"] = inputs["L1_attn_x_staged"]
    r = run(build_attn("l1a"), prep_attn("l1a", inputs), core_ids=cores)
    aA0, aA1 = _fm_to_tm([res["out"] for res in r.results])
    r = run(build_attn("l1b"), prep_attn("l1b", inputs), core_ids=cores)
    aB0, aB1 = _fm_to_tm([res["out"] for res in r.results])
    nc_f1 = build_f1()
    x2_parts = []
    for half in (0, 1):
        r = run(nc_f1, prep_f1(inputs, x1, aA0, aA1, aB0, aB1, half), core_ids=cores)
        x2_parts.append([res["out"] for res in r.results])
    x2 = np.concatenate([np.concatenate([x2_parts[0][c], x2_parts[1][c]], 0) for c in range(8)], 0)

    inputs["L2_attn_x_staged"] = x2.reshape(B, SEQ, DM)
    r = run(build_attn("l2"), prep_attn("l2", inputs), core_ids=cores)
    l2outs = [res["out"] for res in r.results]
    aps = []
    for part in range(2):
        for hg in range(2):
            aps.append(np.concatenate(
                [np.ascontiguousarray(l2outs[2 * b + hg][part * DM:(part + 1) * DM].T) for b in range(4)], 0))
    r = run(build_f2a(), prep_f2a(inputs, x2), core_ids=cores)
    q_parts = [res["qout"] for res in r.results]
    st_parts = [res["stout"] for res in r.results]
    r = run(build_f2b(), prep_f2b(inputs, x2, aps, q_parts, st_parts), core_ids=cores)
    x3 = np.concatenate([res["out"] for res in r.results], 0)

    inputs["L3_attn_x_staged"] = x3.reshape(B, SEQ, DM)
    r = run(build_attn("l3"), prep_attn("l3", inputs), core_ids=cores)
    a3p0, a3p1 = _fm_to_tm([res["out"] for res in r.results])
    r = run(build_f3a(), prep_f3a(inputs), core_ids=cores)
    gT, uT, dT = gather_f3a(inputs, [res for res in r.results])
    r = run(build_f3b(), prep_f3b(inputs, x3, a3p0, a3p1, gT, uT, dT), core_ids=cores)
    x4 = np.concatenate([res["out"] for res in r.results], 0)

    return np.ascontiguousarray(x4.reshape(B, SEQ, DM).astype(np.float32))
